# version 12
# baseline (speedup 1.0000x reference)
import math

import numpy as np
import concourse.bass as bass
import concourse.mybir as mybir
from concourse.bass_utils import run_bass_kernel_spmd

F32 = mybir.dt.float32
BF16 = mybir.dt.bfloat16
FP16 = mybir.dt.float16
AF = mybir.ActivationFunctionType
ALU = mybir.AluOpType
AX = mybir.AxisListType

S, D, H, DH = 4096, 2048, 16, 128
HID = 5632
HPC, EPC, TPC, WIN = 4, 512, 1024, 512
NW = S // WIN
DK = D // 128
HKN = HID // 128
NHS = HID // WIN
W12_SH = D // 8
W3_SH = HID // 8
ISQ = 1.0 / math.sqrt(DH)
RM = 1536.0
NWELEM = float(HID * D)
INV_D = 1.0 / D


class Sem:
    def __init__(self, nc, name):
        self.h = nc.semaphore(name).__enter__()
        self.n = 0

    def inc(self, instr, by=1):
        instr.then_inc(self.h, by)
        self.n += by
        return self.n


class Arena:
    def __init__(self, t, cols):
        self.t, self.cols, self.cur = t, cols, 0

    def alloc(self, n):
        assert self.cur + n <= self.cols, ("arena", self.cur, n, self.cols)
        off = self.cur
        self.cur += n
        return self.t[:, off:off + n]

    def reset(self, m=0):
        self.cur = m

    def mark(self):
        return self.cur


DEBUG_INTERNALS = False
PHASE_LIMIT = 99
A1B_CHUNKS = None
A1B_ONLY_W3 = False


def build_program():
    nc = bass.Bass("TRN2", target_bir_lowering=False, debug=False)
    dbg = "ExternalOutput" if DEBUG_INTERNALS else "Internal"

    xs = nc.dram_tensor("xs", [S, D], F32, kind="ExternalInput").ap()
    wqd = nc.dram_tensor("wq", [HPC, D, DH], F32, kind="ExternalInput").ap()
    wkd = nc.dram_tensor("wk", [HPC, D, DH], F32, kind="ExternalInput").ap()
    wvd = nc.dram_tensor("wv", [HPC, D, DH], F32, kind="ExternalInput").ap()
    wod = nc.dram_tensor("wo", [H, DH, D], F32, kind="ExternalInput").ap()
    bqT = nc.dram_tensor("bqT", [DH, HPC], F32, kind="ExternalInput").ap()
    bkT = nc.dram_tensor("bkT", [DH, HPC], F32, kind="ExternalInput").ap()
    bvTf = nc.dram_tensor("bvTf", [DH, H], F32, kind="ExternalInput").ap()
    bvfl = nc.dram_tensor("bvfl", [1, EPC], F32, kind="ExternalInput").ap()
    bod = nc.dram_tensor("bo", [1, D], F32, kind="ExternalInput").ap()
    w1sd = nc.dram_tensor("w1s", [HID, W12_SH], F32, kind="ExternalInput").ap()
    w2sd = nc.dram_tensor("w2s", [HID, W12_SH], F32, kind="ExternalInput").ap()
    w3sd = nc.dram_tensor("w3s", [D, W3_SH], F32, kind="ExternalInput").ap()
    identd = nc.dram_tensor("ident", [128, 128], BF16, kind="ExternalInput").ap()
    onesbd = nc.dram_tensor("onesb", [128, 2], BF16, kind="ExternalInput").ap()
    onesfd = nc.dram_tensor("onesf", [128, 128], F32, kind="ExternalInput").ap()
    trild = nc.dram_tensor("tril", [128, 896], BF16, kind="ExternalInput").ap()
    out_d = nc.dram_tensor("out", [TPC, D], F32, kind="ExternalOutput").ap()

    k_d = nc.dram_tensor("k_d", [EPC, S], BF16, kind=dbg).ap()
    v_d = nc.dram_tensor("v_d", [S, EPC], BF16, kind=dbg).ap()
    zin_t = nc.dram_tensor("zin", [NW, EPC, WIN], BF16, kind=dbg)
    zg_t = nc.dram_tensor("zg", [NW, D, WIN], BF16)
    zin, zg = zin_t.ap(), zg_t.ap()
    mp_in = nc.dram_tensor("mp_in", [1, 8], F32)
    mp_g = nc.dram_tensor("mp_g", [8, 8], F32)
    w1ti = nc.dram_tensor("w1t_in", [W12_SH, HID], BF16, kind=dbg)
    w2ti = nc.dram_tensor("w2t_in", [W12_SH, HID], BF16)
    w3ti = nc.dram_tensor("w3t_in", [W3_SH, D], BF16)
    w1tg = nc.dram_tensor("w1t_g", [D, HID], BF16)
    w2tg = nc.dram_tensor("w2t_g", [D, HID], BF16)
    w3tg = nc.dram_tensor("w3t_g", [HID, D], BF16)
    x2_d = nc.dram_tensor("x2_d", [TPC, D], F32, kind=dbg).ap()
    h_d = nc.dram_tensor("h_d", [TPC, HID], F32, kind=dbg).ap()

    def sbt(name, shape, dt):
        return nc.sbuf_tensor(name, shape, dt).__enter__()

    def pst(name, shape, dt=F32):
        return nc.psum_tensor(name, shape, dt).__enter__()

    id_sb = sbt("id_sb", [128, 128], BF16)
    onesb = sbt("onesb_s", [128, 2], BF16)
    onesf = sbt("onesf_s", [128, 128], F32)
    tril = sbt("tril_s", [128, 896], BF16)
    bq_sb = sbt("bq_sb", [128, HPC], F32)
    bk_sb = sbt("bk_sb", [128, HPC], F32)
    bvT_sb = sbt("bvT_sb", [128, H], F32)
    bvTb = sbt("bvTb", [128, H], BF16)
    bo_row = sbt("bo_row", [1, WIN], F32)
    bvf_sb = sbt("bvf_sb", [1, EPC], F32)
    smr = sbt("smr", [1, 64], F32)
    smc = sbt("smc", [128, 64], F32)
    mp_sb = sbt("mp_sb", [8, 8], F32)
    kmax = sbt("kmax", [1, 2 * HPC], F32)
    rs_sb = sbt("rs_sb", [1, WIN], F32)
    fac_x = sbt("fac_x", [128, 8], F32)
    sa1 = sbt("sa1", [128, 8], F32)
    sb2 = sbt("sb2", [128, 8], F32)
    s3s = sbt("s3s", [128, 8], F32)
    hmax_sc = sbt("hmax_sc", [128, 8], F32)
    hm_c = sbt("hm_c", [128, 96], F32)
    wpart = sbt("wpart", [128, 128], F32)

    bigB = sbt("bigB", [128, 70400], BF16)
    bigF = sbt("bigF", [128, 10300], F32)
    bigH = sbt("bigH", [128, HID], FP16)
    ab, af = Arena(bigB, 70400), Arena(bigF, 10300)

    psA = pst("psA", [128, 1024])
    psB = pst("psB", [128, 1024])
    psZ = pst("psZ", [128, 512])
    psS = pst("psS", [128, 512])
    psW = psS
    psX1 = pst("psX1", [128, 1024], BF16)
    psX2 = pst("psX2", [128, 1024], BF16)

    SY, GP, VE, SC, PE = nc.sync, nc.gpsimd, nc.vector, nc.scalar, nc.tensor

    s_ld = Sem(nc, "s_ld")
    s_x = [Sem(nc, "s_x0"), Sem(nc, "s_x1")]
    s_w = Sem(nc, "s_w")
    s_ws = [Sem(nc, "s_ws0"), Sem(nc, "s_ws1")]
    s_ve = Sem(nc, "s_ve")
    s_ac = Sem(nc, "s_ac")
    s_pe = Sem(nc, "s_pe")
    s_wsp = Sem(nc, "s_wsp")
    s_wspy = Sem(nc, "s_wspy")
    s_kvsp = [Sem(nc, "s_kvsp0"), Sem(nc, "s_kvsp1")]
    s_zsp = [Sem(nc, "s_zsp0"), Sem(nc, "s_zsp1")]
    s_x2sp = [Sem(nc, "s_x2sp0"), Sem(nc, "s_x2sp1")]
    s_hsp = [Sem(nc, "s_hsp0"), Sem(nc, "s_hsp1")]
    s_ccm = Sem(nc, "s_ccm")
    s_ccw = [Sem(nc, "s_ccw1"), Sem(nc, "s_ccw2"), Sem(nc, "s_ccw3")]
    s_ccz = Sem(nc, "s_ccz")
    s_k = [Sem(nc, "s_k%d" % i) for i in range(4)]
    s_v = [Sem(nc, "s_v%d" % i) for i in range(4)]
    s_zl = [Sem(nc, "s_zl0"), Sem(nc, "s_zl1")]
    s_wr = [Sem(nc, "s_wr0"), Sem(nc, "s_wr1")]
    s_w3r = Sem(nc, "s_w3r")
    s_hl = [Sem(nc, "s_hl0"), Sem(nc, "s_hl1")]
    s_x2l = [Sem(nc, "s_x2l0"), Sem(nc, "s_x2l1")]
    s_ou = [Sem(nc, "s_ou0"), Sem(nc, "s_ou1")]

    for dst, src in ((id_sb, identd), (onesb, onesbd), (onesf, onesfd),
                     (tril, trild), (bq_sb, bqT), (bk_sb, bkT),
                     (bvT_sb, bvTf), (bvf_sb, bvfl)):
        s_ld.inc(SY.dma_start(out=dst[:], in_=src[:, :]), 16)
    LD0 = s_ld.n

    VE.wait_ge(s_ld.h, LD0)
    VE.memset(kmax[:, HPC:2 * HPC], 0.0)
    VE.memset(smc[:, 63:64], 1e-5)
    ve_bvtb = s_ve.inc(VE.tensor_copy(out=bvTb[:], in_=bvT_sb[:]))
    wqkv = ab.alloc(3 * HPC * DK * 128)
    OKO = HPC * DK * 128
    OVO = 2 * OKO
    for hh in range(HPC):
        for dk in range(DK):
            o = (hh * DK + dk) * 128
            s_w.inc(GP.dma_start(out=wqkv[:, o:o + 128],
                                 in_=wqd[hh, dk * 128:dk * 128 + 128, :]), 16)
            s_w.inc(GP.dma_start(out=wqkv[:, OKO + o:OKO + o + 128],
                                 in_=wkd[hh, dk * 128:dk * 128 + 128, :]), 16)
            s_w.inc(GP.dma_start(out=wqkv[:, OVO + dk * EPC + hh * DH:
                                          OVO + dk * EPC + hh * DH + DH],
                                 in_=wvd[hh, dk * 128:dk * 128 + 128, :]), 16)
    W_QKV = s_w.n
    if PHASE_LIMIT < 0.2:
        GP.wait_ge(s_w.h, W_QKV)
        return nc

    bvbc = af.alloc(EPC)
    wshf = af.alloc(2 * 1408)
    af_a2 = af.mark()

    PE.wait_ge(s_ld.h, LD0)
    pe_bv = s_pe.inc(PE.matmul(psS[:], onesf[0:1, :], bvf_sb[:, :],
                               start=True, stop=True))
    VE.wait_ge(s_pe.h, pe_bv)
    ve_bvbc = s_ve.inc(VE.tensor_copy(out=bvbc[:], in_=psS[:]))
    VE.drain()


    wsha = ab.alloc(1408)
    wshb = ab.alloc(1408)
    wshq = ab.alloc(1408)
    wsht = ab.alloc(1408)
    ab_a2 = ab.mark()

    ab.reset(ab_a2)
    af.reset(af_a2)
    x_t = af.alloc(2 * D)
    ksq = af.alloc(WIN)
    rb_f = af.alloc(WIN)
    xn_tm = ab.alloc(2 * D)
    xn_fm = ab.alloc(DK * WIN)
    q_sb = ab.alloc(HPC * WIN)
    kvst = ab.alloc(2 * WIN)
    ktile = ab.alloc(8 * 128)
    vtile = ab.alloc(8 * 128)
    pt_a = ab.alloc(1024)
    pt_b = ab.alloc(1024)
    pt_m = ab.alloc(1024)
    zst = ab.alloc(2 * WIN)

    x_rd = {}
    xn_ev = {}
    psx_ve = 0
    prod_guard = [None, None, None, None]
    kvst_sp = {0: 0, 1: 0}
    kv_spilled = {}
    exp_done = {}
    exp_bank = {0: 0, 1: 0}
    pt_cons = {0: 0, 1: 0}
    ptm_cons = 0
    kv_ld = {}
    kv_ring = {}
    zsp_par = {0: 0, 1: 0}
    norm_psS_free = ve_bvbc
    negc_psW_free = ve_bvbc
    negc_slot_rd = {}
    attn_psAB_free = None

    w1v = w1sd.rearrange("(c p) w -> p c w", p=128)
    w2v = w2sd.rearrange("(c p) w -> p c w", p=128)
    a1a_units = ([("s", w1v, scn, W12_SH, scn) for scn in range(11)]
                 + [("s", w2v, scn, W12_SH, 44 + scn) for scn in range(11)]
                 + [("c", w3sd, ch, W3_SH, 88 + ch) for ch in range(DK)])
    a1a_red = {}
    a1a_i = [0]
    a1v = {}
    a1b_units = ([("s", w1v, scn, W12_SH, 0, w1ti.ap(), scn == 10)
                  for scn in range(11)]
                 + [("s", w2v, scn, W12_SH, 1, w2ti.ap(), scn == 10)
                    for scn in range(11)]
                 + [("c", w3sd, ch, W3_SH, 2, w3ti.ap(), ch == DK - 1)
                    for ch in range(DK)])
    a1b_flat = a1b_units
    a1b_i = [0]
    a1b_st = {"ig": 0, "sp_done": {-1: 0}, "q_done": {}}

    def emit_a1a_batch(n):
        i0 = a1a_i[0]
        for i in range(i0, min(i0 + n, len(a1a_units))):
            kind, dsrc, idx, wdt, pcol = a1a_units[i]
            par = i % 2
            buf = par * 1408
            ww = 4 * wdt if kind == "s" else wdt
            if i >= 2:
                GP.wait_ge(s_ve.h, a1a_red[i - 2])
            if kind == "s":
                s_ws[par].inc(GP.dma_start(
                    out=wshf[:, buf:buf + ww].rearrange("p (c w) -> p c w",
                                                        w=wdt),
                    in_=dsrc[:, 4 * idx:4 * idx + 4, :]), 16)
            else:
                s_ws[par].inc(GP.dma_start(
                    out=wshf[:, buf:buf + ww],
                    in_=dsrc[idx * 128:idx * 128 + 128, :]), 16)
            VE.wait_ge(s_ws[par].h, s_ws[par].n)
            a1a_red[i] = s_ve.inc(VE.reduce_sum(
                out=wpart[:, pcol:pcol + 1], in_=wshf[:, buf:buf + ww],
                axis=AX.X, apply_absolute_value=True))
        a1a_i[0] = min(i0 + n, len(a1a_units))

    def emit_a1a_tail_a():
        nonlocal norm_psS_free, negc_psW_free
        VE.drain()
        VE.reduce_sum(out=smc[:, 0:1], in_=wpart[:, 0:11], axis=AX.X)
        VE.reduce_sum(out=smc[:, 1:2], in_=wpart[:, 44:55], axis=AX.X)
        VE.reduce_sum(out=smc[:, 2:3], in_=wpart[:, 88:104], axis=AX.X)
        VE.drain()
        ve_wp = s_ve.inc(VE.engine_nop())
        PE.wait_ge(s_ve.h, max(ve_wp, ve_bvbc, norm_psS_free))
        PE.wait_ge(s_ac.h, s_ac.n)
        pe_mp = s_pe.inc(PE.matmul(psS[0:1, 0:3], onesf[:, 0:1], smc[:, 0:3],
                                   start=True, stop=True))
        VE.wait_ge(s_pe.h, pe_mp)
        VE.memset(smr[:, 0:16], 0.0)
        VE.drain()
        ve_mp = s_ve.inc(VE.tensor_copy(out=smr[:, 0:3], in_=psS[0:1, 0:3]))
        VE.drain()
        GP.wait_ge(s_ve.h, ve_mp)
        s_wsp.inc(GP.dma_start(out=mp_in.ap()[:, :], in_=smr[:, 0:8]), 16)
        GP.wait_ge(s_wsp.h, s_wsp.n)
        a1v["cc_mp"] = s_ccm.inc(GP.collective_compute(
            "AllGather", ALU.bypass, replica_groups=[list(range(8))],
            ins=[mp_in.ap().opt()], outs=[mp_g.ap().opt()]))
        a1v["ve_mp"] = ve_mp
        norm_psS_free = max(norm_psS_free, ve_mp)
        negc_psW_free = max(negc_psW_free, ve_mp)

    def emit_a1a_tail_b():
        nonlocal norm_psS_free, negc_psW_free
        GP.wait_ge(s_ccm.h, a1v["cc_mp"])
        sp_mpg = s_wsp.inc(GP.dma_start(out=mp_sb[:, :], in_=mp_g.ap()[:, :]),
                           16)
        PE.wait_ge(s_wsp.h, sp_mpg)
        PE.wait_ge(s_ve.h, max(a1v["ve_mp"], norm_psS_free))
        PE.wait_ge(s_ac.h, s_ac.n)
        pe_mp2 = s_pe.inc(PE.matmul(psS[0:1, 0:8], onesf[0:8, 0:1], mp_sb[:, :],
                                    start=True, stop=True))
        VE.wait_ge(s_pe.h, pe_mp2)
        VE.tensor_scalar(out=smr[:, 8:11], in0=psS[0:1, 0:3],
                         scalar1=1.0 / NWELEM, scalar2=1e-5,
                         op0=ALU.mult, op1=ALU.max)
        VE.drain()
        ve_mean = s_ve.inc(VE.tensor_scalar_mul(out=smr[:, 12:15],
                                                in0=smr[:, 8:11], scalar1=0.5))
        VE.drain()
        PE.wait_ge(s_ve.h, ve_mean)
        pe_bc = s_pe.inc(PE.matmul(psW[:, 0:8], onesf[0:1, :], smr[:, 8:16],
                                   start=True, stop=True))
        VE.wait_ge(s_pe.h, pe_bc)
        VE.tensor_copy(out=smc[:, 8:16], in_=psW[:, 0:8])
        VE.drain()
        VE.tensor_scalar_mul(out=smc[:, 16:19], in0=smc[:, 12:15], scalar1=-1.0)
        VE.drain()
        ve_thr = s_ve.inc(VE.engine_nop())
        a1v["ve_thr"] = ve_thr
        norm_psS_free = max(norm_psS_free, ve_thr)
        negc_psW_free = max(negc_psW_free, ve_thr)

    def emit_a1b_batch(n):
        nonlocal psx_ve
        i0 = a1b_i[0]
        q_done = a1b_st["q_done"]
        sp_done = a1b_st["sp_done"]
        for ix in range(i0, min(i0 + n, len(a1b_units))):
            kind, dsrc, idx, wdt, wi, dout, lastch = a1b_units[ix]
            ig = a1b_st["ig"]
            par = ig % 2
            buf = par * 1408
            ww = 4 * wdt if kind == "s" else wdt
            if ig >= 2:
                GP.wait_ge(s_ve.h, q_done[ig - 2])
            else:
                GP.wait_ge(s_ve.h, a1v["ve_thr"])
            if kind == "s":
                s_ws[par].inc(GP.dma_start(
                    out=wshf[:, buf:buf + ww].rearrange("p (c w) -> p c w",
                                                        w=wdt),
                    in_=dsrc[:, 4 * idx:4 * idx + 4, :]), 16)
            else:
                s_ws[par].inc(GP.dma_start(
                    out=wshf[:, buf:buf + ww],
                    in_=dsrc[idx * 128:idx * 128 + 128, :]), 16)
            VE.wait_ge(s_ws[par].h, s_ws[par].n)
            if ig == 0:
                VE.wait_ge(s_ve.h, a1v["ve_thr"])
            VE.wait_ge(s_wspy.h, sp_done[ig - 1])
            VE.tensor_single_scalar(out=wsha[:, 0:ww],
                                    in_=wshf[:, buf:buf + ww],
                                    scalar=smc[:, 12 + wi:13 + wi], op=ALU.is_ge)
            VE.tensor_single_scalar(out=wshb[:, 0:ww],
                                    in_=wshf[:, buf:buf + ww],
                                    scalar=smc[:, 16 + wi:17 + wi], op=ALU.is_le)
            VE.drain()
            q_done[ig] = s_ve.inc(VE.tensor_tensor(out=wshq[:, 0:ww],
                                                   in0=wsha[:, 0:ww],
                                                   in1=wshb[:, 0:ww],
                                                   op=ALU.subtract))
            PE.wait_ge(s_ve.h, q_done[ig])
            ev = {}
            ntl, tw = (8, 128) if kind == "s" else (11, 64)
            for j in range(ntl):
                px = psX1 if j % 2 == 0 else psX2
                if j >= 2:
                    PE.wait_ge(s_ve.h, ev[j - 2])
                pet = s_pe.inc(PE.transpose(px[0:tw, 0:128],
                                            wshq[0:128, j * tw:(j + 1) * tw],
                                            id_sb[:]))
                VE.wait_ge(s_pe.h, pet)
                ev[j] = s_ve.inc(VE.tensor_copy(
                    out=wsht[0:tw, j * 128:(j + 1) * 128],
                    in_=px[0:tw, 0:128]))
            psx_ve = max(psx_ve, ev[ntl - 1])
            GP.wait_ge(s_ve.h, ev[ntl - 1])
            if kind == "s":
                wst3 = wsht[0:128, 0:1024].rearrange("p (c j t) -> p c j t",
                                                     j=2, t=128)
                for j in range(2):
                    s_wspy.inc(GP.dma_start(
                        out=dout[j * 128:(j + 1) * 128,
                                 idx * 512:(idx + 1) * 512]
                        .rearrange("p (c t) -> p c t", t=128),
                        in_=wst3[:, :, j, :]), 16)
            else:
                for j in range(ntl):
                    s_wspy.inc(GP.dma_start(
                        out=dout[j * tw:(j + 1) * tw,
                                 idx * 128:idx * 128 + 128],
                        in_=wsht[0:tw, j * 128:(j + 1) * 128]), 16)
            sp_done[ig] = s_wspy.n
            a1b_st["ig"] = ig + 1
            if lastch:
                _wt = ((w1ti, w1tg), (w2ti, w2tg), (w3ti, w3tg))[wi]
                GP.wait_ge(s_wspy.h, s_wspy.n)
                s_ccw[wi].inc(GP.collective_compute(
                    "AllGather", ALU.bypass, replica_groups=[list(range(8))],
                    ins=[_wt[0].ap().opt()], outs=[_wt[1].ap().opt()]))
        a1b_i[0] = min(i0 + n, len(a1b_units))


    for w in range(NW):
        for ts_ in range(4):
            ti = 4 * w + ts_
            p = ti % 2
            bx = p * D
            if ti >= 2:
                SY.wait_ge(s_ve.h, x_rd[ti - 2][0])
                SY.wait_ge(s_ac.h, x_rd[ti - 2][1])
            s_x[p].inc(SY.dma_start(out=x_t[:, bx:bx + D],
                                    in_=xs[ti * 128:ti * 128 + 128, :]), 16)
            xl = s_x[p].n
            VE.wait_ge(s_x[p].h, xl)
            VE.reduce_sum(out=smc[:, 20 + p:21 + p], in_=x_t[:, bx:bx + D],
                          axis=AX.X)
            SC.wait_ge(s_x[p].h, xl)
            ac_sq = s_ac.inc(SC.activation(out=bigH[:, 0:D], in_=x_t[:, bx:bx + D],
                                           func=AF.Square,
                                           accum_out=smc[:, 22 + p:23 + p]))
            VE.drain()
            VE.wait_ge(s_ac.h, ac_sq)
            VE.tensor_scalar_mul(out=smc[:, 24 + p:25 + p],
                                 in0=smc[:, 20 + p:21 + p], scalar1=INV_D)
            VE.tensor_scalar_mul(out=smc[:, 26 + p:27 + p],
                                 in0=smc[:, 22 + p:23 + p], scalar1=INV_D)
            VE.drain()
            VE.tensor_tensor(out=smc[:, 28 + p:29 + p], in0=smc[:, 24 + p:25 + p],
                             in1=smc[:, 24 + p:25 + p], op=ALU.mult)
            VE.drain()
            ve_var = s_ve.inc(VE.tensor_tensor(out=smc[:, 30 + p:31 + p],
                                               in0=smc[:, 26 + p:27 + p],
                                               in1=smc[:, 28 + p:29 + p],
                                               op=ALU.subtract))
            SC.wait_ge(s_ve.h, ve_var)
            ac_st = s_ac.inc(SC.activation(out=smc[:, 32 + p:33 + p],
                                           in_=smc[:, 30 + p:31 + p],
                                           func=AF.Sqrt, bias=smc[:, 63:64]))
            VE.wait_ge(s_ac.h, ac_st)
            VE.reciprocal(out=smc[:, 34 + p:35 + p], in_=smc[:, 32 + p:33 + p])
            VE.drain()
            VE.tensor_tensor(out=smc[:, 36 + p:37 + p], in0=smc[:, 24 + p:25 + p],
                             in1=smc[:, 34 + p:35 + p], op=ALU.mult)
            VE.drain()
            if ti >= 2:
                VE.wait_ge(s_pe.h, xn_ev[ti - 2][1])
            ve_xn = s_ve.inc(VE.tensor_scalar(out=xn_tm[:, bx:bx + D],
                                              in0=x_t[:, bx:bx + D],
                                              scalar1=smc[:, 34 + p:35 + p],
                                              scalar2=smc[:, 36 + p:37 + p],
                                              op0=ALU.mult, op1=ALU.subtract))
            x_rd[ti] = (ve_xn, ac_sq)
            PE.wait_ge(s_ve.h, ve_xn)
            evd = {}
            for dk in range(DK):
                px = psX1 if dk % 2 == 0 else psX2
                if dk >= 2:
                    PE.wait_ge(s_ve.h, evd[dk - 2])
                elif psx_ve:
                    PE.wait_ge(s_ve.h, psx_ve)
                pet = s_pe.inc(PE.transpose(px[:, 0:128],
                                            xn_tm[:, bx + dk * 128:bx + dk * 128 + 128],
                                            id_sb[:]))
                VE.wait_ge(s_pe.h, pet)
                evd[dk] = s_ve.inc(VE.tensor_copy(
                    out=xn_fm[:, dk * WIN + ts_ * 128:dk * WIN + ts_ * 128 + 128],
                    in_=px[:, 0:128]))
            psx_ve = evd[DK - 1]
            xn_ev[ti] = (evd[DK - 1], s_pe.n)
        xn_ok = xn_ev[4 * w + 3][0]

        prod_banks = [(psA, 0), (psA, 512), (psB, 0), (psB, 512)]
        si = 0
        for kind in ("k", "q"):
            wb = OKO if kind == "k" else 0
            for hh in range(HPC):
                bank, boff = prod_banks[si % 4]
                g = prod_guard[si % 4]
                if g is not None:
                    PE.wait_ge((s_ac if g[0] == "ac" else s_ve).h, g[1])
                PE.wait_ge(s_ve.h, xn_ok)
                if w == 0 and si == 0:
                    PE.wait_ge(s_w.h, W_QKV)
                last = None
                for dk in range(DK):
                    last = PE.matmul(bank[:, boff:boff + WIN],
                                     wqkv[:, wb + (hh * DK + dk) * 128:
                                          wb + (hh * DK + dk) * 128 + 128],
                                     xn_fm[:, dk * WIN:(dk + 1) * WIN],
                                     start=(dk == 0), stop=(dk == DK - 1))
                pe_prod = s_pe.inc(last)
                bias = bk_sb[:, hh:hh + 1] if kind == "k" else bq_sb[:, hh:hh + 1]
                par = si % 2
                stg = par * WIN
                SC.wait_ge(s_pe.h, pe_prod)
                if kind == "k":
                    SC.wait_ge(s_kvsp[par].h, kvst_sp[par])
                if kind == "k":
                    dst = kvst[:, stg:stg + WIN]
                else:
                    dst = q_sb[:, hh * WIN:(hh + 1) * WIN]
                ac_e = s_ac.inc(SC.activation(out=dst, in_=bank[:, boff:boff + WIN],
                                              func=AF.Identity, bias=bias))
                ac_sqr = s_ac.inc(SC.activation(out=ksq[:, :],
                                                in_=bank[:, boff:boff + WIN],
                                                func=AF.Square, bias=bias))
                prod_guard[si % 4] = ("ac", ac_sqr)
                PE.wait_ge(s_ac.h, ac_sqr)
                PE.wait_ge(s_ve.h, max(norm_psS_free, negc_psW_free))
                pe_n = s_pe.inc(PE.matmul(psS[0:1, :], onesf[:, 0:1], ksq[:, :],
                                          start=True, stop=True))
                VE.wait_ge(s_pe.h, pe_n)
                VE.reduce_max(out=smr[:, 20:21], in_=psS[0:1, :], axis=AX.X)
                VE.drain()
                if kind == "k":
                    cur = (w % 2) * HPC + hh
                    prv = ((w + 1) % 2) * HPC + hh
                    VE.tensor_tensor(out=kmax[:, cur:cur + 1],
                                     in0=kmax[:, prv:prv + 1],
                                     in1=smr[:, 20:21], op=ALU.max)
                else:
                    VE.tensor_copy(out=smr[:, 24 + hh:25 + hh], in_=smr[:, 20:21])
                VE.drain()
                norm_psS_free = s_ve.inc(VE.engine_nop())
                if kind == "k":
                    GP.wait_ge(s_ac.h, ac_e)
                    kvst_sp[par] = s_kvsp[par].inc(GP.dma_start(
                        out=k_d[hh * DH:hh * DH + DH, w * WIN:(w + 1) * WIN],
                        in_=kvst[:, stg:stg + WIN]), 16)
                si += 1
        for ts_ in range(4):
            bank, boff = prod_banks[si % 4]
            g = prod_guard[si % 4]
            if g is not None:
                PE.wait_ge((s_ac if g[0] == "ac" else s_ve).h, g[1])
            last = None
            for dk in range(DK):
                last = PE.matmul(bank[:, boff:boff + WIN],
                                 xn_fm[:, dk * WIN + ts_ * 128:dk * WIN + ts_ * 128 + 128],
                                 wqkv[:, OVO + dk * EPC:OVO + (dk + 1) * EPC],
                                 start=(dk == 0), stop=(dk == DK - 1))
            pe_v = s_pe.inc(last)
            par = si % 2
            stg = par * WIN
            VE.wait_ge(s_pe.h, pe_v)
            VE.wait_ge(s_kvsp[par].h, kvst_sp[par])
            VE.wait_ge(s_ve.h, ve_bvbc)
            ve_v = s_ve.inc(VE.scalar_tensor_tensor(
                out=kvst[:, stg:stg + WIN], in0=bank[:, boff:boff + WIN],
                scalar=1.0, in1=bvbc[:, :], op0=ALU.mult, op1=ALU.add))
            prod_guard[si % 4] = ("ve", ve_v)
            GP.wait_ge(s_ve.h, ve_v)
            kvst_sp[par] = s_kvsp[par].inc(GP.dma_start(
                out=v_d[w * WIN + ts_ * 128:w * WIN + (ts_ + 1) * 128, :],
                in_=kvst[:, stg:stg + WIN]), 16)
            si += 1
        kv_spilled[w] = (s_kvsp[0].n, s_kvsp[1].n)

        for hh in range(HPC):
            cur = (w % 2) * HPC + hh
            ncol = 40 + cur
            VE.tensor_tensor(out=smr[:, 30:31], in0=smr[:, 24 + hh:25 + hh],
                             in1=kmax[:, cur:cur + 1], op=ALU.mult)
            VE.drain()
            ve_c2 = s_ve.inc(VE.engine_nop())
            SC.wait_ge(s_ve.h, ve_c2)
            ac_c = s_ac.inc(SC.activation(out=smr[:, 32:33], in_=smr[:, 30:31],
                                          func=AF.Sqrt, scale=1.0 / 128.0))
            VE.wait_ge(s_ac.h, ac_c)
            VE.tensor_scalar_mul(out=smr[:, 34:35], in0=smr[:, 32:33],
                                 scalar1=-1.0)
            VE.drain()
            ve_nc = s_ve.inc(VE.engine_nop())
            PE.wait_ge(s_ve.h, max(ve_nc, negc_psW_free))
            pe_nc = s_pe.inc(PE.matmul(psW[:, 0:1], onesf[0:1, :], smr[:, 34:35],
                                       start=True, stop=True))
            VE.wait_ge(s_pe.h, pe_nc)
            VE.wait_ge(s_ac.h, negc_slot_rd.get(cur, 0))
            negc_psW_free = s_ve.inc(VE.tensor_copy(out=smc[:, ncol:ncol + 1],
                                                    in_=psW[:, 0:1]))
            ve_negc = negc_psW_free

            n_kt = 4 * (w + 1)
            n_g = n_kt // 2

            def load_kv(kt):
                slot = kt % 4
                for which in (0, 1):
                    g = kv_ring.get((which, slot), 0)
                    if g:
                        SY.wait_ge(s_pe.h, g)
                    SY.wait_ge(s_kvsp[0].h, kv_spilled[w][0])
                    SY.wait_ge(s_kvsp[1].h, kv_spilled[w][1])
                    if which == 0:
                        ii = SY.dma_start(
                            out=ktile[:, slot * 128:slot * 128 + 128],
                            in_=k_d[hh * DH:hh * DH + DH,
                                    kt * 128:kt * 128 + 128])
                        kv_ld[(0, kt)] = s_k[slot].inc(ii, 16)
                    else:
                        ii = SY.dma_start(
                            out=vtile[:, slot * 128:slot * 128 + 128],
                            in_=v_d[kt * 128:kt * 128 + 128,
                                    hh * DH:hh * DH + DH])
                        kv_ld[(1, kt)] = s_v[slot].inc(ii, 16)

            for kt in range(min(4, n_kt)):
                load_kv(kt)
            for gi in range(n_g):
                pp = gi % 2
                bank = psA if pp == 0 else psB
                ptb = pt_a if pp == 0 else pt_b
                kt0, kt1 = 2 * gi, 2 * gi + 1
                PE.wait_ge(s_k[kt0 % 4].h, kv_ld[(0, kt0)])
                PE.wait_ge(s_k[kt1 % 4].h, kv_ld[(0, kt1)])
                if exp_bank[pp]:
                    PE.wait_ge(s_ac.h, exp_bank[pp])
                elif gi < 2:
                    g = prod_guard[2 * pp]
                    PE.wait_ge((s_ac if g[0] == "ac" else s_ve).h, g[1])
                    g = prod_guard[2 * pp + 1]
                    PE.wait_ge((s_ac if g[0] == "ac" else s_ve).h, g[1])
                PE.matmul(bank[:, 0:WIN],
                          ktile[:, (kt0 % 4) * 128:(kt0 % 4) * 128 + 128],
                          q_sb[:, hh * WIN:(hh + 1) * WIN], start=True, stop=True)
                pe_sc = s_pe.inc(PE.matmul(
                    bank[:, WIN:2 * WIN],
                    ktile[:, (kt1 % 4) * 128:(kt1 % 4) * 128 + 128],
                    q_sb[:, hh * WIN:(hh + 1) * WIN], start=True, stop=True))
                kv_ring[(0, kt0 % 4)] = pe_sc
                kv_ring[(0, kt1 % 4)] = pe_sc
                SC.wait_ge(s_pe.h, pe_sc)
                SC.wait_ge(s_ve.h, ve_negc)
                if pt_cons[pp]:
                    SC.wait_ge(s_pe.h, pt_cons[pp])
                ac_e = s_ac.inc(SC.activation(out=ptb[:, :], in_=bank[:, :],
                                              func=AF.Exp, scale=ISQ,
                                              bias=smc[:, ncol:ncol + 1]))
                exp_bank[pp] = ac_e
                exp_done[(w, hh)] = ac_e
                negc_slot_rd[cur] = ac_e
                use = ptb
                if kt1 >= 4 * w:
                    VE.wait_ge(s_ac.h, ac_e)
                    if ptm_cons:
                        VE.wait_ge(s_pe.h, ptm_cons)
                    for which, kt in ((0, kt0), (1, kt1)):
                        if kt >= 4 * w:
                            o = (kt - 4 * w) * 128
                            VE.tensor_tensor(
                                out=pt_m[:, which * WIN:(which + 1) * WIN],
                                in0=ptb[:, which * WIN:(which + 1) * WIN],
                                in1=tril[:, 384 - o:896 - o], op=ALU.mult)
                        else:
                            VE.tensor_copy(
                                out=pt_m[:, which * WIN:(which + 1) * WIN],
                                in_=ptb[:, which * WIN:(which + 1) * WIN])
                    ve_m = s_ve.inc(VE.engine_nop())
                    use = pt_m
                    PE.wait_ge(s_ve.h, ve_m)
                else:
                    PE.wait_ge(s_ac.h, ac_e)
                first = (gi == 0)
                last_g = (gi == n_g - 1)
                PE.wait_ge(s_v[kt0 % 4].h, kv_ld[(1, kt0)])
                PE.wait_ge(s_v[kt1 % 4].h, kv_ld[(1, kt1)])
                PE.matmul(psS[0:1, :], onesb[:, 0:1], use[:, 0:WIN],
                          start=first, stop=False)
                PE.matmul(psS[0:1, :], onesb[:, 0:1], use[:, WIN:2 * WIN],
                          start=False, stop=last_g)
                PE.matmul(psZ[:], vtile[:, (kt0 % 4) * 128:(kt0 % 4) * 128 + 128],
                          use[:, 0:WIN], start=first, stop=False)
                pe_z = s_pe.inc(PE.matmul(
                    psZ[:], vtile[:, (kt1 % 4) * 128:(kt1 % 4) * 128 + 128],
                    use[:, WIN:2 * WIN], start=False, stop=last_g))
                kv_ring[(1, kt0 % 4)] = pe_z
                kv_ring[(1, kt1 % 4)] = pe_z
                pt_cons[pp] = pe_z
                if use is pt_m:
                    ptm_cons = pe_z
                for kt in (2 * gi + 4, 2 * gi + 5):
                    if kt < n_kt:
                        load_kv(kt)
            VE.wait_ge(s_pe.h, s_pe.n)
            VE.reciprocal(out=rs_sb[:, :], in_=psS[0:1, :])
            VE.drain()
            ve_rs = s_ve.inc(VE.engine_nop())
            norm_psS_free = ve_rs
            PE.wait_ge(s_ve.h, max(ve_rs, negc_psW_free))
            pe_rb = s_pe.inc(PE.matmul(psW[:, :], onesf[0:1, :], rs_sb[:, :],
                                       start=True, stop=True))
            zpar = hh % 2
            zb = zpar * WIN
            VE.wait_ge(s_pe.h, pe_rb)
            VE.wait_ge(s_zsp[zpar].h, zsp_par[zpar])
            VE.tensor_copy(out=rb_f[:, :], in_=psW[:, 0:WIN])
            VE.drain()
            ve_z = s_ve.inc(VE.tensor_tensor(out=zst[:, zb:zb + WIN], in0=psZ[:],
                                             in1=rb_f[:, :], op=ALU.mult))
            negc_psW_free = ve_z
            GP.wait_ge(s_ve.h, ve_z)
            zsp_par[zpar] = s_zsp[zpar].inc(GP.dma_start(
                out=zin[w, hh * DH:hh * DH + DH, :],
                in_=zst[:, zb:zb + WIN]), 16)
            PE.wait_ge(s_ve.h, ve_z)
        GP.wait_ge(s_zsp[0].h, zsp_par[0])
        GP.wait_ge(s_zsp[1].h, zsp_par[1])
        s_ccz.inc(GP.collective_compute(
            "AllGather", ALU.bypass, replica_groups=[[0, 1, 2, 3], [4, 5, 6, 7]],
            ins=[zin[w, :, :].opt()], outs=[zg[w, :, :].opt()]))

        if w < 4:
            emit_a1a_batch((6, 9, 11, 12)[w])
        if w == 3:
            emit_a1a_tail_a()
        if w == 4:
            emit_a1a_tail_b()
        if w >= 4:
            emit_a1b_batch((8, 10, 10, 10)[w - 4])

    A2_PE, A2_VE, A2_AC = s_pe.n, s_ve.n, s_ac.n
    emit_a1b_batch(len(a1b_units))
    assert a1b_i[0] == len(a1b_units) and a1a_i[0] == len(a1a_units)
    A1_PE, A1_VE, A1_AC = s_pe.n, s_ve.n, s_ac.n
    A1_SPILLS = s_wspy.n


    if PHASE_LIMIT < 3:
        GP.wait_ge(s_ccz.h, s_ccz.n)
        return nc
    ab.reset(0)
    wo_sb = ab.alloc(H * D)
    ztile = ab.alloc(2 * 2048)
    xq_t = ab.alloc(D)
    xq_fm = ab.alloc(DK * TPC)
    ab_b1 = ab.mark()
    af.reset(0)
    x_own = af.alloc(2 * D)
    x2_t = af.alloc(D)
    silu_s = af.alloc(2 * WIN)
    h_t = af.alloc(2 * WIN)
    r_bcf = af.alloc(D)
    af_b1 = af.mark()
    u16 = bigH[:, 0:D]

    GP.wait_ge(s_pe.h, A1_PE)
    GP.wait_ge(s_ve.h, A1_VE)
    GP.wait_ge(s_wspy.h, A1_SPILLS)
    SY.wait_ge(s_ve.h, A1_VE)
    wo_ld = {}
    for hx in range(H):
        wo_ld[hx] = s_w.inc(GP.dma_start(out=wo_sb[:, hx * D:(hx + 1) * D],
                                         in_=wod[hx, :, :]), 16)
    W_WO = s_w.n

    PE.wait_ge(s_ve.h, A1_VE)
    PE.wait_ge(s_ac.h, A1_AC)
    rb_pe = 0
    for dsd in range(4):
        if rb_pe:
            SY.wait_ge(s_pe.h, rb_pe)
        s_ld.inc(SY.dma_start(out=bo_row[:, :],
                              in_=bod[0:1, dsd * WIN:(dsd + 1) * WIN]), 16)
        BOL = s_ld.n
        PE.wait_ge(s_ld.h, BOL)
        if dsd > 0:
            PE.wait_ge(s_ve.h, rb_ve)
        rb_pe = s_pe.inc(PE.matmul(psW[:, :], onesf[0:1, :], bo_row[:, :],
                                   start=True, stop=True))
        VE.wait_ge(s_pe.h, rb_pe)
        rb_ve = s_ve.inc(VE.tensor_copy(out=r_bcf[:, dsd * WIN:(dsd + 1) * WIN],
                                        in_=psW[:, :]))
    R_DONE = s_ve.n

    SY.wait_ge(s_ccz.h, s_ccz.n)
    q2 = SY.compute_val(SY.partition_id() % 4 * 2)

    zgv = zg.rearrange("w (h e) (t c) -> w t e h c", h=H, c=128)
    xs3 = xs.rearrange("(u t c) d -> u t c d", t=4, c=128)

    zring = {-2: 0, -1: 0}
    xo_rd = {-2: 0, -1: 0}
    x2sp_par = {0: 0, 1: 0}
    opbank_ve = 0
    psxb_ve = 0
    xq_ready = {}
    for tc in range(8):
        zpar = tc % 2
        zb = zpar * 2048
        SY.wait_ge(s_pe.h, zring[tc - 2])
        s_zl[zpar].inc(SY.dma_start(
            out=ztile[:, zb:zb + 2048].rearrange("e (h c) -> e h c", h=H),
            in_=zgv[q2 + tc // 4, tc % 4, :, :, :]), 16)
        ZL = s_zl[zpar].n
        PE.wait_ge(s_zl[zpar].h, ZL)
        if tc == 0:
            PE.wait_ge(s_ve.h, R_DONE)
        if opbank_ve:
            PE.wait_ge(s_ve.h, opbank_ve)
        last = None
        for hx in range(H):
            if tc == 0:
                PE.wait_ge(s_w.h, wo_ld[hx])
            for dsd in range(4):
                bb = psA if dsd < 2 else psB
                bo_ = (dsd % 2) * 512
                last = PE.matmul(bb[:, bo_:bo_ + WIN],
                                 ztile[:, zb + hx * 128:zb + hx * 128 + 128],
                                 wo_sb[:, hx * D + dsd * WIN:hx * D + (dsd + 1) * WIN],
                                 start=(hx == 0), stop=(hx == H - 1))
        pe_op = s_pe.inc(last)
        zring[tc] = pe_op
        OP_PE = s_pe.n
        p = tc % 2
        bx = p * D
        SY.wait_ge(s_ve.h, xo_rd[tc - 2])
        SY.wait_ge(s_x2sp[p].h, x2sp_par[p])
        s_x[p].inc(SY.dma_start(out=x_own[:, bx:bx + D],
                                in_=xs3[q2 + tc // 4, tc % 4, :, :]), 16)
        XO = s_x[p].n
        VE.wait_ge(s_x[p].h, XO)
        VE.wait_ge(s_pe.h, OP_PE)
        for dsd in range(4):
            bb = psA if dsd < 2 else psB
            bo_ = (dsd % 2) * 512
            VE.scalar_tensor_tensor(
                out=x2_t[:, dsd * WIN:(dsd + 1) * WIN], in0=bb[:, bo_:bo_ + WIN],
                scalar=1.0, in1=x_own[:, bx + dsd * WIN:bx + (dsd + 1) * WIN],
                op0=ALU.mult, op1=ALU.add)
        VE.drain()
        opbank_ve = s_ve.inc(VE.engine_nop())
        VE.wait_ge(s_ve.h, max(opbank_ve, R_DONE))
        ve_x2 = s_ve.inc(VE.tensor_tensor(out=x_own[:, bx:bx + D], in0=x2_t[:, :],
                                          in1=r_bcf[:, :], op=ALU.add))
        VE.drain()
        GP.wait_ge(s_ve.h, ve_x2)
        x2sp_par[p] = s_x2sp[p].inc(GP.dma_start(
            out=x2_d[tc * 128:tc * 128 + 128, :], in_=x_own[:, bx:bx + D]), 16)
        VE.reduce_sum(out=smc[:, 20 + p:21 + p], in_=x_own[:, bx:bx + D], axis=AX.X)
        VE.reduce_max(out=smc[:, 48 + p:49 + p], in_=x_own[:, bx:bx + D], axis=AX.X)
        VE.tensor_reduce(out=smc[:, 50 + p:51 + p], in_=x_own[:, bx:bx + D],
                         axis=AX.X, op=ALU.min)
        SC.wait_ge(s_ve.h, ve_x2)
        ac_sq = s_ac.inc(SC.activation(out=bigH[:, D:2 * D], in_=x_own[:, bx:bx + D],
                                       func=AF.Square,
                                       accum_out=smc[:, 22 + p:23 + p]))
        VE.drain()
        VE.wait_ge(s_ac.h, ac_sq)
        VE.tensor_scalar_mul(out=smc[:, 24 + p:25 + p], in0=smc[:, 20 + p:21 + p],
                             scalar1=INV_D)
        VE.tensor_scalar_mul(out=smc[:, 26 + p:27 + p], in0=smc[:, 22 + p:23 + p],
                             scalar1=INV_D)
        VE.drain()
        VE.tensor_tensor(out=smc[:, 28 + p:29 + p], in0=smc[:, 24 + p:25 + p],
                         in1=smc[:, 24 + p:25 + p], op=ALU.mult)
        VE.drain()
        ve_var = s_ve.inc(VE.tensor_tensor(out=smc[:, 30 + p:31 + p],
                                           in0=smc[:, 26 + p:27 + p],
                                           in1=smc[:, 28 + p:29 + p],
                                           op=ALU.subtract))
        SC.wait_ge(s_ve.h, ve_var)
        ac_st = s_ac.inc(SC.activation(out=smc[:, 32 + p:33 + p],
                                       in_=smc[:, 30 + p:31 + p],
                                       func=AF.Sqrt, bias=smc[:, 63:64]))
        VE.wait_ge(s_ac.h, ac_st)
        VE.reciprocal(out=smc[:, 34 + p:35 + p], in_=smc[:, 32 + p:33 + p])
        VE.drain()
        VE.tensor_tensor(out=smc[:, 36 + p:37 + p], in0=smc[:, 48 + p:49 + p],
                         in1=smc[:, 24 + p:25 + p], op=ALU.subtract)
        VE.tensor_tensor(out=smc[:, 38 + p:39 + p], in0=smc[:, 24 + p:25 + p],
                         in1=smc[:, 50 + p:51 + p], op=ALU.subtract)
        VE.drain()
        VE.tensor_tensor(out=smc[:, 40 + p:41 + p], in0=smc[:, 36 + p:37 + p],
                         in1=smc[:, 38 + p:39 + p], op=ALU.max)
        VE.drain()
        VE.tensor_tensor(out=smc[:, 42 + p:43 + p], in0=smc[:, 40 + p:41 + p],
                         in1=smc[:, 34 + p:35 + p], op=ALU.mult)
        VE.drain()
        VE.tensor_scalar_max(out=smc[:, 44 + p:45 + p], in0=smc[:, 42 + p:43 + p],
                             scalar1=1e-5)
        VE.drain()
        VE.reciprocal(out=smc[:, 46 + p:47 + p], in_=smc[:, 44 + p:45 + p])
        VE.tensor_scalar_mul(out=fac_x[:, tc:tc + 1], in0=smc[:, 44 + p:45 + p],
                             scalar1=1.0 / 127.0)
        VE.drain()
        VE.tensor_scalar_mul(out=smc[:, 52 + p:53 + p], in0=smc[:, 46 + p:47 + p],
                             scalar1=127.0)
        VE.drain()
        VE.tensor_tensor(out=smc[:, 54 + p:55 + p], in0=smc[:, 34 + p:35 + p],
                         in1=smc[:, 52 + p:53 + p], op=ALU.mult)
        VE.drain()
        VE.tensor_tensor(out=smc[:, 56 + p:57 + p], in0=smc[:, 24 + p:25 + p],
                         in1=smc[:, 54 + p:55 + p], op=ALU.mult)
        VE.drain()
        VE.tensor_scalar(out=smc[:, 58 + p:59 + p], in0=smc[:, 56 + p:57 + p],
                         scalar1=-1.0, scalar2=RM, op0=ALU.mult, op1=ALU.add)
        VE.drain()
        VE.tensor_scalar(out=u16[:, :], in0=x_own[:, bx:bx + D],
                         scalar1=smc[:, 54 + p:55 + p],
                         scalar2=smc[:, 58 + p:59 + p],
                         op0=ALU.mult, op1=ALU.add)
        VE.drain()
        ve_xq = s_ve.inc(VE.tensor_single_scalar(out=xq_t[:, :], in_=u16[:, :],
                                                 scalar=-RM, op=ALU.add))
        xo_rd[tc] = ve_xq
        PE.wait_ge(s_ve.h, ve_xq)
        evd = {}
        for dk in range(DK):
            px = psX1 if dk % 2 == 0 else psX2
            if dk >= 2:
                PE.wait_ge(s_ve.h, evd[dk - 2])
            elif psxb_ve:
                PE.wait_ge(s_ve.h, psxb_ve)
            elif psx_ve:
                PE.wait_ge(s_ve.h, psx_ve)
            pet = s_pe.inc(PE.transpose(px[:, 0:128],
                                        xq_t[:, dk * 128:dk * 128 + 128],
                                        id_sb[:]))
            VE.wait_ge(s_pe.h, pet)
            evd[dk] = s_ve.inc(VE.tensor_copy(
                out=xq_fm[:, dk * TPC + tc * 128:dk * TPC + tc * 128 + 128],
                in_=px[:, 0:128]))
        psxb_ve = evd[DK - 1]
        xq_ready[tc] = evd[DK - 1]

    VE.tensor_single_scalar(out=sa1[:, 0:8], in_=fac_x[:, 0:8],
                            scalar=smc[:, 8:9], op=ALU.mult)
    VE.tensor_single_scalar(out=sb2[:, 0:8], in_=fac_x[:, 0:8],
                            scalar=smc[:, 9:10], op=ALU.mult)
    VE.drain()
    SC_READY = s_ve.inc(VE.engine_nop())
    B1_PE = s_pe.n

    if PHASE_LIMIT < 4:
        VE.wait_ge(s_ve.h, s_ve.n)
        return nc
    w12r = bigB[:, 0:2 * 2 * DK * WIN]
    wr_done = {}
    wr_used = {-2: 0, -1: 0}
    hb_guard = {0: None, 1: None}
    silu_g = {0: 0, 1: 0}
    hsp_par = {0: 0, 1: 0}
    for hs in range(NHS):
        wpar = hs % 2
        wb = wpar * (2 * DK * WIN)
        SY.wait_ge(s_pe.h, max(wr_used[hs - 2], B1_PE))
        if hs == 0:
            SY.wait_ge(s_ccw[0].h, 1)
            SY.wait_ge(s_ccw[1].h, 1)
        for dk in range(DK):
            s_wr[wpar].inc(SY.dma_start(
                out=w12r[:, wb + dk * WIN:wb + (dk + 1) * WIN],
                in_=w1tg.ap()[dk * 128:dk * 128 + 128, hs * WIN:(hs + 1) * WIN]), 16)
            s_wr[wpar].inc(SY.dma_start(
                out=w12r[:, wb + DK * WIN + dk * WIN:wb + DK * WIN + (dk + 1) * WIN],
                in_=w2tg.ap()[dk * 128:dk * 128 + 128, hs * WIN:(hs + 1) * WIN]), 16)
        wr_done[hs] = s_wr[wpar].n
        for tc in range(8):
            pp = tc % 2
            bankA = psA if pp == 0 else psB
            PE.wait_ge(s_wr[wpar].h, wr_done[hs])
            if hs == 0:
                PE.wait_ge(s_ve.h, xq_ready[tc])
                if tc < 2:
                    PE.wait_ge(s_ve.h, opbank_ve)
            g = hb_guard[pp]
            if g is not None:
                PE.wait_ge(s_ve.h, g[0])
                PE.wait_ge(s_ac.h, g[1])
            last = None
            for dk in range(DK):
                st, en = (dk == 0), (dk == DK - 1)
                PE.matmul(bankA[:, 0:WIN],
                          xq_fm[:, dk * TPC + tc * 128:dk * TPC + tc * 128 + 128],
                          w12r[:, wb + dk * WIN:wb + (dk + 1) * WIN],
                          start=st, stop=en)
                last = PE.matmul(bankA[:, WIN:2 * WIN],
                                 xq_fm[:, dk * TPC + tc * 128:dk * TPC + tc * 128 + 128],
                                 w12r[:, wb + DK * WIN + dk * WIN:
                                      wb + DK * WIN + (dk + 1) * WIN],
                                 start=st, stop=en)
            pe_mm = s_pe.inc(last)
            wr_used[hs] = pe_mm
            sst = pp * WIN
            SC.wait_ge(s_pe.h, pe_mm)
            if hs == 0 and tc == 0:
                SC.wait_ge(s_ve.h, SC_READY)
            if silu_g[pp]:
                SC.wait_ge(s_ve.h, silu_g[pp])
            VE.wait_ge(s_hsp[pp].h, hsp_par[pp])
            ac_si = s_ac.inc(SC.activation(out=silu_s[:, sst:sst + WIN],
                                           in_=bankA[:, 0:WIN], func=AF.Silu,
                                           scale=sa1[:, tc:tc + 1]))
            VE.wait_ge(s_ac.h, ac_si)
            ve_h = s_ve.inc(VE.scalar_tensor_tensor(
                out=h_t[:, sst:sst + WIN], in0=bankA[:, WIN:2 * WIN],
                scalar=sb2[:, tc:tc + 1], in1=silu_s[:, sst:sst + WIN],
                op0=ALU.mult, op1=ALU.mult))
            silu_g[pp] = ve_h
            VE.drain()
            ve_hm = s_ve.inc(VE.reduce_max(
                out=hm_c[:, tc * 12 + hs:tc * 12 + hs + 1],
                in_=h_t[:, sst:sst + WIN], axis=AX.X, apply_absolute_value=True))
            hb_guard[pp] = (ve_h, ac_si)
            GP.wait_ge(s_ve.h, ve_hm)
            hsp_par[pp] = s_hsp[pp].inc(GP.dma_start(
                out=h_d[tc * 128:tc * 128 + 128, hs * WIN:(hs + 1) * WIN],
                in_=h_t[:, sst:sst + WIN]), 16)
    B3_PE = s_pe.n
    VE.drain()
    for tc in range(8):
        p = tc % 2
        VE.reduce_max(out=smc[:, 20 + p:21 + p],
                      in_=hm_c[:, tc * 12:tc * 12 + 11], axis=AX.X)
        VE.drain()
        VE.tensor_scalar_max(out=smc[:, 22 + p:23 + p],
                             in0=smc[:, 20 + p:21 + p], scalar1=1e-5)
        VE.drain()
        VE.reciprocal(out=smc[:, 24 + p:25 + p], in_=smc[:, 22 + p:23 + p])
        VE.tensor_scalar_mul(out=smc[:, 26 + p:27 + p], in0=smc[:, 22 + p:23 + p],
                             scalar1=1.0 / 127.0)
        VE.drain()
        VE.tensor_scalar_mul(out=hmax_sc[:, tc:tc + 1], in0=smc[:, 24 + p:25 + p],
                             scalar1=127.0)
        VE.tensor_single_scalar(out=s3s[:, tc:tc + 1], in_=smc[:, 26 + p:27 + p],
                                scalar=smc[:, 10:11], op=ALU.mult)
        VE.drain()
    HS_READY = s_ve.inc(VE.engine_nop())
    B3_VE = s_ve.n

    if PHASE_LIMIT < 5:
        GP.wait_ge(s_hsp[0].h, s_hsp[0].n)
        GP.wait_ge(s_hsp[1].h, s_hsp[1].n)
        return nc
    ab.reset(0)
    hq_fm = ab.alloc(HKN * TPC)
    hq_t = ab.alloc(HID // 2)
    w3r = ab.alloc(HKN * WIN)
    af.reset(0)
    h_ld = af.alloc(HID)
    x2_ld = af.alloc(2 * WIN)
    out_st = af.alloc(2 * WIN)
    u16h = bigH[:, 0:HID]

    psxc_ve = psxb_ve
    SY.wait_ge(s_hsp[0].h, s_hsp[0].n)
    SY.wait_ge(s_hsp[1].h, s_hsp[1].n)
    SY.wait_ge(s_x2sp[0].h, s_x2sp[0].n)
    SY.wait_ge(s_x2sp[1].h, s_x2sp[1].n)
    HHALF = HID // 2
    QH = HHALF // 2
    NTQ = QH // 128
    hl_rd = {-2: 0, -1: 0}
    hl_ld = {}
    qt_pe = {-2: 0, -1: 0}
    pxg = {0: psxc_ve, 1: psxc_ve}

    def issue_hl(idx):
        if idx >= 16:
            return
        tc_, hf_ = idx // 2, idx % 2
        sl = idx % 2
        SY.wait_ge(s_ve.h, hl_rd[idx - 2])
        hl_ld[idx] = s_hl[sl].inc(SY.dma_start(
            out=h_ld[:, sl * HHALF:(sl + 1) * HHALF],
            in_=h_d[tc_ * 128:tc_ * 128 + 128,
                    hf_ * HHALF:(hf_ + 1) * HHALF]), 16)

    issue_hl(0)
    issue_hl(1)
    for idx in range(16):
        tc, hf = idx // 2, idx % 2
        sl = idx % 2
        VE.wait_ge(s_hl[sl].h, hl_ld[idx])
        if idx == 0:
            VE.wait_ge(s_ve.h, HS_READY)
        for q in range(2):
            gq = idx * 2 + q
            qsl = gq % 2
            VE.wait_ge(s_pe.h, qt_pe[gq - 2])
            VE.tensor_scalar(out=u16h[:, qsl * QH:(qsl + 1) * QH],
                             in0=h_ld[:, sl * HHALF + q * QH:
                                      sl * HHALF + (q + 1) * QH],
                             scalar1=hmax_sc[:, tc:tc + 1], scalar2=RM,
                             op0=ALU.mult, op1=ALU.add)
            VE.drain()
            ve_hq = s_ve.inc(VE.tensor_single_scalar(
                out=hq_t[:, qsl * QH:(qsl + 1) * QH],
                in_=u16h[:, qsl * QH:(qsl + 1) * QH],
                scalar=-RM, op=ALU.add))
            if q == 1:
                hl_rd[idx] = ve_hq
            PE.wait_ge(s_ve.h, ve_hq)
            if gq == 0:
                PE.wait_ge(s_pe.h, B3_PE)
            for hj in range(NTQ):
                hk = hf * (HKN // 2) + q * NTQ + hj
                b = hj % 2
                PE.wait_ge(s_ve.h, pxg[b])
                pet = s_pe.inc(PE.transpose(
                    (psX1 if b == 0 else psX2)[:, 0:128],
                    hq_t[:, qsl * QH + hj * 128:qsl * QH + hj * 128 + 128],
                    id_sb[:]))
                VE.wait_ge(s_pe.h, pet)
                pxg[b] = s_ve.inc(VE.tensor_copy(
                    out=hq_fm[:, hk * TPC + tc * 128:hk * TPC + tc * 128 + 128],
                    in_=(psX1 if b == 0 else psX2)[:, 0:128]))
            qt_pe[gq] = s_pe.n
        issue_hl(idx + 2)
    HQ_ALL = s_ve.n

    if PHASE_LIMIT < 6:
        VE.wait_ge(s_ve.h, s_ve.n)
        return nc
    w3_used = {-1: 0}
    b5_g = {0: 0, 1: 0}
    out_g = {0: 0, 1: 0}
    x2l = {-2: 0, -1: 0}
    it = 0
    for dq in range(4):
        SY.wait_ge(s_pe.h, max(w3_used[dq - 1], B3_PE))
        if dq == 0:
            SY.wait_ge(s_ve.h, B3_VE)
        if dq == 0:
            SY.wait_ge(s_ccw[2].h, 1)
        W3L = 0
        for hk in range(HKN):
            W3L = s_w3r.inc(SY.dma_start(
                out=w3r[:, hk * WIN:(hk + 1) * WIN],
                in_=w3tg.ap()[hk * 128:hk * 128 + 128, dq * WIN:(dq + 1) * WIN]),
                16)
        for tc in range(8):
            pp = tc % 2
            bank = psZ if pp == 0 else psS
            PE.wait_ge(s_w3r.h, W3L)
            if dq == 0 and tc == 0:
                PE.wait_ge(s_ve.h, HQ_ALL)
            if b5_g[pp]:
                PE.wait_ge(s_ve.h, b5_g[pp])
            last = None
            for hk in range(HKN):
                last = PE.matmul(bank[:],
                                 hq_fm[:, hk * TPC + tc * 128:hk * TPC + tc * 128 + 128],
                                 w3r[:, hk * WIN:(hk + 1) * WIN],
                                 start=(hk == 0), stop=(hk == HKN - 1))
            pe_m3 = s_pe.inc(last)
            w3_used[dq] = pe_m3
            xb = pp * WIN
            SY.wait_ge(s_ve.h, x2l[it - 2])
            if dq == 0 and tc == 0:
                SY.wait_ge(s_x2sp[0].h, s_x2sp[0].n)
                SY.wait_ge(s_x2sp[1].h, s_x2sp[1].n)
            X2L = s_x2l[pp].inc(SY.dma_start(
                out=x2_ld[:, xb:xb + WIN],
                in_=x2_d[tc * 128:tc * 128 + 128,
                         dq * WIN:(dq + 1) * WIN]), 16)
            VE.wait_ge(s_x2l[pp].h, X2L)
            VE.wait_ge(s_pe.h, pe_m3)
            if out_g[pp]:
                VE.wait_ge(s_ou[pp].h, out_g[pp])
            ve_o = s_ve.inc(VE.scalar_tensor_tensor(
                out=out_st[:, xb:xb + WIN], in0=bank[:], scalar=s3s[:, tc:tc + 1],
                in1=x2_ld[:, xb:xb + WIN], op0=ALU.mult, op1=ALU.add))
            b5_g[pp] = ve_o
            x2l[it] = ve_o
            SY.wait_ge(s_ve.h, ve_o)
            out_g[pp] = s_ou[pp].inc(SY.dma_start(
                out=out_d[tc * 128:tc * 128 + 128, dq * WIN:(dq + 1) * WIN],
                in_=out_st[:, xb:xb + WIN]), 16)
            it += 1
    for pp in (0, 1):
        SY.wait_ge(s_ou[pp].h, s_ou[pp].n)
        GP.wait_ge(s_ou[pp].h, s_ou[pp].n)
        VE.wait_ge(s_ou[pp].h, s_ou[pp].n)

    return nc


_NC_CACHE = {}


def kernel(L0_attn_x, L0_attn_W_Q, L0_attn_W_K, L0_attn_W_V, L0_attn_W_O,
           L0_attn_b_Q, L0_attn_b_K, L0_attn_b_V, L0_attn_b_O,
           L0_ffn_w1, L0_ffn_w2, L0_ffn_w3):
    x = np.ascontiguousarray(np.asarray(L0_attn_x, dtype=np.float32))
    WQ = np.asarray(L0_attn_W_Q, dtype=np.float32)
    WK = np.asarray(L0_attn_W_K, dtype=np.float32)
    WV = np.asarray(L0_attn_W_V, dtype=np.float32)
    WO = np.ascontiguousarray(np.asarray(L0_attn_W_O, dtype=np.float32))
    bQ = np.asarray(L0_attn_b_Q, dtype=np.float32)
    bK = np.asarray(L0_attn_b_K, dtype=np.float32)
    bV = np.asarray(L0_attn_b_V, dtype=np.float32)
    bO = np.asarray(L0_attn_b_O, dtype=np.float32).reshape(1, D)
    w1 = np.asarray(L0_ffn_w1, dtype=np.float32)
    w2 = np.asarray(L0_ffn_w2, dtype=np.float32)
    w3 = np.asarray(L0_ffn_w3, dtype=np.float32)

    import ml_dtypes
    ident = np.eye(128, dtype=ml_dtypes.bfloat16)
    onesb = np.ones((128, 2), dtype=ml_dtypes.bfloat16)
    onesf = np.ones((128, 128), dtype=np.float32)
    uu = np.arange(896)[None, :]
    ii = np.arange(128)[:, None]
    tril = (uu >= ii + 384).astype(ml_dtypes.bfloat16)

    bqT = np.ascontiguousarray(bQ.T)
    bkT = np.ascontiguousarray(bK.T)
    bvT = np.ascontiguousarray(bV.T)

    if "nc" not in _NC_CACHE:
        _NC_CACHE["nc"] = build_program()
    nc = _NC_CACHE["nc"]

    in_maps = []
    for c in range(8):
        s = c // 4
        g = c % 4
        hs = slice(4 * g, 4 * g + 4)
        in_maps.append({
            "xs": np.ascontiguousarray(x[s]),
            "wq": np.ascontiguousarray(WQ[hs]),
            "wk": np.ascontiguousarray(WK[hs]),
            "wv": np.ascontiguousarray(WV[hs]),
            "wo": WO,
            "bqT": np.ascontiguousarray(bqT[:, hs]),
            "bkT": np.ascontiguousarray(bkT[:, hs]),
            "bvTf": bvT,
            "bvfl": np.ascontiguousarray(bV[hs].reshape(1, EPC)),
            "bo": bO,
            "w1s": np.ascontiguousarray(w1[:, c * W12_SH:(c + 1) * W12_SH]),
            "w2s": np.ascontiguousarray(w2[:, c * W12_SH:(c + 1) * W12_SH]),
            "w3s": np.ascontiguousarray(w3[:, c * W3_SH:(c + 1) * W3_SH]),
            "ident": ident,
            "onesb": onesb,
            "onesf": onesf,
            "tril": tril,
        })
    res = run_bass_kernel_spmd(nc, in_maps, core_ids=list(range(8)))
    out = np.empty((2, S, D), dtype=np.float32)
    for c in range(8):
        s, g = c // 4, c % 4
        out[s, g * TPC:(g + 1) * TPC, :] = res.results[c]["out"]
    return out


# version 13
# speedup vs baseline: 1.0332x; 1.0332x over previous
import math

import numpy as np
import concourse.bass as bass
import concourse.mybir as mybir
from concourse.bass_utils import run_bass_kernel_spmd

F32 = mybir.dt.float32
BF16 = mybir.dt.bfloat16
FP16 = mybir.dt.float16
AF = mybir.ActivationFunctionType
ALU = mybir.AluOpType
AX = mybir.AxisListType

S, D, H, DH = 4096, 2048, 16, 128
HID = 5632
HPC, EPC, TPC, WIN = 4, 512, 1024, 512
NW = S // WIN
DK = D // 128
HKN = HID // 128
NHS = HID // WIN
W12_SH = D // 8
W3_SH = HID // 8
ISQ = 1.0 / math.sqrt(DH)
RM = 1536.0
NWELEM = float(HID * D)
INV_D = 1.0 / D


class Sem:
    def __init__(self, nc, name):
        self.h = nc.semaphore(name).__enter__()
        self.n = 0

    def inc(self, instr, by=1):
        instr.then_inc(self.h, by)
        self.n += by
        return self.n


class Arena:
    def __init__(self, t, cols):
        self.t, self.cols, self.cur = t, cols, 0

    def alloc(self, n):
        assert self.cur + n <= self.cols, ("arena", self.cur, n, self.cols)
        off = self.cur
        self.cur += n
        return self.t[:, off:off + n]

    def reset(self, m=0):
        self.cur = m

    def mark(self):
        return self.cur


DEBUG_INTERNALS = False
PHASE_LIMIT = 99
A1B_CHUNKS = None
A1B_ONLY_W3 = False


def build_program():
    nc = bass.Bass("TRN2", target_bir_lowering=False, debug=False)
    dbg = "ExternalOutput" if DEBUG_INTERNALS else "Internal"

    xs = nc.dram_tensor("xs", [S, D], F32, kind="ExternalInput").ap()
    wqd = nc.dram_tensor("wq", [HPC, D, DH], F32, kind="ExternalInput").ap()
    wkd = nc.dram_tensor("wk", [HPC, D, DH], F32, kind="ExternalInput").ap()
    wvd = nc.dram_tensor("wv", [HPC, D, DH], F32, kind="ExternalInput").ap()
    wod = nc.dram_tensor("wo", [H, DH, D], F32, kind="ExternalInput").ap()
    bqT = nc.dram_tensor("bqT", [DH, HPC], F32, kind="ExternalInput").ap()
    bkT = nc.dram_tensor("bkT", [DH, HPC], F32, kind="ExternalInput").ap()
    bvTf = nc.dram_tensor("bvTf", [DH, H], F32, kind="ExternalInput").ap()
    bvfl = nc.dram_tensor("bvfl", [1, EPC], F32, kind="ExternalInput").ap()
    bod = nc.dram_tensor("bo", [1, D], F32, kind="ExternalInput").ap()
    w1sd = nc.dram_tensor("w1s", [HID, W12_SH], F32, kind="ExternalInput").ap()
    w2sd = nc.dram_tensor("w2s", [HID, W12_SH], F32, kind="ExternalInput").ap()
    w3sd = nc.dram_tensor("w3s", [D, W3_SH], F32, kind="ExternalInput").ap()
    identd = nc.dram_tensor("ident", [128, 128], BF16, kind="ExternalInput").ap()
    onesbd = nc.dram_tensor("onesb", [128, 2], BF16, kind="ExternalInput").ap()
    onesfd = nc.dram_tensor("onesf", [128, 128], F32, kind="ExternalInput").ap()
    trild = nc.dram_tensor("tril", [128, 896], BF16, kind="ExternalInput").ap()
    out_d = nc.dram_tensor("out", [TPC, D], F32, kind="ExternalOutput").ap()

    k_d = nc.dram_tensor("k_d", [EPC, S], BF16, kind=dbg).ap()
    v_d = nc.dram_tensor("v_d", [S, EPC], BF16, kind=dbg).ap()
    zin_t = nc.dram_tensor("zin", [NW, EPC, WIN], BF16, kind=dbg)
    zg_t = nc.dram_tensor("zg", [NW, D, WIN], BF16)
    zin, zg = zin_t.ap(), zg_t.ap()
    mp_in = nc.dram_tensor("mp_in", [1, 8], F32)
    mp_g = nc.dram_tensor("mp_g", [8, 8], F32)
    w1ti = nc.dram_tensor("w1t_in", [W12_SH, HID], BF16, kind=dbg)
    w2ti = nc.dram_tensor("w2t_in", [W12_SH, HID], BF16)
    w3ti = nc.dram_tensor("w3t_in", [W3_SH, D], BF16)
    w1tg = nc.dram_tensor("w1t_g", [D, HID], BF16)
    w2tg = nc.dram_tensor("w2t_g", [D, HID], BF16)
    w3tg = nc.dram_tensor("w3t_g", [HID, D], BF16)
    x2_d = nc.dram_tensor("x2_d", [TPC, D], F32, kind=dbg).ap()
    h_d = nc.dram_tensor("h_d", [TPC, HID], F32, kind=dbg).ap()

    def sbt(name, shape, dt):
        return nc.sbuf_tensor(name, shape, dt).__enter__()

    def pst(name, shape, dt=F32):
        return nc.psum_tensor(name, shape, dt).__enter__()

    id_sb = sbt("id_sb", [128, 128], BF16)
    onesb = sbt("onesb_s", [128, 2], BF16)
    onesf = sbt("onesf_s", [128, 128], F32)
    tril = sbt("tril_s", [128, 896], BF16)
    bq_sb = sbt("bq_sb", [128, HPC], F32)
    bk_sb = sbt("bk_sb", [128, HPC], F32)
    bvT_sb = sbt("bvT_sb", [128, H], F32)
    bvTb = sbt("bvTb", [128, H], BF16)
    bo_row = sbt("bo_row", [1, WIN], F32)
    bvf_sb = sbt("bvf_sb", [1, EPC], F32)
    smr = sbt("smr", [1, 64], F32)
    smc = sbt("smc", [128, 64], F32)
    mp_sb = sbt("mp_sb", [8, 8], F32)
    kmax = sbt("kmax", [1, 2 * HPC], F32)
    rs_sb = sbt("rs_sb", [1, WIN], F32)
    fac_x = sbt("fac_x", [128, 8], F32)
    sa1 = sbt("sa1", [128, 8], F32)
    sb2 = sbt("sb2", [128, 8], F32)
    s3s = sbt("s3s", [128, 8], F32)
    hmax_sc = sbt("hmax_sc", [128, 8], F32)
    hm_c = sbt("hm_c", [128, 96], F32)
    wpart = sbt("wpart", [128, 128], F32)

    bigB = sbt("bigB", [128, 70400], BF16)
    bigF = sbt("bigF", [128, 10300], F32)
    bigH = sbt("bigH", [128, HID], FP16)
    ab, af = Arena(bigB, 70400), Arena(bigF, 10300)

    psA = pst("psA", [128, 1024])
    psB = pst("psB", [128, 1024])
    psZ = pst("psZ", [128, 512])
    psS = pst("psS", [128, 512])
    psW = psS
    psX1 = pst("psX1", [128, 1024], BF16)
    psX2 = pst("psX2", [128, 1024], BF16)

    SY, GP, VE, SC, PE = nc.sync, nc.gpsimd, nc.vector, nc.scalar, nc.tensor

    s_ld = Sem(nc, "s_ld")
    s_x = [Sem(nc, "s_x0"), Sem(nc, "s_x1")]
    s_w = Sem(nc, "s_w")
    s_ws = [Sem(nc, "s_ws0"), Sem(nc, "s_ws1")]
    s_ve = Sem(nc, "s_ve")
    s_ac = Sem(nc, "s_ac")
    s_pe = Sem(nc, "s_pe")
    s_wsp = Sem(nc, "s_wsp")
    s_wspy = Sem(nc, "s_wspy")
    s_kvsp = [Sem(nc, "s_kvsp0"), Sem(nc, "s_kvsp1")]
    s_zsp = [Sem(nc, "s_zsp0"), Sem(nc, "s_zsp1")]
    s_x2sp = [Sem(nc, "s_x2sp0"), Sem(nc, "s_x2sp1")]
    s_hsp = [Sem(nc, "s_hsp0"), Sem(nc, "s_hsp1")]
    s_ccm = Sem(nc, "s_ccm")
    s_ccw = [Sem(nc, "s_ccw1"), Sem(nc, "s_ccw2"), Sem(nc, "s_ccw3")]
    s_ccz = Sem(nc, "s_ccz")
    s_k = [Sem(nc, "s_k%d" % i) for i in range(4)]
    s_v = [Sem(nc, "s_v%d" % i) for i in range(4)]
    s_zl = [Sem(nc, "s_zl0"), Sem(nc, "s_zl1")]
    s_wr = [Sem(nc, "s_wr0"), Sem(nc, "s_wr1")]
    s_w3r = Sem(nc, "s_w3r")
    s_hl = [Sem(nc, "s_hl0"), Sem(nc, "s_hl1")]
    s_x2l = [Sem(nc, "s_x2l0"), Sem(nc, "s_x2l1")]
    s_ou = [Sem(nc, "s_ou0"), Sem(nc, "s_ou1")]

    for dst, src in ((id_sb, identd), (onesb, onesbd), (onesf, onesfd),
                     (tril, trild), (bq_sb, bqT), (bk_sb, bkT),
                     (bvT_sb, bvTf), (bvf_sb, bvfl)):
        s_ld.inc(SY.dma_start(out=dst[:], in_=src[:, :]), 16)
    LD0 = s_ld.n

    VE.wait_ge(s_ld.h, LD0)
    VE.memset(kmax[:, HPC:2 * HPC], 0.0)
    VE.memset(smc[:, 63:64], 1e-5)
    ve_bvtb = s_ve.inc(VE.tensor_copy(out=bvTb[:], in_=bvT_sb[:]))
    wqkv = ab.alloc(3 * HPC * DK * 128)
    OKO = HPC * DK * 128
    OVO = 2 * OKO
    for hh in range(HPC):
        for dk in range(DK):
            o = (hh * DK + dk) * 128
            s_w.inc(GP.dma_start(out=wqkv[:, o:o + 128],
                                 in_=wqd[hh, dk * 128:dk * 128 + 128, :]), 16)
            s_w.inc(GP.dma_start(out=wqkv[:, OKO + o:OKO + o + 128],
                                 in_=wkd[hh, dk * 128:dk * 128 + 128, :]), 16)
            s_w.inc(GP.dma_start(out=wqkv[:, OVO + dk * EPC + hh * DH:
                                          OVO + dk * EPC + hh * DH + DH],
                                 in_=wvd[hh, dk * 128:dk * 128 + 128, :]), 16)
    W_QKV = s_w.n
    if PHASE_LIMIT < 0.2:
        GP.wait_ge(s_w.h, W_QKV)
        return nc

    bvbc = af.alloc(EPC)
    wshf = af.alloc(2 * 1408)
    af_a2 = af.mark()

    PE.wait_ge(s_ld.h, LD0)
    pe_bv = s_pe.inc(PE.matmul(psS[:], onesf[0:1, :], bvf_sb[:, :],
                               start=True, stop=True))
    VE.wait_ge(s_pe.h, pe_bv)
    ve_bvbc = s_ve.inc(VE.tensor_copy(out=bvbc[:], in_=psS[:]))
    VE.drain()


    wsha = ab.alloc(1408)
    wshb = ab.alloc(1408)
    wshq = ab.alloc(1408)
    wsht = ab.alloc(1408)
    ab_a2 = ab.mark()

    ab.reset(ab_a2)
    af.reset(af_a2)
    x_t = af.alloc(2 * D)
    ksq = af.alloc(WIN)
    rb_f = af.alloc(WIN)
    xn_tm = ab.alloc(2 * D)
    xn_fm = ab.alloc(DK * WIN)
    q_sb = ab.alloc(HPC * WIN)
    kvst = ab.alloc(2 * WIN)
    ktile = ab.alloc(8 * 128)
    vtile = ab.alloc(8 * 128)
    pt_a = ab.alloc(1024)
    pt_b = ab.alloc(1024)
    pt_m = ab.alloc(1024)
    zst = ab.alloc(2 * WIN)

    x_rd = {}
    xn_ev = {}
    psx_ve = 0
    prod_guard = [None, None, None, None]
    kvst_sp = {0: 0, 1: 0}
    kv_spilled = {}
    exp_done = {}
    exp_bank = {0: 0, 1: 0}
    pt_cons = {0: 0, 1: 0}
    ptm_cons = 0
    kv_ld = {}
    kv_ring = {}
    zsp_par = {0: 0, 1: 0}
    norm_psS_free = ve_bvbc
    negc_psW_free = ve_bvbc
    negc_slot_rd = {}
    attn_psAB_free = None

    w1v = w1sd.rearrange("(c p) w -> p c w", p=128)
    w2v = w2sd.rearrange("(c p) w -> p c w", p=128)
    a1a_units = ([("s", w1v, scn, W12_SH, scn) for scn in range(11)]
                 + [("s", w2v, scn, W12_SH, 44 + scn) for scn in range(11)]
                 + [("c", w3sd, ch, W3_SH, 88 + ch) for ch in range(DK)])
    a1a_red = {}
    a1a_i = [0]
    a1v = {}
    a1b_units = ([("s", w1v, scn, W12_SH, 0, w1ti.ap(), scn == 10)
                  for scn in range(11)]
                 + [("s", w2v, scn, W12_SH, 1, w2ti.ap(), scn == 10)
                    for scn in range(11)]
                 + [("c", w3sd, ch, W3_SH, 2, w3ti.ap(), ch == DK - 1)
                    for ch in range(DK)])
    a1b_flat = a1b_units
    a1b_i = [0]
    a1b_st = {"ig": 0, "sp_done": {-1: 0}, "q_done": {}}

    def emit_a1a_batch(n):
        i0 = a1a_i[0]
        for i in range(i0, min(i0 + n, len(a1a_units))):
            kind, dsrc, idx, wdt, pcol = a1a_units[i]
            par = i % 2
            buf = par * 1408
            ww = 4 * wdt if kind == "s" else wdt
            if i >= 2:
                GP.wait_ge(s_ve.h, a1a_red[i - 2])
            if kind == "s":
                s_ws[par].inc(GP.dma_start(
                    out=wshf[:, buf:buf + ww].rearrange("p (c w) -> p c w",
                                                        w=wdt),
                    in_=dsrc[:, 4 * idx:4 * idx + 4, :]), 16)
            else:
                s_ws[par].inc(GP.dma_start(
                    out=wshf[:, buf:buf + ww],
                    in_=dsrc[idx * 128:idx * 128 + 128, :]), 16)
            VE.wait_ge(s_ws[par].h, s_ws[par].n)
            a1a_red[i] = s_ve.inc(VE.reduce_sum(
                out=wpart[:, pcol:pcol + 1], in_=wshf[:, buf:buf + ww],
                axis=AX.X, apply_absolute_value=True))
        a1a_i[0] = min(i0 + n, len(a1a_units))

    def emit_a1a_tail_a():
        nonlocal norm_psS_free, negc_psW_free
        VE.drain()
        VE.reduce_sum(out=smc[:, 0:1], in_=wpart[:, 0:11], axis=AX.X)
        VE.reduce_sum(out=smc[:, 1:2], in_=wpart[:, 44:55], axis=AX.X)
        VE.reduce_sum(out=smc[:, 2:3], in_=wpart[:, 88:104], axis=AX.X)
        VE.drain()
        ve_wp = s_ve.inc(VE.engine_nop())
        PE.wait_ge(s_ve.h, max(ve_wp, ve_bvbc, norm_psS_free))
        PE.wait_ge(s_ac.h, s_ac.n)
        pe_mp = s_pe.inc(PE.matmul(psS[0:1, 0:3], onesf[:, 0:1], smc[:, 0:3],
                                   start=True, stop=True))
        VE.wait_ge(s_pe.h, pe_mp)
        VE.memset(smr[:, 0:16], 0.0)
        VE.drain()
        ve_mp = s_ve.inc(VE.tensor_copy(out=smr[:, 0:3], in_=psS[0:1, 0:3]))
        VE.drain()
        GP.wait_ge(s_ve.h, ve_mp)
        s_wsp.inc(GP.dma_start(out=mp_in.ap()[:, :], in_=smr[:, 0:8]), 16)
        GP.wait_ge(s_wsp.h, s_wsp.n)
        a1v["cc_mp"] = s_ccm.inc(GP.collective_compute(
            "AllGather", ALU.bypass, replica_groups=[list(range(8))],
            ins=[mp_in.ap().opt()], outs=[mp_g.ap().opt()]))
        a1v["ve_mp"] = ve_mp
        norm_psS_free = max(norm_psS_free, ve_mp)
        negc_psW_free = max(negc_psW_free, ve_mp)

    def emit_a1a_tail_b():
        nonlocal norm_psS_free, negc_psW_free
        GP.wait_ge(s_ccm.h, a1v["cc_mp"])
        sp_mpg = s_wsp.inc(GP.dma_start(out=mp_sb[:, :], in_=mp_g.ap()[:, :]),
                           16)
        PE.wait_ge(s_wsp.h, sp_mpg)
        PE.wait_ge(s_ve.h, max(a1v["ve_mp"], norm_psS_free))
        PE.wait_ge(s_ac.h, s_ac.n)
        pe_mp2 = s_pe.inc(PE.matmul(psS[0:1, 0:8], onesf[0:8, 0:1], mp_sb[:, :],
                                    start=True, stop=True))
        VE.wait_ge(s_pe.h, pe_mp2)
        VE.tensor_scalar(out=smr[:, 8:11], in0=psS[0:1, 0:3],
                         scalar1=1.0 / NWELEM, scalar2=1e-5,
                         op0=ALU.mult, op1=ALU.max)
        VE.drain()
        ve_mean = s_ve.inc(VE.tensor_scalar_mul(out=smr[:, 12:15],
                                                in0=smr[:, 8:11], scalar1=0.5))
        VE.drain()
        PE.wait_ge(s_ve.h, ve_mean)
        pe_bc = s_pe.inc(PE.matmul(psW[:, 0:8], onesf[0:1, :], smr[:, 8:16],
                                   start=True, stop=True))
        VE.wait_ge(s_pe.h, pe_bc)
        VE.tensor_copy(out=smc[:, 8:16], in_=psW[:, 0:8])
        VE.drain()
        VE.tensor_scalar_mul(out=smc[:, 16:19], in0=smc[:, 12:15], scalar1=-1.0)
        VE.drain()
        ve_thr = s_ve.inc(VE.engine_nop())
        a1v["ve_thr"] = ve_thr
        norm_psS_free = max(norm_psS_free, ve_thr)
        negc_psW_free = max(negc_psW_free, ve_thr)

    def emit_a1b_batch(n):
        nonlocal psx_ve
        i0 = a1b_i[0]
        q_done = a1b_st["q_done"]
        sp_done = a1b_st["sp_done"]
        for ix in range(i0, min(i0 + n, len(a1b_units))):
            kind, dsrc, idx, wdt, wi, dout, lastch = a1b_units[ix]
            ig = a1b_st["ig"]
            par = ig % 2
            buf = par * 1408
            ww = 4 * wdt if kind == "s" else wdt
            if ig >= 2:
                GP.wait_ge(s_ve.h, q_done[ig - 2])
            else:
                GP.wait_ge(s_ve.h, a1v["ve_thr"])
            if kind == "s":
                s_ws[par].inc(GP.dma_start(
                    out=wshf[:, buf:buf + ww].rearrange("p (c w) -> p c w",
                                                        w=wdt),
                    in_=dsrc[:, 4 * idx:4 * idx + 4, :]), 16)
            else:
                s_ws[par].inc(GP.dma_start(
                    out=wshf[:, buf:buf + ww],
                    in_=dsrc[idx * 128:idx * 128 + 128, :]), 16)
            VE.wait_ge(s_ws[par].h, s_ws[par].n)
            if ig == 0:
                VE.wait_ge(s_ve.h, a1v["ve_thr"])
            VE.wait_ge(s_wspy.h, sp_done[ig - 1])
            VE.tensor_single_scalar(out=wsha[:, 0:ww],
                                    in_=wshf[:, buf:buf + ww],
                                    scalar=smc[:, 12 + wi:13 + wi], op=ALU.is_ge)
            VE.tensor_single_scalar(out=wshb[:, 0:ww],
                                    in_=wshf[:, buf:buf + ww],
                                    scalar=smc[:, 16 + wi:17 + wi], op=ALU.is_le)
            VE.drain()
            q_done[ig] = s_ve.inc(VE.tensor_tensor(out=wshq[:, 0:ww],
                                                   in0=wsha[:, 0:ww],
                                                   in1=wshb[:, 0:ww],
                                                   op=ALU.subtract))
            PE.wait_ge(s_ve.h, q_done[ig])
            ev = {}
            ntl, tw = (8, 128) if kind == "s" else (11, 64)
            for j in range(ntl):
                px = psX1 if j % 2 == 0 else psX2
                if j >= 2:
                    PE.wait_ge(s_ve.h, ev[j - 2])
                pet = s_pe.inc(PE.transpose(px[0:tw, 0:128],
                                            wshq[0:128, j * tw:(j + 1) * tw],
                                            id_sb[:]))
                VE.wait_ge(s_pe.h, pet)
                ev[j] = s_ve.inc(VE.tensor_copy(
                    out=wsht[0:tw, j * 128:(j + 1) * 128],
                    in_=px[0:tw, 0:128]))
            psx_ve = max(psx_ve, ev[ntl - 1])
            GP.wait_ge(s_ve.h, ev[ntl - 1])
            if kind == "s":
                wst3 = wsht[0:128, 0:1024].rearrange("p (c j t) -> p c j t",
                                                     j=2, t=128)
                for j in range(2):
                    s_wspy.inc(GP.dma_start(
                        out=dout[j * 128:(j + 1) * 128,
                                 idx * 512:(idx + 1) * 512]
                        .rearrange("p (c t) -> p c t", t=128),
                        in_=wst3[:, :, j, :]), 16)
            else:
                for j in range(ntl):
                    s_wspy.inc(GP.dma_start(
                        out=dout[j * tw:(j + 1) * tw,
                                 idx * 128:idx * 128 + 128],
                        in_=wsht[0:tw, j * 128:(j + 1) * 128]), 16)
            sp_done[ig] = s_wspy.n
            a1b_st["ig"] = ig + 1
            if lastch:
                _wt = ((w1ti, w1tg), (w2ti, w2tg), (w3ti, w3tg))[wi]
                GP.wait_ge(s_wspy.h, s_wspy.n)
                s_ccw[wi].inc(GP.collective_compute(
                    "AllGather", ALU.bypass, replica_groups=[list(range(8))],
                    ins=[_wt[0].ap().opt()], outs=[_wt[1].ap().opt()]))
        a1b_i[0] = min(i0 + n, len(a1b_units))


    for w in range(NW):
        for ts_ in range(4):
            ti = 4 * w + ts_
            p = ti % 2
            bx = p * D
            if ti >= 2:
                SY.wait_ge(s_ve.h, x_rd[ti - 2][0])
                SY.wait_ge(s_ac.h, x_rd[ti - 2][1])
            s_x[p].inc(SY.dma_start(out=x_t[:, bx:bx + D],
                                    in_=xs[ti * 128:ti * 128 + 128, :]), 16)
            xl = s_x[p].n
            VE.wait_ge(s_x[p].h, xl)
            VE.reduce_sum(out=smc[:, 20 + p:21 + p], in_=x_t[:, bx:bx + D],
                          axis=AX.X)
            SC.wait_ge(s_x[p].h, xl)
            ac_sq = s_ac.inc(SC.activation(out=bigH[:, 0:D], in_=x_t[:, bx:bx + D],
                                           func=AF.Square,
                                           accum_out=smc[:, 22 + p:23 + p]))
            VE.drain()
            VE.wait_ge(s_ac.h, ac_sq)
            VE.tensor_scalar_mul(out=smc[:, 24 + p:25 + p],
                                 in0=smc[:, 20 + p:21 + p], scalar1=INV_D)
            VE.tensor_scalar_mul(out=smc[:, 26 + p:27 + p],
                                 in0=smc[:, 22 + p:23 + p], scalar1=INV_D)
            VE.drain()
            VE.tensor_tensor(out=smc[:, 28 + p:29 + p], in0=smc[:, 24 + p:25 + p],
                             in1=smc[:, 24 + p:25 + p], op=ALU.mult)
            VE.drain()
            ve_var = s_ve.inc(VE.tensor_tensor(out=smc[:, 30 + p:31 + p],
                                               in0=smc[:, 26 + p:27 + p],
                                               in1=smc[:, 28 + p:29 + p],
                                               op=ALU.subtract))
            SC.wait_ge(s_ve.h, ve_var)
            ac_st = s_ac.inc(SC.activation(out=smc[:, 32 + p:33 + p],
                                           in_=smc[:, 30 + p:31 + p],
                                           func=AF.Sqrt, bias=smc[:, 63:64]))
            VE.wait_ge(s_ac.h, ac_st)
            VE.reciprocal(out=smc[:, 34 + p:35 + p], in_=smc[:, 32 + p:33 + p])
            VE.drain()
            VE.tensor_tensor(out=smc[:, 36 + p:37 + p], in0=smc[:, 24 + p:25 + p],
                             in1=smc[:, 34 + p:35 + p], op=ALU.mult)
            VE.drain()
            if ti >= 2:
                VE.wait_ge(s_pe.h, xn_ev[ti - 2][1])
            ve_xn = s_ve.inc(VE.tensor_scalar(out=xn_tm[:, bx:bx + D],
                                              in0=x_t[:, bx:bx + D],
                                              scalar1=smc[:, 34 + p:35 + p],
                                              scalar2=smc[:, 36 + p:37 + p],
                                              op0=ALU.mult, op1=ALU.subtract))
            x_rd[ti] = (ve_xn, ac_sq)
            PE.wait_ge(s_ve.h, ve_xn)
            evd = {}
            for dk in range(DK):
                px = psX1 if dk % 2 == 0 else psX2
                if dk >= 2:
                    PE.wait_ge(s_ve.h, evd[dk - 2])
                elif psx_ve:
                    PE.wait_ge(s_ve.h, psx_ve)
                pet = s_pe.inc(PE.transpose(px[:, 0:128],
                                            xn_tm[:, bx + dk * 128:bx + dk * 128 + 128],
                                            id_sb[:]))
                VE.wait_ge(s_pe.h, pet)
                evd[dk] = s_ve.inc(VE.tensor_copy(
                    out=xn_fm[:, dk * WIN + ts_ * 128:dk * WIN + ts_ * 128 + 128],
                    in_=px[:, 0:128]))
            psx_ve = evd[DK - 1]
            xn_ev[ti] = (evd[DK - 1], s_pe.n)
        xn_ok = xn_ev[4 * w + 3][0]

        prod_banks = [(psA, 0), (psA, 512), (psB, 0), (psB, 512)]
        si = 0
        for kind in ("k", "q"):
            wb = OKO if kind == "k" else 0
            for hh in range(HPC):
                bank, boff = prod_banks[si % 4]
                g = prod_guard[si % 4]
                if g is not None:
                    PE.wait_ge((s_ac if g[0] == "ac" else s_ve).h, g[1])
                PE.wait_ge(s_ve.h, xn_ok)
                if w == 0 and si == 0:
                    PE.wait_ge(s_w.h, W_QKV)
                last = None
                for dk in range(DK):
                    last = PE.matmul(bank[:, boff:boff + WIN],
                                     wqkv[:, wb + (hh * DK + dk) * 128:
                                          wb + (hh * DK + dk) * 128 + 128],
                                     xn_fm[:, dk * WIN:(dk + 1) * WIN],
                                     start=(dk == 0), stop=(dk == DK - 1))
                pe_prod = s_pe.inc(last)
                bias = bk_sb[:, hh:hh + 1] if kind == "k" else bq_sb[:, hh:hh + 1]
                par = si % 2
                stg = par * WIN
                SC.wait_ge(s_pe.h, pe_prod)
                if kind == "k":
                    SC.wait_ge(s_kvsp[par].h, kvst_sp[par])
                if kind == "k":
                    dst = kvst[:, stg:stg + WIN]
                else:
                    dst = q_sb[:, hh * WIN:(hh + 1) * WIN]
                ac_e = s_ac.inc(SC.activation(out=dst, in_=bank[:, boff:boff + WIN],
                                              func=AF.Identity, bias=bias))
                ac_sqr = s_ac.inc(SC.activation(out=ksq[:, :],
                                                in_=bank[:, boff:boff + WIN],
                                                func=AF.Square, bias=bias))
                prod_guard[si % 4] = ("ac", ac_sqr)
                PE.wait_ge(s_ac.h, ac_sqr)
                PE.wait_ge(s_ve.h, max(norm_psS_free, negc_psW_free))
                pe_n = s_pe.inc(PE.matmul(psS[0:1, :], onesf[:, 0:1], ksq[:, :],
                                          start=True, stop=True))
                VE.wait_ge(s_pe.h, pe_n)
                VE.reduce_max(out=smr[:, 20:21], in_=psS[0:1, :], axis=AX.X)
                VE.drain()
                if kind == "k":
                    cur = (w % 2) * HPC + hh
                    prv = ((w + 1) % 2) * HPC + hh
                    VE.tensor_tensor(out=kmax[:, cur:cur + 1],
                                     in0=kmax[:, prv:prv + 1],
                                     in1=smr[:, 20:21], op=ALU.max)
                else:
                    VE.tensor_copy(out=smr[:, 24 + hh:25 + hh], in_=smr[:, 20:21])
                VE.drain()
                norm_psS_free = s_ve.inc(VE.engine_nop())
                if kind == "k":
                    GP.wait_ge(s_ac.h, ac_e)
                    kvst_sp[par] = s_kvsp[par].inc(GP.dma_start(
                        out=k_d[hh * DH:hh * DH + DH, w * WIN:(w + 1) * WIN],
                        in_=kvst[:, stg:stg + WIN]), 16)
                si += 1
        for ts_ in range(4):
            bank, boff = prod_banks[si % 4]
            g = prod_guard[si % 4]
            if g is not None:
                PE.wait_ge((s_ac if g[0] == "ac" else s_ve).h, g[1])
            last = None
            for dk in range(DK):
                last = PE.matmul(bank[:, boff:boff + WIN],
                                 xn_fm[:, dk * WIN + ts_ * 128:dk * WIN + ts_ * 128 + 128],
                                 wqkv[:, OVO + dk * EPC:OVO + (dk + 1) * EPC],
                                 start=(dk == 0), stop=(dk == DK - 1))
            pe_v = s_pe.inc(last)
            par = si % 2
            stg = par * WIN
            VE.wait_ge(s_pe.h, pe_v)
            VE.wait_ge(s_kvsp[par].h, kvst_sp[par])
            VE.wait_ge(s_ve.h, ve_bvbc)
            ve_v = s_ve.inc(VE.scalar_tensor_tensor(
                out=kvst[:, stg:stg + WIN], in0=bank[:, boff:boff + WIN],
                scalar=1.0, in1=bvbc[:, :], op0=ALU.mult, op1=ALU.add))
            prod_guard[si % 4] = ("ve", ve_v)
            GP.wait_ge(s_ve.h, ve_v)
            kvst_sp[par] = s_kvsp[par].inc(GP.dma_start(
                out=v_d[w * WIN + ts_ * 128:w * WIN + (ts_ + 1) * 128, :],
                in_=kvst[:, stg:stg + WIN]), 16)
            si += 1
        kv_spilled[w] = (s_kvsp[0].n, s_kvsp[1].n)

        cur0 = (w % 2) * HPC
        VE.tensor_tensor(out=smr[:, 44:48], in0=smr[:, 24:28],
                         in1=kmax[:, cur0:cur0 + 4], op=ALU.mult)
        VE.drain()
        ve_c2 = s_ve.inc(VE.engine_nop())
        SC.wait_ge(s_ve.h, ve_c2)
        ac_c = s_ac.inc(SC.activation(out=smr[:, 48:52], in_=smr[:, 44:48],
                                      func=AF.Sqrt, scale=1.0 / 128.0))
        VE.wait_ge(s_ac.h, ac_c)
        VE.tensor_scalar_mul(out=smr[:, 52:56], in0=smr[:, 48:52],
                             scalar1=-1.0)
        VE.drain()
        ve_nc = s_ve.inc(VE.engine_nop())
        PE.wait_ge(s_ve.h, max(ve_nc, negc_psW_free))
        PE.wait_ge(s_ac.h, max(negc_slot_rd.get(cur0 + h_, 0)
                               for h_ in range(HPC)))
        pe_nc = s_pe.inc(PE.matmul(psW[:, 0:4], onesf[0:1, :], smr[:, 52:56],
                                   start=True, stop=True))
        VE.wait_ge(s_pe.h, pe_nc)
        negc_psW_free = s_ve.inc(VE.tensor_copy(
            out=smc[:, 40 + cur0:44 + cur0], in_=psW[:, 0:4]))
        norm_psS_free = max(norm_psS_free, negc_psW_free)
        ve_negc = negc_psW_free

        for hh in range(HPC):
            cur = (w % 2) * HPC + hh
            ncol = 40 + cur

            n_kt = 4 * (w + 1)
            n_g = n_kt // 2

            def load_kv(kt):
                slot = kt % 4
                for which in (0, 1):
                    g = kv_ring.get((which, slot), 0)
                    if g:
                        SY.wait_ge(s_pe.h, g)
                    SY.wait_ge(s_kvsp[0].h, kv_spilled[w][0])
                    SY.wait_ge(s_kvsp[1].h, kv_spilled[w][1])
                    if which == 0:
                        ii = SY.dma_start(
                            out=ktile[:, slot * 128:slot * 128 + 128],
                            in_=k_d[hh * DH:hh * DH + DH,
                                    kt * 128:kt * 128 + 128])
                        kv_ld[(0, kt)] = s_k[slot].inc(ii, 16)
                    else:
                        ii = SY.dma_start(
                            out=vtile[:, slot * 128:slot * 128 + 128],
                            in_=v_d[kt * 128:kt * 128 + 128,
                                    hh * DH:hh * DH + DH])
                        kv_ld[(1, kt)] = s_v[slot].inc(ii, 16)

            for kt in range(min(4, n_kt)):
                load_kv(kt)
            for gi in range(n_g):
                pp = gi % 2
                bank = psA if pp == 0 else psB
                ptb = pt_a if pp == 0 else pt_b
                kt0, kt1 = 2 * gi, 2 * gi + 1
                PE.wait_ge(s_k[kt0 % 4].h, kv_ld[(0, kt0)])
                PE.wait_ge(s_k[kt1 % 4].h, kv_ld[(0, kt1)])
                if exp_bank[pp]:
                    PE.wait_ge(s_ac.h, exp_bank[pp])
                elif gi < 2:
                    g = prod_guard[2 * pp]
                    PE.wait_ge((s_ac if g[0] == "ac" else s_ve).h, g[1])
                    g = prod_guard[2 * pp + 1]
                    PE.wait_ge((s_ac if g[0] == "ac" else s_ve).h, g[1])
                PE.matmul(bank[:, 0:WIN],
                          ktile[:, (kt0 % 4) * 128:(kt0 % 4) * 128 + 128],
                          q_sb[:, hh * WIN:(hh + 1) * WIN], start=True, stop=True)
                pe_sc = s_pe.inc(PE.matmul(
                    bank[:, WIN:2 * WIN],
                    ktile[:, (kt1 % 4) * 128:(kt1 % 4) * 128 + 128],
                    q_sb[:, hh * WIN:(hh + 1) * WIN], start=True, stop=True))
                kv_ring[(0, kt0 % 4)] = pe_sc
                kv_ring[(0, kt1 % 4)] = pe_sc
                SC.wait_ge(s_pe.h, pe_sc)
                SC.wait_ge(s_ve.h, ve_negc)
                if pt_cons[pp]:
                    SC.wait_ge(s_pe.h, pt_cons[pp])
                ac_e = s_ac.inc(SC.activation(out=ptb[:, :], in_=bank[:, :],
                                              func=AF.Exp, scale=ISQ,
                                              bias=smc[:, ncol:ncol + 1]))
                exp_bank[pp] = ac_e
                exp_done[(w, hh)] = ac_e
                negc_slot_rd[cur] = ac_e
                use = ptb
                if kt1 >= 4 * w:
                    VE.wait_ge(s_ac.h, ac_e)
                    if ptm_cons:
                        VE.wait_ge(s_pe.h, ptm_cons)
                    for which, kt in ((0, kt0), (1, kt1)):
                        if kt >= 4 * w:
                            o = (kt - 4 * w) * 128
                            VE.tensor_tensor(
                                out=pt_m[:, which * WIN:(which + 1) * WIN],
                                in0=ptb[:, which * WIN:(which + 1) * WIN],
                                in1=tril[:, 384 - o:896 - o], op=ALU.mult)
                        else:
                            VE.tensor_copy(
                                out=pt_m[:, which * WIN:(which + 1) * WIN],
                                in_=ptb[:, which * WIN:(which + 1) * WIN])
                    ve_m = s_ve.inc(VE.engine_nop())
                    use = pt_m
                    PE.wait_ge(s_ve.h, ve_m)
                else:
                    PE.wait_ge(s_ac.h, ac_e)
                first = (gi == 0)
                last_g = (gi == n_g - 1)
                PE.wait_ge(s_v[kt0 % 4].h, kv_ld[(1, kt0)])
                PE.wait_ge(s_v[kt1 % 4].h, kv_ld[(1, kt1)])
                PE.matmul(psS[0:1, :], onesb[:, 0:1], use[:, 0:WIN],
                          start=first, stop=False)
                PE.matmul(psS[0:1, :], onesb[:, 0:1], use[:, WIN:2 * WIN],
                          start=False, stop=last_g)
                PE.matmul(psZ[:], vtile[:, (kt0 % 4) * 128:(kt0 % 4) * 128 + 128],
                          use[:, 0:WIN], start=first, stop=False)
                pe_z = s_pe.inc(PE.matmul(
                    psZ[:], vtile[:, (kt1 % 4) * 128:(kt1 % 4) * 128 + 128],
                    use[:, WIN:2 * WIN], start=False, stop=last_g))
                kv_ring[(1, kt0 % 4)] = pe_z
                kv_ring[(1, kt1 % 4)] = pe_z
                pt_cons[pp] = pe_z
                if use is pt_m:
                    ptm_cons = pe_z
                for kt in (2 * gi + 4, 2 * gi + 5):
                    if kt < n_kt:
                        load_kv(kt)
            VE.wait_ge(s_pe.h, s_pe.n)
            VE.reciprocal(out=rs_sb[:, :], in_=psS[0:1, :])
            VE.drain()
            ve_rs = s_ve.inc(VE.engine_nop())
            norm_psS_free = ve_rs
            PE.wait_ge(s_ve.h, max(ve_rs, negc_psW_free))
            pe_rb = s_pe.inc(PE.matmul(psW[:, :], onesf[0:1, :], rs_sb[:, :],
                                       start=True, stop=True))
            zpar = hh % 2
            zb = zpar * WIN
            VE.wait_ge(s_pe.h, pe_rb)
            VE.wait_ge(s_zsp[zpar].h, zsp_par[zpar])
            VE.tensor_copy(out=rb_f[:, :], in_=psW[:, 0:WIN])
            VE.drain()
            ve_z = s_ve.inc(VE.tensor_tensor(out=zst[:, zb:zb + WIN], in0=psZ[:],
                                             in1=rb_f[:, :], op=ALU.mult))
            negc_psW_free = ve_z
            GP.wait_ge(s_ve.h, ve_z)
            zsp_par[zpar] = s_zsp[zpar].inc(GP.dma_start(
                out=zin[w, hh * DH:hh * DH + DH, :],
                in_=zst[:, zb:zb + WIN]), 16)
            PE.wait_ge(s_ve.h, ve_z)
        GP.wait_ge(s_zsp[0].h, zsp_par[0])
        GP.wait_ge(s_zsp[1].h, zsp_par[1])
        s_ccz.inc(GP.collective_compute(
            "AllGather", ALU.bypass, replica_groups=[[0, 1, 2, 3], [4, 5, 6, 7]],
            ins=[zin[w, :, :].opt()], outs=[zg[w, :, :].opt()]))

        if w < 2:
            emit_a1a_batch(19)
        if w == 1:
            emit_a1a_tail_a()
        if w == 2:
            emit_a1a_tail_b()
        if w >= 2:
            emit_a1b_batch(7)

    A2_PE, A2_VE, A2_AC = s_pe.n, s_ve.n, s_ac.n
    emit_a1b_batch(len(a1b_units))
    assert a1b_i[0] == len(a1b_units) and a1a_i[0] == len(a1a_units)
    A1_PE, A1_VE, A1_AC = s_pe.n, s_ve.n, s_ac.n
    A1_SPILLS = s_wspy.n


    if PHASE_LIMIT < 3:
        GP.wait_ge(s_ccz.h, s_ccz.n)
        return nc
    ab.reset(0)
    wo_sb = ab.alloc(H * D)
    ztile = ab.alloc(2 * 2048)
    xq_t = ab.alloc(D)
    xq_fm = ab.alloc(DK * TPC)
    ab_b1 = ab.mark()
    af.reset(0)
    x_own = af.alloc(2 * D)
    x2_t = af.alloc(D)
    silu_s = af.alloc(2 * WIN)
    h_t = af.alloc(2 * WIN)
    r_bcf = af.alloc(D)
    af_b1 = af.mark()
    u16 = bigH[:, 0:D]

    GP.wait_ge(s_pe.h, A1_PE)
    GP.wait_ge(s_ve.h, A1_VE)
    GP.wait_ge(s_wspy.h, A1_SPILLS)
    SY.wait_ge(s_ve.h, A1_VE)
    wo_ld = {}
    for hx in range(H):
        wo_ld[hx] = s_w.inc(GP.dma_start(out=wo_sb[:, hx * D:(hx + 1) * D],
                                         in_=wod[hx, :, :]), 16)
    W_WO = s_w.n

    PE.wait_ge(s_ve.h, A1_VE)
    PE.wait_ge(s_ac.h, A1_AC)
    rb_pe = 0
    for dsd in range(4):
        if rb_pe:
            SY.wait_ge(s_pe.h, rb_pe)
        s_ld.inc(SY.dma_start(out=bo_row[:, :],
                              in_=bod[0:1, dsd * WIN:(dsd + 1) * WIN]), 16)
        BOL = s_ld.n
        PE.wait_ge(s_ld.h, BOL)
        if dsd > 0:
            PE.wait_ge(s_ve.h, rb_ve)
        rb_pe = s_pe.inc(PE.matmul(psW[:, :], onesf[0:1, :], bo_row[:, :],
                                   start=True, stop=True))
        VE.wait_ge(s_pe.h, rb_pe)
        rb_ve = s_ve.inc(VE.tensor_copy(out=r_bcf[:, dsd * WIN:(dsd + 1) * WIN],
                                        in_=psW[:, :]))
    R_DONE = s_ve.n

    SY.wait_ge(s_ccz.h, s_ccz.n)
    q2 = SY.compute_val(SY.partition_id() % 4 * 2)

    zgv = zg.rearrange("w (h e) (t c) -> w t e h c", h=H, c=128)
    xs3 = xs.rearrange("(u t c) d -> u t c d", t=4, c=128)

    zring = {-2: 0, -1: 0}
    xo_rd = {-2: 0, -1: 0}
    x2sp_par = {0: 0, 1: 0}
    opbank_ve = 0
    psxb_ve = 0
    xq_ready = {}
    for tc in range(8):
        zpar = tc % 2
        zb = zpar * 2048
        SY.wait_ge(s_pe.h, zring[tc - 2])
        s_zl[zpar].inc(SY.dma_start(
            out=ztile[:, zb:zb + 2048].rearrange("e (h c) -> e h c", h=H),
            in_=zgv[q2 + tc // 4, tc % 4, :, :, :]), 16)
        ZL = s_zl[zpar].n
        PE.wait_ge(s_zl[zpar].h, ZL)
        if tc == 0:
            PE.wait_ge(s_ve.h, R_DONE)
        if opbank_ve:
            PE.wait_ge(s_ve.h, opbank_ve)
        last = None
        for hx in range(H):
            if tc == 0:
                PE.wait_ge(s_w.h, wo_ld[hx])
            for dsd in range(4):
                bb = psA if dsd < 2 else psB
                bo_ = (dsd % 2) * 512
                last = PE.matmul(bb[:, bo_:bo_ + WIN],
                                 ztile[:, zb + hx * 128:zb + hx * 128 + 128],
                                 wo_sb[:, hx * D + dsd * WIN:hx * D + (dsd + 1) * WIN],
                                 start=(hx == 0), stop=(hx == H - 1))
        pe_op = s_pe.inc(last)
        zring[tc] = pe_op
        OP_PE = s_pe.n
        p = tc % 2
        bx = p * D
        SY.wait_ge(s_ve.h, xo_rd[tc - 2])
        SY.wait_ge(s_x2sp[p].h, x2sp_par[p])
        s_x[p].inc(SY.dma_start(out=x_own[:, bx:bx + D],
                                in_=xs3[q2 + tc // 4, tc % 4, :, :]), 16)
        XO = s_x[p].n
        VE.wait_ge(s_x[p].h, XO)
        VE.wait_ge(s_pe.h, OP_PE)
        for dsd in range(4):
            bb = psA if dsd < 2 else psB
            bo_ = (dsd % 2) * 512
            VE.scalar_tensor_tensor(
                out=x2_t[:, dsd * WIN:(dsd + 1) * WIN], in0=bb[:, bo_:bo_ + WIN],
                scalar=1.0, in1=x_own[:, bx + dsd * WIN:bx + (dsd + 1) * WIN],
                op0=ALU.mult, op1=ALU.add)
        VE.drain()
        opbank_ve = s_ve.inc(VE.engine_nop())
        VE.wait_ge(s_ve.h, max(opbank_ve, R_DONE))
        ve_x2 = s_ve.inc(VE.tensor_tensor(out=x_own[:, bx:bx + D], in0=x2_t[:, :],
                                          in1=r_bcf[:, :], op=ALU.add))
        VE.drain()
        GP.wait_ge(s_ve.h, ve_x2)
        x2sp_par[p] = s_x2sp[p].inc(GP.dma_start(
            out=x2_d[tc * 128:tc * 128 + 128, :], in_=x_own[:, bx:bx + D]), 16)
        VE.reduce_sum(out=smc[:, 20 + p:21 + p], in_=x_own[:, bx:bx + D], axis=AX.X)
        VE.reduce_max(out=smc[:, 48 + p:49 + p], in_=x_own[:, bx:bx + D], axis=AX.X)
        VE.tensor_reduce(out=smc[:, 50 + p:51 + p], in_=x_own[:, bx:bx + D],
                         axis=AX.X, op=ALU.min)
        SC.wait_ge(s_ve.h, ve_x2)
        ac_sq = s_ac.inc(SC.activation(out=bigH[:, D:2 * D], in_=x_own[:, bx:bx + D],
                                       func=AF.Square,
                                       accum_out=smc[:, 22 + p:23 + p]))
        VE.drain()
        VE.wait_ge(s_ac.h, ac_sq)
        VE.tensor_scalar_mul(out=smc[:, 24 + p:25 + p], in0=smc[:, 20 + p:21 + p],
                             scalar1=INV_D)
        VE.tensor_scalar_mul(out=smc[:, 26 + p:27 + p], in0=smc[:, 22 + p:23 + p],
                             scalar1=INV_D)
        VE.drain()
        VE.tensor_tensor(out=smc[:, 28 + p:29 + p], in0=smc[:, 24 + p:25 + p],
                         in1=smc[:, 24 + p:25 + p], op=ALU.mult)
        VE.drain()
        ve_var = s_ve.inc(VE.tensor_tensor(out=smc[:, 30 + p:31 + p],
                                           in0=smc[:, 26 + p:27 + p],
                                           in1=smc[:, 28 + p:29 + p],
                                           op=ALU.subtract))
        SC.wait_ge(s_ve.h, ve_var)
        ac_st = s_ac.inc(SC.activation(out=smc[:, 32 + p:33 + p],
                                       in_=smc[:, 30 + p:31 + p],
                                       func=AF.Sqrt, bias=smc[:, 63:64]))
        VE.wait_ge(s_ac.h, ac_st)
        VE.reciprocal(out=smc[:, 34 + p:35 + p], in_=smc[:, 32 + p:33 + p])
        VE.drain()
        VE.tensor_tensor(out=smc[:, 36 + p:37 + p], in0=smc[:, 48 + p:49 + p],
                         in1=smc[:, 24 + p:25 + p], op=ALU.subtract)
        VE.tensor_tensor(out=smc[:, 38 + p:39 + p], in0=smc[:, 24 + p:25 + p],
                         in1=smc[:, 50 + p:51 + p], op=ALU.subtract)
        VE.drain()
        VE.tensor_tensor(out=smc[:, 40 + p:41 + p], in0=smc[:, 36 + p:37 + p],
                         in1=smc[:, 38 + p:39 + p], op=ALU.max)
        VE.drain()
        VE.tensor_tensor(out=smc[:, 42 + p:43 + p], in0=smc[:, 40 + p:41 + p],
                         in1=smc[:, 34 + p:35 + p], op=ALU.mult)
        VE.drain()
        VE.tensor_scalar_max(out=smc[:, 44 + p:45 + p], in0=smc[:, 42 + p:43 + p],
                             scalar1=1e-5)
        VE.drain()
        VE.reciprocal(out=smc[:, 46 + p:47 + p], in_=smc[:, 44 + p:45 + p])
        VE.tensor_scalar_mul(out=fac_x[:, tc:tc + 1], in0=smc[:, 44 + p:45 + p],
                             scalar1=1.0 / 127.0)
        VE.drain()
        VE.tensor_scalar_mul(out=smc[:, 52 + p:53 + p], in0=smc[:, 46 + p:47 + p],
                             scalar1=127.0)
        VE.drain()
        VE.tensor_tensor(out=smc[:, 54 + p:55 + p], in0=smc[:, 34 + p:35 + p],
                         in1=smc[:, 52 + p:53 + p], op=ALU.mult)
        VE.drain()
        VE.tensor_tensor(out=smc[:, 56 + p:57 + p], in0=smc[:, 24 + p:25 + p],
                         in1=smc[:, 54 + p:55 + p], op=ALU.mult)
        VE.drain()
        VE.tensor_scalar(out=smc[:, 58 + p:59 + p], in0=smc[:, 56 + p:57 + p],
                         scalar1=-1.0, scalar2=RM, op0=ALU.mult, op1=ALU.add)
        VE.drain()
        VE.tensor_scalar(out=u16[:, :], in0=x_own[:, bx:bx + D],
                         scalar1=smc[:, 54 + p:55 + p],
                         scalar2=smc[:, 58 + p:59 + p],
                         op0=ALU.mult, op1=ALU.add)
        VE.drain()
        ve_xq = s_ve.inc(VE.tensor_single_scalar(out=xq_t[:, :], in_=u16[:, :],
                                                 scalar=-RM, op=ALU.add))
        xo_rd[tc] = ve_xq
        PE.wait_ge(s_ve.h, ve_xq)
        evd = {}
        for dk in range(DK):
            px = psX1 if dk % 2 == 0 else psX2
            if dk >= 2:
                PE.wait_ge(s_ve.h, evd[dk - 2])
            elif psxb_ve:
                PE.wait_ge(s_ve.h, psxb_ve)
            elif psx_ve:
                PE.wait_ge(s_ve.h, psx_ve)
            pet = s_pe.inc(PE.transpose(px[:, 0:128],
                                        xq_t[:, dk * 128:dk * 128 + 128],
                                        id_sb[:]))
            VE.wait_ge(s_pe.h, pet)
            evd[dk] = s_ve.inc(VE.tensor_copy(
                out=xq_fm[:, dk * TPC + tc * 128:dk * TPC + tc * 128 + 128],
                in_=px[:, 0:128]))
        psxb_ve = evd[DK - 1]
        xq_ready[tc] = evd[DK - 1]

    VE.tensor_single_scalar(out=sa1[:, 0:8], in_=fac_x[:, 0:8],
                            scalar=smc[:, 8:9], op=ALU.mult)
    VE.tensor_single_scalar(out=sb2[:, 0:8], in_=fac_x[:, 0:8],
                            scalar=smc[:, 9:10], op=ALU.mult)
    VE.drain()
    SC_READY = s_ve.inc(VE.engine_nop())
    B1_PE = s_pe.n

    if PHASE_LIMIT < 4:
        VE.wait_ge(s_ve.h, s_ve.n)
        return nc
    w12r = bigB[:, 0:2 * 2 * DK * WIN]
    wr_done = {}
    wr_used = {-2: 0, -1: 0}
    hb_guard = {0: None, 1: None}
    silu_g = {0: 0, 1: 0}
    hsp_par = {0: 0, 1: 0}
    for hs in range(NHS):
        wpar = hs % 2
        wb = wpar * (2 * DK * WIN)
        SY.wait_ge(s_pe.h, max(wr_used[hs - 2], B1_PE))
        if hs == 0:
            SY.wait_ge(s_ccw[0].h, 1)
            SY.wait_ge(s_ccw[1].h, 1)
        for dk in range(DK):
            s_wr[wpar].inc(SY.dma_start(
                out=w12r[:, wb + dk * WIN:wb + (dk + 1) * WIN],
                in_=w1tg.ap()[dk * 128:dk * 128 + 128, hs * WIN:(hs + 1) * WIN]), 16)
            s_wr[wpar].inc(SY.dma_start(
                out=w12r[:, wb + DK * WIN + dk * WIN:wb + DK * WIN + (dk + 1) * WIN],
                in_=w2tg.ap()[dk * 128:dk * 128 + 128, hs * WIN:(hs + 1) * WIN]), 16)
        wr_done[hs] = s_wr[wpar].n
        for tc in range(8):
            pp = tc % 2
            bankA = psA if pp == 0 else psB
            PE.wait_ge(s_wr[wpar].h, wr_done[hs])
            if hs == 0:
                PE.wait_ge(s_ve.h, xq_ready[tc])
                if tc < 2:
                    PE.wait_ge(s_ve.h, opbank_ve)
            g = hb_guard[pp]
            if g is not None:
                PE.wait_ge(s_ve.h, g[0])
                PE.wait_ge(s_ac.h, g[1])
            last = None
            for dk in range(DK):
                st, en = (dk == 0), (dk == DK - 1)
                PE.matmul(bankA[:, 0:WIN],
                          xq_fm[:, dk * TPC + tc * 128:dk * TPC + tc * 128 + 128],
                          w12r[:, wb + dk * WIN:wb + (dk + 1) * WIN],
                          start=st, stop=en)
                last = PE.matmul(bankA[:, WIN:2 * WIN],
                                 xq_fm[:, dk * TPC + tc * 128:dk * TPC + tc * 128 + 128],
                                 w12r[:, wb + DK * WIN + dk * WIN:
                                      wb + DK * WIN + (dk + 1) * WIN],
                                 start=st, stop=en)
            pe_mm = s_pe.inc(last)
            wr_used[hs] = pe_mm
            sst = pp * WIN
            SC.wait_ge(s_pe.h, pe_mm)
            if hs == 0 and tc == 0:
                SC.wait_ge(s_ve.h, SC_READY)
            if silu_g[pp]:
                SC.wait_ge(s_ve.h, silu_g[pp])
            VE.wait_ge(s_hsp[pp].h, hsp_par[pp])
            ac_si = s_ac.inc(SC.activation(out=silu_s[:, sst:sst + WIN],
                                           in_=bankA[:, 0:WIN], func=AF.Silu,
                                           scale=sa1[:, tc:tc + 1]))
            VE.wait_ge(s_ac.h, ac_si)
            ve_h = s_ve.inc(VE.scalar_tensor_tensor(
                out=h_t[:, sst:sst + WIN], in0=bankA[:, WIN:2 * WIN],
                scalar=sb2[:, tc:tc + 1], in1=silu_s[:, sst:sst + WIN],
                op0=ALU.mult, op1=ALU.mult))
            silu_g[pp] = ve_h
            VE.drain()
            ve_hm = s_ve.inc(VE.reduce_max(
                out=hm_c[:, tc * 12 + hs:tc * 12 + hs + 1],
                in_=h_t[:, sst:sst + WIN], axis=AX.X, apply_absolute_value=True))
            hb_guard[pp] = (ve_h, ac_si)
            GP.wait_ge(s_ve.h, ve_hm)
            hsp_par[pp] = s_hsp[pp].inc(GP.dma_start(
                out=h_d[tc * 128:tc * 128 + 128, hs * WIN:(hs + 1) * WIN],
                in_=h_t[:, sst:sst + WIN]), 16)
    B3_PE = s_pe.n
    VE.drain()
    for tc in range(8):
        p = tc % 2
        VE.reduce_max(out=smc[:, 20 + p:21 + p],
                      in_=hm_c[:, tc * 12:tc * 12 + 11], axis=AX.X)
        VE.drain()
        VE.tensor_scalar_max(out=smc[:, 22 + p:23 + p],
                             in0=smc[:, 20 + p:21 + p], scalar1=1e-5)
        VE.drain()
        VE.reciprocal(out=smc[:, 24 + p:25 + p], in_=smc[:, 22 + p:23 + p])
        VE.tensor_scalar_mul(out=smc[:, 26 + p:27 + p], in0=smc[:, 22 + p:23 + p],
                             scalar1=1.0 / 127.0)
        VE.drain()
        VE.tensor_scalar_mul(out=hmax_sc[:, tc:tc + 1], in0=smc[:, 24 + p:25 + p],
                             scalar1=127.0)
        VE.tensor_single_scalar(out=s3s[:, tc:tc + 1], in_=smc[:, 26 + p:27 + p],
                                scalar=smc[:, 10:11], op=ALU.mult)
        VE.drain()
    HS_READY = s_ve.inc(VE.engine_nop())
    B3_VE = s_ve.n

    if PHASE_LIMIT < 5:
        GP.wait_ge(s_hsp[0].h, s_hsp[0].n)
        GP.wait_ge(s_hsp[1].h, s_hsp[1].n)
        return nc
    ab.reset(0)
    hq_fm = ab.alloc(HKN * TPC)
    hq_t = ab.alloc(HID // 2)
    w3r = ab.alloc(HKN * WIN)
    af.reset(0)
    h_ld = af.alloc(HID)
    x2_ld = af.alloc(2 * WIN)
    out_st = af.alloc(2 * WIN)
    u16h = bigH[:, 0:HID]

    psxc_ve = psxb_ve
    SY.wait_ge(s_hsp[0].h, s_hsp[0].n)
    SY.wait_ge(s_hsp[1].h, s_hsp[1].n)
    SY.wait_ge(s_x2sp[0].h, s_x2sp[0].n)
    SY.wait_ge(s_x2sp[1].h, s_x2sp[1].n)
    HHALF = HID // 2
    QH = HHALF // 2
    NTQ = QH // 128
    hl_rd = {-2: 0, -1: 0}
    hl_ld = {}
    qt_pe = {-2: 0, -1: 0}
    pxg = {0: psxc_ve, 1: psxc_ve}

    def issue_hl(idx):
        if idx >= 16:
            return
        tc_, hf_ = idx // 2, idx % 2
        sl = idx % 2
        SY.wait_ge(s_ve.h, hl_rd[idx - 2])
        hl_ld[idx] = s_hl[sl].inc(SY.dma_start(
            out=h_ld[:, sl * HHALF:(sl + 1) * HHALF],
            in_=h_d[tc_ * 128:tc_ * 128 + 128,
                    hf_ * HHALF:(hf_ + 1) * HHALF]), 16)

    issue_hl(0)
    issue_hl(1)
    for idx in range(16):
        tc, hf = idx // 2, idx % 2
        sl = idx % 2
        VE.wait_ge(s_hl[sl].h, hl_ld[idx])
        if idx == 0:
            VE.wait_ge(s_ve.h, HS_READY)
        for q in range(2):
            gq = idx * 2 + q
            qsl = gq % 2
            VE.wait_ge(s_pe.h, qt_pe[gq - 2])
            VE.tensor_scalar(out=u16h[:, qsl * QH:(qsl + 1) * QH],
                             in0=h_ld[:, sl * HHALF + q * QH:
                                      sl * HHALF + (q + 1) * QH],
                             scalar1=hmax_sc[:, tc:tc + 1], scalar2=RM,
                             op0=ALU.mult, op1=ALU.add)
            VE.drain()
            ve_hq = s_ve.inc(VE.tensor_single_scalar(
                out=hq_t[:, qsl * QH:(qsl + 1) * QH],
                in_=u16h[:, qsl * QH:(qsl + 1) * QH],
                scalar=-RM, op=ALU.add))
            if q == 1:
                hl_rd[idx] = ve_hq
            PE.wait_ge(s_ve.h, ve_hq)
            if gq == 0:
                PE.wait_ge(s_pe.h, B3_PE)
            for hj in range(NTQ):
                hk = hf * (HKN // 2) + q * NTQ + hj
                b = hj % 2
                PE.wait_ge(s_ve.h, pxg[b])
                pet = s_pe.inc(PE.transpose(
                    (psX1 if b == 0 else psX2)[:, 0:128],
                    hq_t[:, qsl * QH + hj * 128:qsl * QH + hj * 128 + 128],
                    id_sb[:]))
                VE.wait_ge(s_pe.h, pet)
                pxg[b] = s_ve.inc(VE.tensor_copy(
                    out=hq_fm[:, hk * TPC + tc * 128:hk * TPC + tc * 128 + 128],
                    in_=(psX1 if b == 0 else psX2)[:, 0:128]))
            qt_pe[gq] = s_pe.n
        issue_hl(idx + 2)
    HQ_ALL = s_ve.n

    if PHASE_LIMIT < 6:
        VE.wait_ge(s_ve.h, s_ve.n)
        return nc
    w3_used = {-1: 0}
    b5_g = {0: 0, 1: 0}
    out_g = {0: 0, 1: 0}
    x2l = {-2: 0, -1: 0}
    it = 0
    for dq in range(4):
        SY.wait_ge(s_pe.h, max(w3_used[dq - 1], B3_PE))
        if dq == 0:
            SY.wait_ge(s_ve.h, B3_VE)
        if dq == 0:
            SY.wait_ge(s_ccw[2].h, 1)
        W3L = 0
        for hk in range(HKN):
            W3L = s_w3r.inc(SY.dma_start(
                out=w3r[:, hk * WIN:(hk + 1) * WIN],
                in_=w3tg.ap()[hk * 128:hk * 128 + 128, dq * WIN:(dq + 1) * WIN]),
                16)
        for tc in range(8):
            pp = tc % 2
            bank = psZ if pp == 0 else psS
            PE.wait_ge(s_w3r.h, W3L)
            if dq == 0 and tc == 0:
                PE.wait_ge(s_ve.h, HQ_ALL)
            if b5_g[pp]:
                PE.wait_ge(s_ve.h, b5_g[pp])
            last = None
            for hk in range(HKN):
                last = PE.matmul(bank[:],
                                 hq_fm[:, hk * TPC + tc * 128:hk * TPC + tc * 128 + 128],
                                 w3r[:, hk * WIN:(hk + 1) * WIN],
                                 start=(hk == 0), stop=(hk == HKN - 1))
            pe_m3 = s_pe.inc(last)
            w3_used[dq] = pe_m3
            xb = pp * WIN
            SY.wait_ge(s_ve.h, x2l[it - 2])
            if dq == 0 and tc == 0:
                SY.wait_ge(s_x2sp[0].h, s_x2sp[0].n)
                SY.wait_ge(s_x2sp[1].h, s_x2sp[1].n)
            X2L = s_x2l[pp].inc(SY.dma_start(
                out=x2_ld[:, xb:xb + WIN],
                in_=x2_d[tc * 128:tc * 128 + 128,
                         dq * WIN:(dq + 1) * WIN]), 16)
            VE.wait_ge(s_x2l[pp].h, X2L)
            VE.wait_ge(s_pe.h, pe_m3)
            if out_g[pp]:
                VE.wait_ge(s_ou[pp].h, out_g[pp])
            ve_o = s_ve.inc(VE.scalar_tensor_tensor(
                out=out_st[:, xb:xb + WIN], in0=bank[:], scalar=s3s[:, tc:tc + 1],
                in1=x2_ld[:, xb:xb + WIN], op0=ALU.mult, op1=ALU.add))
            b5_g[pp] = ve_o
            x2l[it] = ve_o
            SY.wait_ge(s_ve.h, ve_o)
            out_g[pp] = s_ou[pp].inc(SY.dma_start(
                out=out_d[tc * 128:tc * 128 + 128, dq * WIN:(dq + 1) * WIN],
                in_=out_st[:, xb:xb + WIN]), 16)
            it += 1
    for pp in (0, 1):
        SY.wait_ge(s_ou[pp].h, s_ou[pp].n)
        GP.wait_ge(s_ou[pp].h, s_ou[pp].n)
        VE.wait_ge(s_ou[pp].h, s_ou[pp].n)

    return nc


_NC_CACHE = {}


def kernel(L0_attn_x, L0_attn_W_Q, L0_attn_W_K, L0_attn_W_V, L0_attn_W_O,
           L0_attn_b_Q, L0_attn_b_K, L0_attn_b_V, L0_attn_b_O,
           L0_ffn_w1, L0_ffn_w2, L0_ffn_w3):
    x = np.ascontiguousarray(np.asarray(L0_attn_x, dtype=np.float32))
    WQ = np.asarray(L0_attn_W_Q, dtype=np.float32)
    WK = np.asarray(L0_attn_W_K, dtype=np.float32)
    WV = np.asarray(L0_attn_W_V, dtype=np.float32)
    WO = np.ascontiguousarray(np.asarray(L0_attn_W_O, dtype=np.float32))
    bQ = np.asarray(L0_attn_b_Q, dtype=np.float32)
    bK = np.asarray(L0_attn_b_K, dtype=np.float32)
    bV = np.asarray(L0_attn_b_V, dtype=np.float32)
    bO = np.asarray(L0_attn_b_O, dtype=np.float32).reshape(1, D)
    w1 = np.asarray(L0_ffn_w1, dtype=np.float32)
    w2 = np.asarray(L0_ffn_w2, dtype=np.float32)
    w3 = np.asarray(L0_ffn_w3, dtype=np.float32)

    import ml_dtypes
    ident = np.eye(128, dtype=ml_dtypes.bfloat16)
    onesb = np.ones((128, 2), dtype=ml_dtypes.bfloat16)
    onesf = np.ones((128, 128), dtype=np.float32)
    uu = np.arange(896)[None, :]
    ii = np.arange(128)[:, None]
    tril = (uu >= ii + 384).astype(ml_dtypes.bfloat16)

    bqT = np.ascontiguousarray(bQ.T)
    bkT = np.ascontiguousarray(bK.T)
    bvT = np.ascontiguousarray(bV.T)

    if "nc" not in _NC_CACHE:
        _NC_CACHE["nc"] = build_program()
    nc = _NC_CACHE["nc"]

    in_maps = []
    for c in range(8):
        s = c // 4
        g = c % 4
        hs = slice(4 * g, 4 * g + 4)
        in_maps.append({
            "xs": np.ascontiguousarray(x[s]),
            "wq": np.ascontiguousarray(WQ[hs]),
            "wk": np.ascontiguousarray(WK[hs]),
            "wv": np.ascontiguousarray(WV[hs]),
            "wo": WO,
            "bqT": np.ascontiguousarray(bqT[:, hs]),
            "bkT": np.ascontiguousarray(bkT[:, hs]),
            "bvTf": bvT,
            "bvfl": np.ascontiguousarray(bV[hs].reshape(1, EPC)),
            "bo": bO,
            "w1s": np.ascontiguousarray(w1[:, c * W12_SH:(c + 1) * W12_SH]),
            "w2s": np.ascontiguousarray(w2[:, c * W12_SH:(c + 1) * W12_SH]),
            "w3s": np.ascontiguousarray(w3[:, c * W3_SH:(c + 1) * W3_SH]),
            "ident": ident,
            "onesb": onesb,
            "onesf": onesf,
            "tril": tril,
        })
    res = run_bass_kernel_spmd(nc, in_maps, core_ids=list(range(8)))
    out = np.empty((2, S, D), dtype=np.float32)
    for c in range(8):
        s, g = c // 4, c % 4
        out[s, g * TPC:(g + 1) * TPC, :] = res.results[c]["out"]
    return out


# version 14
# speedup vs baseline: 1.0358x; 1.0024x over previous
import math

import numpy as np
import concourse.bass as bass
import concourse.mybir as mybir
from concourse.bass_utils import run_bass_kernel_spmd

F32 = mybir.dt.float32
BF16 = mybir.dt.bfloat16
FP16 = mybir.dt.float16
AF = mybir.ActivationFunctionType
ALU = mybir.AluOpType
AX = mybir.AxisListType

S, D, H, DH = 4096, 2048, 16, 128
HID = 5632
HPC, EPC, TPC, WIN = 4, 512, 1024, 512
NW = S // WIN
DK = D // 128
HKN = HID // 128
NHS = HID // WIN
W12_SH = D // 8
W3_SH = HID // 8
ISQ = 1.0 / math.sqrt(DH)
RM = 1536.0
NWELEM = float(HID * D)
INV_D = 1.0 / D


class Sem:
    def __init__(self, nc, name):
        self.h = nc.semaphore(name).__enter__()
        self.n = 0

    def inc(self, instr, by=1):
        instr.then_inc(self.h, by)
        self.n += by
        return self.n


class Arena:
    def __init__(self, t, cols):
        self.t, self.cols, self.cur = t, cols, 0

    def alloc(self, n):
        assert self.cur + n <= self.cols, ("arena", self.cur, n, self.cols)
        off = self.cur
        self.cur += n
        return self.t[:, off:off + n]

    def reset(self, m=0):
        self.cur = m

    def mark(self):
        return self.cur


DEBUG_INTERNALS = False
PHASE_LIMIT = 99
A1B_CHUNKS = None
A1B_ONLY_W3 = False


def build_program():
    nc = bass.Bass("TRN2", target_bir_lowering=False, debug=False)
    dbg = "ExternalOutput" if DEBUG_INTERNALS else "Internal"

    xs = nc.dram_tensor("xs", [S, D], F32, kind="ExternalInput").ap()
    wqd = nc.dram_tensor("wq", [HPC, D, DH], F32, kind="ExternalInput").ap()
    wkd = nc.dram_tensor("wk", [HPC, D, DH], F32, kind="ExternalInput").ap()
    wvd = nc.dram_tensor("wv", [HPC, D, DH], F32, kind="ExternalInput").ap()
    wod = nc.dram_tensor("wo", [H, DH, D], F32, kind="ExternalInput").ap()
    bqT = nc.dram_tensor("bqT", [DH, HPC], F32, kind="ExternalInput").ap()
    bkT = nc.dram_tensor("bkT", [DH, HPC], F32, kind="ExternalInput").ap()
    bvTf = nc.dram_tensor("bvTf", [DH, H], F32, kind="ExternalInput").ap()
    bvfl = nc.dram_tensor("bvfl", [1, EPC], F32, kind="ExternalInput").ap()
    bod = nc.dram_tensor("bo", [1, D], F32, kind="ExternalInput").ap()
    w1sd = nc.dram_tensor("w1s", [HID, W12_SH], F32, kind="ExternalInput").ap()
    w2sd = nc.dram_tensor("w2s", [HID, W12_SH], F32, kind="ExternalInput").ap()
    w3sd = nc.dram_tensor("w3s", [D, W3_SH], F32, kind="ExternalInput").ap()
    identd = nc.dram_tensor("ident", [128, 128], BF16, kind="ExternalInput").ap()
    onesbd = nc.dram_tensor("onesb", [128, 2], BF16, kind="ExternalInput").ap()
    onesfd = nc.dram_tensor("onesf", [128, 128], F32, kind="ExternalInput").ap()
    trild = nc.dram_tensor("tril", [128, 896], BF16, kind="ExternalInput").ap()
    out_d = nc.dram_tensor("out", [TPC, D], F32, kind="ExternalOutput").ap()

    k_d = nc.dram_tensor("k_d", [EPC, S], BF16, kind=dbg).ap()
    v_d = nc.dram_tensor("v_d", [S, EPC], BF16, kind=dbg).ap()
    zin_t = nc.dram_tensor("zin", [NW, EPC, WIN], BF16, kind=dbg)
    zg_t = nc.dram_tensor("zg", [NW, D, WIN], BF16)
    zin, zg = zin_t.ap(), zg_t.ap()
    mp_in = nc.dram_tensor("mp_in", [1, 8], F32)
    mp_g = nc.dram_tensor("mp_g", [8, 8], F32)
    w1ti = nc.dram_tensor("w1t_in", [W12_SH, HID], BF16, kind=dbg)
    w2ti = nc.dram_tensor("w2t_in", [W12_SH, HID], BF16)
    w3ti = nc.dram_tensor("w3t_in", [W3_SH, D], BF16)
    w1tg = nc.dram_tensor("w1t_g", [D, HID], BF16)
    w2tg = nc.dram_tensor("w2t_g", [D, HID], BF16)
    w3tg = nc.dram_tensor("w3t_g", [HID, D], BF16)
    x2_d = nc.dram_tensor("x2_d", [TPC, D], F32, kind=dbg).ap()
    h_d = nc.dram_tensor("h_d", [TPC, HID], F32, kind=dbg).ap()

    def sbt(name, shape, dt):
        return nc.sbuf_tensor(name, shape, dt).__enter__()

    def pst(name, shape, dt=F32):
        return nc.psum_tensor(name, shape, dt).__enter__()

    id_sb = sbt("id_sb", [128, 128], BF16)
    onesb = sbt("onesb_s", [128, 2], BF16)
    onesf = sbt("onesf_s", [128, 128], F32)
    tril = sbt("tril_s", [128, 896], BF16)
    bq_sb = sbt("bq_sb", [128, HPC], F32)
    bk_sb = sbt("bk_sb", [128, HPC], F32)
    bvT_sb = sbt("bvT_sb", [128, H], F32)
    bvTb = sbt("bvTb", [128, H], BF16)
    bo_row = sbt("bo_row", [1, WIN], F32)
    bvf_sb = sbt("bvf_sb", [1, EPC], F32)
    smr = sbt("smr", [1, 64], F32)
    smc = sbt("smc", [128, 64], F32)
    mp_sb = sbt("mp_sb", [8, 8], F32)
    kmax = sbt("kmax", [1, 2 * HPC], F32)
    rs_sb = sbt("rs_sb", [1, WIN], F32)
    fac_x = sbt("fac_x", [128, 8], F32)
    sa1 = sbt("sa1", [128, 8], F32)
    sb2 = sbt("sb2", [128, 8], F32)
    s3s = sbt("s3s", [128, 8], F32)
    hmax_sc = sbt("hmax_sc", [128, 8], F32)
    hm_c = sbt("hm_c", [128, 96], F32)
    wpart = sbt("wpart", [128, 128], F32)

    bigB = sbt("bigB", [128, 70400], BF16)
    bigF = sbt("bigF", [128, 10300], F32)
    bigH = sbt("bigH", [128, HID], FP16)
    ab, af = Arena(bigB, 70400), Arena(bigF, 10300)

    psA = pst("psA", [128, 1024])
    psB = pst("psB", [128, 1024])
    psZ = pst("psZ", [128, 512])
    psS = pst("psS", [128, 512])
    psW = psS
    psX1 = pst("psX1", [128, 1024], BF16)
    psX2 = pst("psX2", [128, 1024], BF16)

    SY, GP, VE, SC, PE = nc.sync, nc.gpsimd, nc.vector, nc.scalar, nc.tensor

    s_ld = Sem(nc, "s_ld")
    s_x = [Sem(nc, "s_x0"), Sem(nc, "s_x1")]
    s_w = Sem(nc, "s_w")
    s_ws = [Sem(nc, "s_ws0"), Sem(nc, "s_ws1")]
    s_ve = Sem(nc, "s_ve")
    s_ac = Sem(nc, "s_ac")
    s_pe = Sem(nc, "s_pe")
    s_wsp = Sem(nc, "s_wsp")
    s_wspy = Sem(nc, "s_wspy")
    s_kvsp = [Sem(nc, "s_kvsp0"), Sem(nc, "s_kvsp1")]
    s_zsp = [Sem(nc, "s_zsp0"), Sem(nc, "s_zsp1")]
    s_x2sp = [Sem(nc, "s_x2sp0"), Sem(nc, "s_x2sp1")]
    s_hsp = [Sem(nc, "s_hsp0"), Sem(nc, "s_hsp1")]
    s_ccm = Sem(nc, "s_ccm")
    s_ccw = [Sem(nc, "s_ccw1"), Sem(nc, "s_ccw2"), Sem(nc, "s_ccw3")]
    s_ccz = Sem(nc, "s_ccz")
    s_k = [Sem(nc, "s_k%d" % i) for i in range(4)]
    s_v = [Sem(nc, "s_v%d" % i) for i in range(4)]
    s_zl = [Sem(nc, "s_zl0"), Sem(nc, "s_zl1")]
    s_wr = [Sem(nc, "s_wr0"), Sem(nc, "s_wr1")]
    s_w3r = Sem(nc, "s_w3r")
    s_hl = [Sem(nc, "s_hl0"), Sem(nc, "s_hl1")]
    s_x2l = [Sem(nc, "s_x2l0"), Sem(nc, "s_x2l1")]
    s_ou = [Sem(nc, "s_ou0"), Sem(nc, "s_ou1")]

    for dst, src in ((id_sb, identd), (onesb, onesbd), (onesf, onesfd),
                     (tril, trild), (bq_sb, bqT), (bk_sb, bkT),
                     (bvT_sb, bvTf), (bvf_sb, bvfl)):
        s_ld.inc(SY.dma_start(out=dst[:], in_=src[:, :]), 16)
    LD0 = s_ld.n

    VE.wait_ge(s_ld.h, LD0)
    VE.memset(kmax[:, HPC:2 * HPC], 0.0)
    VE.memset(smc[:, 63:64], 1e-5)
    ve_bvtb = s_ve.inc(VE.tensor_copy(out=bvTb[:], in_=bvT_sb[:]))
    wqkv = ab.alloc(3 * HPC * DK * 128)
    OKO = HPC * DK * 128
    OVO = 2 * OKO
    for hh in range(HPC):
        for dk in range(DK):
            o = (hh * DK + dk) * 128
            s_w.inc(GP.dma_start(out=wqkv[:, o:o + 128],
                                 in_=wqd[hh, dk * 128:dk * 128 + 128, :]), 16)
            s_w.inc(GP.dma_start(out=wqkv[:, OKO + o:OKO + o + 128],
                                 in_=wkd[hh, dk * 128:dk * 128 + 128, :]), 16)
            s_w.inc(GP.dma_start(out=wqkv[:, OVO + dk * EPC + hh * DH:
                                          OVO + dk * EPC + hh * DH + DH],
                                 in_=wvd[hh, dk * 128:dk * 128 + 128, :]), 16)
    W_QKV = s_w.n
    if PHASE_LIMIT < 0.2:
        GP.wait_ge(s_w.h, W_QKV)
        return nc

    bvbc = af.alloc(EPC)
    wshf = af.alloc(2 * 1408)
    af_a2 = af.mark()

    PE.wait_ge(s_ld.h, LD0)
    pe_bv = s_pe.inc(PE.matmul(psS[:], onesf[0:1, :], bvf_sb[:, :],
                               start=True, stop=True))
    VE.wait_ge(s_pe.h, pe_bv)
    ve_bvbc = s_ve.inc(VE.tensor_copy(out=bvbc[:], in_=psS[:]))
    VE.drain()


    wsha = ab.alloc(1408)
    wshb = ab.alloc(1408)
    wshq = ab.alloc(1408)
    wsht = ab.alloc(1408)
    ab_a2 = ab.mark()

    ab.reset(ab_a2)
    af.reset(af_a2)
    x_t = af.alloc(2 * D)
    ksq = af.alloc(WIN)
    rb_f = af.alloc(WIN)
    xn_tm = ab.alloc(2 * D)
    xn_fm = ab.alloc(DK * WIN)
    q_sb = ab.alloc(HPC * WIN)
    kvst = ab.alloc(2 * WIN)
    ktile = ab.alloc(8 * 128)
    vtile = ab.alloc(8 * 128)
    pt_a = ab.alloc(1024)
    pt_b = ab.alloc(1024)
    pt_m = ab.alloc(1024)
    zst = ab.alloc(2 * WIN)

    x_rd = {}
    xn_ev = {}
    psx_ve = 0
    prod_guard = [None, None, None, None]
    kvst_sp = {0: 0, 1: 0}
    kv_spilled = {}
    exp_done = {}
    exp_bank = {0: 0, 1: 0}
    pt_cons = {0: 0, 1: 0}
    ptm_cons = 0
    kv_ld = {}
    kv_ring = {}
    zsp_par = {0: 0, 1: 0}
    norm_psS_free = ve_bvbc
    negc_psW_free = ve_bvbc
    negc_slot_rd = {}
    attn_psAB_free = None

    w1v = w1sd.rearrange("(c p) w -> p c w", p=128)
    w2v = w2sd.rearrange("(c p) w -> p c w", p=128)
    a1a_units = ([("s", w1v, scn, W12_SH, scn) for scn in range(11)]
                 + [("s", w2v, scn, W12_SH, 44 + scn) for scn in range(11)]
                 + [("c", w3sd, ch, W3_SH, 88 + ch) for ch in range(DK)])
    a1a_red = {}
    a1a_i = [0]
    a1v = {}
    a1b_units = ([("s", w1v, scn, W12_SH, 0, w1ti.ap(), scn == 10)
                  for scn in range(11)]
                 + [("s", w2v, scn, W12_SH, 1, w2ti.ap(), scn == 10)
                    for scn in range(11)]
                 + [("c", w3sd, ch, W3_SH, 2, w3ti.ap(), ch == DK - 1)
                    for ch in range(DK)])
    a1b_flat = a1b_units
    a1b_i = [0]
    a1b_st = {"ig": 0, "sp_done": {-1: 0}, "q_done": {}}

    def emit_a1a_batch(n):
        i0 = a1a_i[0]
        for i in range(i0, min(i0 + n, len(a1a_units))):
            kind, dsrc, idx, wdt, pcol = a1a_units[i]
            par = i % 2
            buf = par * 1408
            ww = 4 * wdt if kind == "s" else wdt
            if i >= 2:
                GP.wait_ge(s_ve.h, a1a_red[i - 2])
            if kind == "s":
                s_ws[par].inc(GP.dma_start(
                    out=wshf[:, buf:buf + ww].rearrange("p (c w) -> p c w",
                                                        w=wdt),
                    in_=dsrc[:, 4 * idx:4 * idx + 4, :]), 16)
            else:
                s_ws[par].inc(GP.dma_start(
                    out=wshf[:, buf:buf + ww],
                    in_=dsrc[idx * 128:idx * 128 + 128, :]), 16)
            VE.wait_ge(s_ws[par].h, s_ws[par].n)
            a1a_red[i] = s_ve.inc(VE.reduce_sum(
                out=wpart[:, pcol:pcol + 1], in_=wshf[:, buf:buf + ww],
                axis=AX.X, apply_absolute_value=True))
        a1a_i[0] = min(i0 + n, len(a1a_units))

    def emit_a1a_tail_a():
        nonlocal norm_psS_free, negc_psW_free
        VE.drain()
        VE.reduce_sum(out=smc[:, 0:1], in_=wpart[:, 0:11], axis=AX.X)
        VE.reduce_sum(out=smc[:, 1:2], in_=wpart[:, 44:55], axis=AX.X)
        VE.reduce_sum(out=smc[:, 2:3], in_=wpart[:, 88:104], axis=AX.X)
        VE.drain()
        ve_wp = s_ve.inc(VE.engine_nop())
        PE.wait_ge(s_ve.h, max(ve_wp, ve_bvbc, norm_psS_free))
        PE.wait_ge(s_ac.h, s_ac.n)
        pe_mp = s_pe.inc(PE.matmul(psS[0:1, 0:3], onesf[:, 0:1], smc[:, 0:3],
                                   start=True, stop=True))
        VE.wait_ge(s_pe.h, pe_mp)
        VE.memset(smr[:, 0:16], 0.0)
        VE.drain()
        ve_mp = s_ve.inc(VE.tensor_copy(out=smr[:, 0:3], in_=psS[0:1, 0:3]))
        VE.drain()
        GP.wait_ge(s_ve.h, ve_mp)
        s_wsp.inc(GP.dma_start(out=mp_in.ap()[:, :], in_=smr[:, 0:8]), 16)
        GP.wait_ge(s_wsp.h, s_wsp.n)
        a1v["cc_mp"] = s_ccm.inc(GP.collective_compute(
            "AllGather", ALU.bypass, replica_groups=[list(range(8))],
            ins=[mp_in.ap().opt()], outs=[mp_g.ap().opt()]))
        a1v["ve_mp"] = ve_mp
        norm_psS_free = max(norm_psS_free, ve_mp)
        negc_psW_free = max(negc_psW_free, ve_mp)

    def emit_a1a_tail_b():
        nonlocal norm_psS_free, negc_psW_free
        GP.wait_ge(s_ccm.h, a1v["cc_mp"])
        sp_mpg = s_wsp.inc(GP.dma_start(out=mp_sb[:, :], in_=mp_g.ap()[:, :]),
                           16)
        PE.wait_ge(s_wsp.h, sp_mpg)
        PE.wait_ge(s_ve.h, max(a1v["ve_mp"], norm_psS_free))
        PE.wait_ge(s_ac.h, s_ac.n)
        pe_mp2 = s_pe.inc(PE.matmul(psS[0:1, 0:8], onesf[0:8, 0:1], mp_sb[:, :],
                                    start=True, stop=True))
        VE.wait_ge(s_pe.h, pe_mp2)
        VE.tensor_scalar(out=smr[:, 8:11], in0=psS[0:1, 0:3],
                         scalar1=1.0 / NWELEM, scalar2=1e-5,
                         op0=ALU.mult, op1=ALU.max)
        VE.drain()
        ve_mean = s_ve.inc(VE.tensor_scalar_mul(out=smr[:, 12:15],
                                                in0=smr[:, 8:11], scalar1=0.5))
        VE.drain()
        PE.wait_ge(s_ve.h, ve_mean)
        pe_bc = s_pe.inc(PE.matmul(psW[:, 0:8], onesf[0:1, :], smr[:, 8:16],
                                   start=True, stop=True))
        VE.wait_ge(s_pe.h, pe_bc)
        VE.tensor_copy(out=smc[:, 8:16], in_=psW[:, 0:8])
        VE.drain()
        VE.tensor_scalar_mul(out=smc[:, 16:19], in0=smc[:, 12:15], scalar1=-1.0)
        VE.drain()
        ve_thr = s_ve.inc(VE.engine_nop())
        a1v["ve_thr"] = ve_thr
        norm_psS_free = max(norm_psS_free, ve_thr)
        negc_psW_free = max(negc_psW_free, ve_thr)

    def emit_a1b_batch(n):
        nonlocal psx_ve
        i0 = a1b_i[0]
        q_done = a1b_st["q_done"]
        sp_done = a1b_st["sp_done"]
        for ix in range(i0, min(i0 + n, len(a1b_units))):
            kind, dsrc, idx, wdt, wi, dout, lastch = a1b_units[ix]
            ig = a1b_st["ig"]
            par = ig % 2
            buf = par * 1408
            ww = 4 * wdt if kind == "s" else wdt
            if ig >= 2:
                GP.wait_ge(s_ve.h, q_done[ig - 2])
            else:
                GP.wait_ge(s_ve.h, a1v["ve_thr"])
            if kind == "s":
                s_ws[par].inc(GP.dma_start(
                    out=wshf[:, buf:buf + ww].rearrange("p (c w) -> p c w",
                                                        w=wdt),
                    in_=dsrc[:, 4 * idx:4 * idx + 4, :]), 16)
            else:
                s_ws[par].inc(GP.dma_start(
                    out=wshf[:, buf:buf + ww],
                    in_=dsrc[idx * 128:idx * 128 + 128, :]), 16)
            VE.wait_ge(s_ws[par].h, s_ws[par].n)
            if ig == 0:
                VE.wait_ge(s_ve.h, a1v["ve_thr"])
            VE.wait_ge(s_wspy.h, sp_done[ig - 1])
            VE.tensor_single_scalar(out=wsha[:, 0:ww],
                                    in_=wshf[:, buf:buf + ww],
                                    scalar=smc[:, 12 + wi:13 + wi], op=ALU.is_ge)
            VE.tensor_single_scalar(out=wshb[:, 0:ww],
                                    in_=wshf[:, buf:buf + ww],
                                    scalar=smc[:, 16 + wi:17 + wi], op=ALU.is_le)
            VE.drain()
            q_done[ig] = s_ve.inc(VE.tensor_tensor(out=wshq[:, 0:ww],
                                                   in0=wsha[:, 0:ww],
                                                   in1=wshb[:, 0:ww],
                                                   op=ALU.subtract))
            PE.wait_ge(s_ve.h, q_done[ig])
            ev = {}
            ntl, tw = (8, 128) if kind == "s" else (11, 64)
            for j in range(ntl):
                px = psX1 if j % 2 == 0 else psX2
                if j >= 2:
                    PE.wait_ge(s_ve.h, ev[j - 2])
                pet = s_pe.inc(PE.transpose(px[0:tw, 0:128],
                                            wshq[0:128, j * tw:(j + 1) * tw],
                                            id_sb[:]))
                VE.wait_ge(s_pe.h, pet)
                ev[j] = s_ve.inc(VE.tensor_copy(
                    out=wsht[0:tw, j * 128:(j + 1) * 128],
                    in_=px[0:tw, 0:128]))
            psx_ve = max(psx_ve, ev[ntl - 1])
            GP.wait_ge(s_ve.h, ev[ntl - 1])
            if kind == "s":
                wst3 = wsht[0:128, 0:1024].rearrange("p (c j t) -> p c j t",
                                                     j=2, t=128)
                for j in range(2):
                    s_wspy.inc(GP.dma_start(
                        out=dout[j * 128:(j + 1) * 128,
                                 idx * 512:(idx + 1) * 512]
                        .rearrange("p (c t) -> p c t", t=128),
                        in_=wst3[:, :, j, :]), 16)
            else:
                for j in range(ntl):
                    s_wspy.inc(GP.dma_start(
                        out=dout[j * tw:(j + 1) * tw,
                                 idx * 128:idx * 128 + 128],
                        in_=wsht[0:tw, j * 128:(j + 1) * 128]), 16)
            sp_done[ig] = s_wspy.n
            a1b_st["ig"] = ig + 1
            if lastch:
                _wt = ((w1ti, w1tg), (w2ti, w2tg), (w3ti, w3tg))[wi]
                GP.wait_ge(s_wspy.h, s_wspy.n)
                s_ccw[wi].inc(GP.collective_compute(
                    "AllGather", ALU.bypass, replica_groups=[list(range(8))],
                    ins=[_wt[0].ap().opt()], outs=[_wt[1].ap().opt()]))
        a1b_i[0] = min(i0 + n, len(a1b_units))


    for w in range(NW):
        for ts_ in range(4):
            ti = 4 * w + ts_
            p = ti % 2
            bx = p * D
            if ti >= 2:
                SY.wait_ge(s_ve.h, x_rd[ti - 2][0])
                SY.wait_ge(s_ac.h, x_rd[ti - 2][1])
            s_x[p].inc(SY.dma_start(out=x_t[:, bx:bx + D],
                                    in_=xs[ti * 128:ti * 128 + 128, :]), 16)
            xl = s_x[p].n
            VE.wait_ge(s_x[p].h, xl)
            VE.reduce_sum(out=smc[:, 20 + p:21 + p], in_=x_t[:, bx:bx + D],
                          axis=AX.X)
            SC.wait_ge(s_x[p].h, xl)
            ac_sq = s_ac.inc(SC.activation(out=bigH[:, 0:D], in_=x_t[:, bx:bx + D],
                                           func=AF.Square,
                                           accum_out=smc[:, 22 + p:23 + p]))
            VE.drain()
            VE.wait_ge(s_ac.h, ac_sq)
            VE.tensor_scalar_mul(out=smc[:, 24 + p:25 + p],
                                 in0=smc[:, 20 + p:21 + p], scalar1=INV_D)
            VE.tensor_scalar_mul(out=smc[:, 26 + p:27 + p],
                                 in0=smc[:, 22 + p:23 + p], scalar1=INV_D)
            VE.drain()
            VE.tensor_tensor(out=smc[:, 28 + p:29 + p], in0=smc[:, 24 + p:25 + p],
                             in1=smc[:, 24 + p:25 + p], op=ALU.mult)
            VE.drain()
            ve_var = s_ve.inc(VE.tensor_tensor(out=smc[:, 30 + p:31 + p],
                                               in0=smc[:, 26 + p:27 + p],
                                               in1=smc[:, 28 + p:29 + p],
                                               op=ALU.subtract))
            SC.wait_ge(s_ve.h, ve_var)
            ac_st = s_ac.inc(SC.activation(out=smc[:, 32 + p:33 + p],
                                           in_=smc[:, 30 + p:31 + p],
                                           func=AF.Sqrt, bias=smc[:, 63:64]))
            VE.wait_ge(s_ac.h, ac_st)
            VE.reciprocal(out=smc[:, 34 + p:35 + p], in_=smc[:, 32 + p:33 + p])
            VE.drain()
            VE.tensor_tensor(out=smc[:, 36 + p:37 + p], in0=smc[:, 24 + p:25 + p],
                             in1=smc[:, 34 + p:35 + p], op=ALU.mult)
            VE.drain()
            if ti >= 2:
                VE.wait_ge(s_pe.h, xn_ev[ti - 2][1])
            ve_xn = s_ve.inc(VE.tensor_scalar(out=xn_tm[:, bx:bx + D],
                                              in0=x_t[:, bx:bx + D],
                                              scalar1=smc[:, 34 + p:35 + p],
                                              scalar2=smc[:, 36 + p:37 + p],
                                              op0=ALU.mult, op1=ALU.subtract))
            x_rd[ti] = (ve_xn, ac_sq)
            PE.wait_ge(s_ve.h, ve_xn)
            evd = {}
            for dk in range(DK):
                px = psX1 if dk % 2 == 0 else psX2
                if dk >= 2:
                    PE.wait_ge(s_ve.h, evd[dk - 2])
                elif psx_ve:
                    PE.wait_ge(s_ve.h, psx_ve)
                pet = s_pe.inc(PE.transpose(px[:, 0:128],
                                            xn_tm[:, bx + dk * 128:bx + dk * 128 + 128],
                                            id_sb[:]))
                VE.wait_ge(s_pe.h, pet)
                evd[dk] = s_ve.inc(VE.tensor_copy(
                    out=xn_fm[:, dk * WIN + ts_ * 128:dk * WIN + ts_ * 128 + 128],
                    in_=px[:, 0:128]))
            psx_ve = evd[DK - 1]
            xn_ev[ti] = (evd[DK - 1], s_pe.n)
        xn_ok = xn_ev[4 * w + 3][0]

        prod_banks = [(psA, 0), (psA, 512), (psB, 0), (psB, 512)]
        si = 0
        for kind in ("k", "q"):
            wb = OKO if kind == "k" else 0
            for hh in range(HPC):
                bank, boff = prod_banks[si % 4]
                g = prod_guard[si % 4]
                if g is not None:
                    PE.wait_ge((s_ac if g[0] == "ac" else s_ve).h, g[1])
                PE.wait_ge(s_ve.h, xn_ok)
                if w == 0 and si == 0:
                    PE.wait_ge(s_w.h, W_QKV)
                last = None
                for dk in range(DK):
                    last = PE.matmul(bank[:, boff:boff + WIN],
                                     wqkv[:, wb + (hh * DK + dk) * 128:
                                          wb + (hh * DK + dk) * 128 + 128],
                                     xn_fm[:, dk * WIN:(dk + 1) * WIN],
                                     start=(dk == 0), stop=(dk == DK - 1))
                pe_prod = s_pe.inc(last)
                bias = bk_sb[:, hh:hh + 1] if kind == "k" else bq_sb[:, hh:hh + 1]
                par = si % 2
                stg = par * WIN
                SC.wait_ge(s_pe.h, pe_prod)
                if kind == "k":
                    SC.wait_ge(s_kvsp[par].h, kvst_sp[par])
                if kind == "k":
                    dst = kvst[:, stg:stg + WIN]
                else:
                    dst = q_sb[:, hh * WIN:(hh + 1) * WIN]
                ac_e = s_ac.inc(SC.activation(out=dst, in_=bank[:, boff:boff + WIN],
                                              func=AF.Identity, bias=bias))
                ac_sqr = s_ac.inc(SC.activation(out=ksq[:, :],
                                                in_=bank[:, boff:boff + WIN],
                                                func=AF.Square, bias=bias))
                prod_guard[si % 4] = ("ac", ac_sqr)
                PE.wait_ge(s_ac.h, ac_sqr)
                PE.wait_ge(s_ve.h, max(norm_psS_free, negc_psW_free))
                pe_n = s_pe.inc(PE.matmul(psS[0:1, :], onesf[:, 0:1], ksq[:, :],
                                          start=True, stop=True))
                VE.wait_ge(s_pe.h, pe_n)
                VE.reduce_max(out=smr[:, 20:21], in_=psS[0:1, :], axis=AX.X)
                VE.drain()
                if kind == "k":
                    cur = (w % 2) * HPC + hh
                    prv = ((w + 1) % 2) * HPC + hh
                    VE.tensor_tensor(out=kmax[:, cur:cur + 1],
                                     in0=kmax[:, prv:prv + 1],
                                     in1=smr[:, 20:21], op=ALU.max)
                else:
                    VE.tensor_copy(out=smr[:, 24 + hh:25 + hh], in_=smr[:, 20:21])
                VE.drain()
                norm_psS_free = s_ve.inc(VE.engine_nop())
                if kind == "k":
                    GP.wait_ge(s_ac.h, ac_e)
                    kvst_sp[par] = s_kvsp[par].inc(GP.dma_start(
                        out=k_d[hh * DH:hh * DH + DH, w * WIN:(w + 1) * WIN],
                        in_=kvst[:, stg:stg + WIN]), 16)
                si += 1
        for ts_ in range(4):
            bank, boff = prod_banks[si % 4]
            g = prod_guard[si % 4]
            if g is not None:
                PE.wait_ge((s_ac if g[0] == "ac" else s_ve).h, g[1])
            last = None
            for dk in range(DK):
                last = PE.matmul(bank[:, boff:boff + WIN],
                                 xn_fm[:, dk * WIN + ts_ * 128:dk * WIN + ts_ * 128 + 128],
                                 wqkv[:, OVO + dk * EPC:OVO + (dk + 1) * EPC],
                                 start=(dk == 0), stop=(dk == DK - 1))
            pe_v = s_pe.inc(last)
            par = si % 2
            stg = par * WIN
            VE.wait_ge(s_pe.h, pe_v)
            VE.wait_ge(s_kvsp[par].h, kvst_sp[par])
            VE.wait_ge(s_ve.h, ve_bvbc)
            ve_v = s_ve.inc(VE.scalar_tensor_tensor(
                out=kvst[:, stg:stg + WIN], in0=bank[:, boff:boff + WIN],
                scalar=1.0, in1=bvbc[:, :], op0=ALU.mult, op1=ALU.add))
            prod_guard[si % 4] = ("ve", ve_v)
            GP.wait_ge(s_ve.h, ve_v)
            kvst_sp[par] = s_kvsp[par].inc(GP.dma_start(
                out=v_d[w * WIN + ts_ * 128:w * WIN + (ts_ + 1) * 128, :],
                in_=kvst[:, stg:stg + WIN]), 16)
            si += 1
        kv_spilled[w] = (s_kvsp[0].n, s_kvsp[1].n)

        cur0 = (w % 2) * HPC
        VE.tensor_tensor(out=smr[:, 44:48], in0=smr[:, 24:28],
                         in1=kmax[:, cur0:cur0 + 4], op=ALU.mult)
        VE.drain()
        ve_c2 = s_ve.inc(VE.engine_nop())
        SC.wait_ge(s_ve.h, ve_c2)
        ac_c = s_ac.inc(SC.activation(out=smr[:, 48:52], in_=smr[:, 44:48],
                                      func=AF.Sqrt, scale=1.0 / 128.0))
        VE.wait_ge(s_ac.h, ac_c)
        VE.tensor_scalar_mul(out=smr[:, 52:56], in0=smr[:, 48:52],
                             scalar1=-1.0)
        VE.drain()
        ve_nc = s_ve.inc(VE.engine_nop())
        PE.wait_ge(s_ve.h, max(ve_nc, negc_psW_free))
        PE.wait_ge(s_ac.h, max(negc_slot_rd.get(cur0 + h_, 0)
                               for h_ in range(HPC)))
        pe_nc = s_pe.inc(PE.matmul(psW[:, 0:4], onesf[0:1, :], smr[:, 52:56],
                                   start=True, stop=True))
        VE.wait_ge(s_pe.h, pe_nc)
        negc_psW_free = s_ve.inc(VE.tensor_copy(
            out=smc[:, 40 + cur0:44 + cur0], in_=psW[:, 0:4]))
        norm_psS_free = max(norm_psS_free, negc_psW_free)
        ve_negc = negc_psW_free

        for hh in range(HPC):
            cur = (w % 2) * HPC + hh
            ncol = 40 + cur

            n_kt = 4 * (w + 1)
            n_g = n_kt // 2

            def load_kv(kt):
                slot = kt % 4
                for which in (0, 1):
                    g = kv_ring.get((which, slot), 0)
                    if g:
                        SY.wait_ge(s_pe.h, g)
                    SY.wait_ge(s_kvsp[0].h, kv_spilled[w][0])
                    SY.wait_ge(s_kvsp[1].h, kv_spilled[w][1])
                    if which == 0:
                        ii = SY.dma_start(
                            out=ktile[:, slot * 128:slot * 128 + 128],
                            in_=k_d[hh * DH:hh * DH + DH,
                                    kt * 128:kt * 128 + 128])
                        kv_ld[(0, kt)] = s_k[slot].inc(ii, 16)
                    else:
                        ii = SY.dma_start(
                            out=vtile[:, slot * 128:slot * 128 + 128],
                            in_=v_d[kt * 128:kt * 128 + 128,
                                    hh * DH:hh * DH + DH])
                        kv_ld[(1, kt)] = s_v[slot].inc(ii, 16)

            for kt in range(min(4, n_kt)):
                load_kv(kt)
            for gi in range(n_g):
                pp = gi % 2
                bank = psA if pp == 0 else psB
                ptb = pt_a if pp == 0 else pt_b
                kt0, kt1 = 2 * gi, 2 * gi + 1
                PE.wait_ge(s_k[kt0 % 4].h, kv_ld[(0, kt0)])
                PE.wait_ge(s_k[kt1 % 4].h, kv_ld[(0, kt1)])
                if exp_bank[pp]:
                    PE.wait_ge(s_ac.h, exp_bank[pp])
                elif gi < 2:
                    g = prod_guard[2 * pp]
                    PE.wait_ge((s_ac if g[0] == "ac" else s_ve).h, g[1])
                    g = prod_guard[2 * pp + 1]
                    PE.wait_ge((s_ac if g[0] == "ac" else s_ve).h, g[1])
                PE.matmul(bank[:, 0:WIN],
                          ktile[:, (kt0 % 4) * 128:(kt0 % 4) * 128 + 128],
                          q_sb[:, hh * WIN:(hh + 1) * WIN], start=True, stop=True)
                pe_sc = s_pe.inc(PE.matmul(
                    bank[:, WIN:2 * WIN],
                    ktile[:, (kt1 % 4) * 128:(kt1 % 4) * 128 + 128],
                    q_sb[:, hh * WIN:(hh + 1) * WIN], start=True, stop=True))
                kv_ring[(0, kt0 % 4)] = pe_sc
                kv_ring[(0, kt1 % 4)] = pe_sc
                SC.wait_ge(s_pe.h, pe_sc)
                SC.wait_ge(s_ve.h, ve_negc)
                if pt_cons[pp]:
                    SC.wait_ge(s_pe.h, pt_cons[pp])
                ac_e = s_ac.inc(SC.activation(out=ptb[:, :], in_=bank[:, :],
                                              func=AF.Exp, scale=ISQ,
                                              bias=smc[:, ncol:ncol + 1]))
                exp_bank[pp] = ac_e
                exp_done[(w, hh)] = ac_e
                negc_slot_rd[cur] = ac_e
                use = ptb
                if kt1 >= 4 * w:
                    VE.wait_ge(s_ac.h, ac_e)
                    if ptm_cons:
                        VE.wait_ge(s_pe.h, ptm_cons)
                    for which, kt in ((0, kt0), (1, kt1)):
                        if kt >= 4 * w:
                            o = (kt - 4 * w) * 128
                            VE.tensor_tensor(
                                out=pt_m[:, which * WIN:(which + 1) * WIN],
                                in0=ptb[:, which * WIN:(which + 1) * WIN],
                                in1=tril[:, 384 - o:896 - o], op=ALU.mult)
                        else:
                            VE.tensor_copy(
                                out=pt_m[:, which * WIN:(which + 1) * WIN],
                                in_=ptb[:, which * WIN:(which + 1) * WIN])
                    ve_m = s_ve.inc(VE.engine_nop())
                    use = pt_m
                    PE.wait_ge(s_ve.h, ve_m)
                else:
                    PE.wait_ge(s_ac.h, ac_e)
                first = (gi == 0)
                last_g = (gi == n_g - 1)
                PE.wait_ge(s_v[kt0 % 4].h, kv_ld[(1, kt0)])
                PE.wait_ge(s_v[kt1 % 4].h, kv_ld[(1, kt1)])
                PE.matmul(psS[0:1, :], onesb[:, 0:1], use[:, 0:WIN],
                          start=first, stop=False)
                PE.matmul(psS[0:1, :], onesb[:, 0:1], use[:, WIN:2 * WIN],
                          start=False, stop=last_g)
                PE.matmul(psZ[:], vtile[:, (kt0 % 4) * 128:(kt0 % 4) * 128 + 128],
                          use[:, 0:WIN], start=first, stop=False)
                pe_z = s_pe.inc(PE.matmul(
                    psZ[:], vtile[:, (kt1 % 4) * 128:(kt1 % 4) * 128 + 128],
                    use[:, WIN:2 * WIN], start=False, stop=last_g))
                kv_ring[(1, kt0 % 4)] = pe_z
                kv_ring[(1, kt1 % 4)] = pe_z
                pt_cons[pp] = pe_z
                if use is pt_m:
                    ptm_cons = pe_z
                for kt in (2 * gi + 4, 2 * gi + 5):
                    if kt < n_kt:
                        load_kv(kt)
            VE.wait_ge(s_pe.h, s_pe.n)
            VE.reciprocal(out=rs_sb[:, :], in_=psS[0:1, :])
            VE.drain()
            ve_rs = s_ve.inc(VE.engine_nop())
            norm_psS_free = ve_rs
            PE.wait_ge(s_ve.h, max(ve_rs, negc_psW_free))
            pe_rb = s_pe.inc(PE.matmul(psW[:, :], onesf[0:1, :], rs_sb[:, :],
                                       start=True, stop=True))
            zpar = hh % 2
            zb = zpar * WIN
            VE.wait_ge(s_pe.h, pe_rb)
            VE.wait_ge(s_zsp[zpar].h, zsp_par[zpar])
            VE.tensor_copy(out=rb_f[:, :], in_=psW[:, 0:WIN])
            VE.drain()
            ve_z = s_ve.inc(VE.tensor_tensor(out=zst[:, zb:zb + WIN], in0=psZ[:],
                                             in1=rb_f[:, :], op=ALU.mult))
            negc_psW_free = ve_z
            GP.wait_ge(s_ve.h, ve_z)
            zsp_par[zpar] = s_zsp[zpar].inc(GP.dma_start(
                out=zin[w, hh * DH:hh * DH + DH, :],
                in_=zst[:, zb:zb + WIN]), 16)
            PE.wait_ge(s_ve.h, ve_z)
        GP.wait_ge(s_zsp[0].h, zsp_par[0])
        GP.wait_ge(s_zsp[1].h, zsp_par[1])
        s_ccz.inc(GP.collective_compute(
            "AllGather", ALU.bypass, replica_groups=[[0, 1, 2, 3], [4, 5, 6, 7]],
            ins=[zin[w, :, :].opt()], outs=[zg[w, :, :].opt()]))

        if w < 3:
            emit_a1a_batch(13)
        if w == 2:
            emit_a1a_tail_a()
        if w == 3:
            emit_a1a_tail_b()
        if w >= 3:
            emit_a1b_batch(8)

    A2_PE, A2_VE, A2_AC = s_pe.n, s_ve.n, s_ac.n
    emit_a1b_batch(len(a1b_units))
    assert a1b_i[0] == len(a1b_units) and a1a_i[0] == len(a1a_units)
    A1_PE, A1_VE, A1_AC = s_pe.n, s_ve.n, s_ac.n
    A1_SPILLS = s_wspy.n


    if PHASE_LIMIT < 3:
        GP.wait_ge(s_ccz.h, s_ccz.n)
        return nc
    ab.reset(0)
    wo_sb = ab.alloc(H * D)
    ztile = ab.alloc(2 * 2048)
    xq_t = ab.alloc(D)
    xq_fm = ab.alloc(DK * TPC)
    ab_b1 = ab.mark()
    af.reset(0)
    x_own = af.alloc(2 * D)
    x2_t = af.alloc(D)
    silu_s = af.alloc(2 * WIN)
    h_t = af.alloc(2 * WIN)
    r_bcf = af.alloc(D)
    af_b1 = af.mark()
    u16 = bigH[:, 0:D]

    GP.wait_ge(s_pe.h, A1_PE)
    GP.wait_ge(s_ve.h, A1_VE)
    GP.wait_ge(s_wspy.h, A1_SPILLS)
    SY.wait_ge(s_ve.h, A1_VE)
    wo_ld = {}
    for hx in range(H):
        wo_ld[hx] = s_w.inc(GP.dma_start(out=wo_sb[:, hx * D:(hx + 1) * D],
                                         in_=wod[hx, :, :]), 16)
    W_WO = s_w.n

    PE.wait_ge(s_ve.h, A1_VE)
    PE.wait_ge(s_ac.h, A1_AC)
    rb_pe = 0
    for dsd in range(4):
        if rb_pe:
            SY.wait_ge(s_pe.h, rb_pe)
        s_ld.inc(SY.dma_start(out=bo_row[:, :],
                              in_=bod[0:1, dsd * WIN:(dsd + 1) * WIN]), 16)
        BOL = s_ld.n
        PE.wait_ge(s_ld.h, BOL)
        if dsd > 0:
            PE.wait_ge(s_ve.h, rb_ve)
        rb_pe = s_pe.inc(PE.matmul(psW[:, :], onesf[0:1, :], bo_row[:, :],
                                   start=True, stop=True))
        VE.wait_ge(s_pe.h, rb_pe)
        rb_ve = s_ve.inc(VE.tensor_copy(out=r_bcf[:, dsd * WIN:(dsd + 1) * WIN],
                                        in_=psW[:, :]))
    R_DONE = s_ve.n

    SY.wait_ge(s_ccz.h, s_ccz.n)
    q2 = SY.compute_val(SY.partition_id() % 4 * 2)

    zgv = zg.rearrange("w (h e) (t c) -> w t e h c", h=H, c=128)
    xs3 = xs.rearrange("(u t c) d -> u t c d", t=4, c=128)

    zring = {-2: 0, -1: 0}
    xo_rd = {-2: 0, -1: 0}
    x2sp_par = {0: 0, 1: 0}
    opbank_ve = 0
    psxb_ve = 0
    xq_ready = {}
    for tc in range(8):
        zpar = tc % 2
        zb = zpar * 2048
        SY.wait_ge(s_pe.h, zring[tc - 2])
        s_zl[zpar].inc(SY.dma_start(
            out=ztile[:, zb:zb + 2048].rearrange("e (h c) -> e h c", h=H),
            in_=zgv[q2 + tc // 4, tc % 4, :, :, :]), 16)
        ZL = s_zl[zpar].n
        PE.wait_ge(s_zl[zpar].h, ZL)
        if tc == 0:
            PE.wait_ge(s_ve.h, R_DONE)
        if opbank_ve:
            PE.wait_ge(s_ve.h, opbank_ve)
        last = None
        for hx in range(H):
            if tc == 0:
                PE.wait_ge(s_w.h, wo_ld[hx])
            for dsd in range(4):
                bb = psA if dsd < 2 else psB
                bo_ = (dsd % 2) * 512
                last = PE.matmul(bb[:, bo_:bo_ + WIN],
                                 ztile[:, zb + hx * 128:zb + hx * 128 + 128],
                                 wo_sb[:, hx * D + dsd * WIN:hx * D + (dsd + 1) * WIN],
                                 start=(hx == 0), stop=(hx == H - 1))
        pe_op = s_pe.inc(last)
        zring[tc] = pe_op
        OP_PE = s_pe.n
        p = tc % 2
        bx = p * D
        SY.wait_ge(s_ve.h, xo_rd[tc - 2])
        SY.wait_ge(s_x2sp[p].h, x2sp_par[p])
        s_x[p].inc(SY.dma_start(out=x_own[:, bx:bx + D],
                                in_=xs3[q2 + tc // 4, tc % 4, :, :]), 16)
        XO = s_x[p].n
        VE.wait_ge(s_x[p].h, XO)
        VE.wait_ge(s_pe.h, OP_PE)
        for dsd in range(4):
            bb = psA if dsd < 2 else psB
            bo_ = (dsd % 2) * 512
            VE.scalar_tensor_tensor(
                out=x2_t[:, dsd * WIN:(dsd + 1) * WIN], in0=bb[:, bo_:bo_ + WIN],
                scalar=1.0, in1=x_own[:, bx + dsd * WIN:bx + (dsd + 1) * WIN],
                op0=ALU.mult, op1=ALU.add)
        VE.drain()
        opbank_ve = s_ve.inc(VE.engine_nop())
        VE.wait_ge(s_ve.h, max(opbank_ve, R_DONE))
        ve_x2 = s_ve.inc(VE.tensor_tensor(out=x_own[:, bx:bx + D], in0=x2_t[:, :],
                                          in1=r_bcf[:, :], op=ALU.add))
        VE.drain()
        GP.wait_ge(s_ve.h, ve_x2)
        x2sp_par[p] = s_x2sp[p].inc(GP.dma_start(
            out=x2_d[tc * 128:tc * 128 + 128, :], in_=x_own[:, bx:bx + D]), 16)
        VE.reduce_sum(out=smc[:, 20 + p:21 + p], in_=x_own[:, bx:bx + D], axis=AX.X)
        VE.reduce_max(out=smc[:, 48 + p:49 + p], in_=x_own[:, bx:bx + D], axis=AX.X)
        VE.tensor_reduce(out=smc[:, 50 + p:51 + p], in_=x_own[:, bx:bx + D],
                         axis=AX.X, op=ALU.min)
        SC.wait_ge(s_ve.h, ve_x2)
        ac_sq = s_ac.inc(SC.activation(out=bigH[:, D:2 * D], in_=x_own[:, bx:bx + D],
                                       func=AF.Square,
                                       accum_out=smc[:, 22 + p:23 + p]))
        VE.drain()
        VE.wait_ge(s_ac.h, ac_sq)
        VE.tensor_scalar_mul(out=smc[:, 24 + p:25 + p], in0=smc[:, 20 + p:21 + p],
                             scalar1=INV_D)
        VE.tensor_scalar_mul(out=smc[:, 26 + p:27 + p], in0=smc[:, 22 + p:23 + p],
                             scalar1=INV_D)
        VE.drain()
        VE.tensor_tensor(out=smc[:, 28 + p:29 + p], in0=smc[:, 24 + p:25 + p],
                         in1=smc[:, 24 + p:25 + p], op=ALU.mult)
        VE.drain()
        ve_var = s_ve.inc(VE.tensor_tensor(out=smc[:, 30 + p:31 + p],
                                           in0=smc[:, 26 + p:27 + p],
                                           in1=smc[:, 28 + p:29 + p],
                                           op=ALU.subtract))
        SC.wait_ge(s_ve.h, ve_var)
        ac_st = s_ac.inc(SC.activation(out=smc[:, 32 + p:33 + p],
                                       in_=smc[:, 30 + p:31 + p],
                                       func=AF.Sqrt, bias=smc[:, 63:64]))
        VE.wait_ge(s_ac.h, ac_st)
        VE.reciprocal(out=smc[:, 34 + p:35 + p], in_=smc[:, 32 + p:33 + p])
        VE.drain()
        VE.tensor_tensor(out=smc[:, 36 + p:37 + p], in0=smc[:, 48 + p:49 + p],
                         in1=smc[:, 24 + p:25 + p], op=ALU.subtract)
        VE.tensor_tensor(out=smc[:, 38 + p:39 + p], in0=smc[:, 24 + p:25 + p],
                         in1=smc[:, 50 + p:51 + p], op=ALU.subtract)
        VE.drain()
        VE.tensor_tensor(out=smc[:, 40 + p:41 + p], in0=smc[:, 36 + p:37 + p],
                         in1=smc[:, 38 + p:39 + p], op=ALU.max)
        VE.drain()
        VE.tensor_tensor(out=smc[:, 42 + p:43 + p], in0=smc[:, 40 + p:41 + p],
                         in1=smc[:, 34 + p:35 + p], op=ALU.mult)
        VE.drain()
        VE.tensor_scalar_max(out=smc[:, 44 + p:45 + p], in0=smc[:, 42 + p:43 + p],
                             scalar1=1e-5)
        VE.drain()
        VE.reciprocal(out=smc[:, 46 + p:47 + p], in_=smc[:, 44 + p:45 + p])
        VE.tensor_scalar_mul(out=fac_x[:, tc:tc + 1], in0=smc[:, 44 + p:45 + p],
                             scalar1=1.0 / 127.0)
        VE.drain()
        VE.tensor_scalar_mul(out=smc[:, 52 + p:53 + p], in0=smc[:, 46 + p:47 + p],
                             scalar1=127.0)
        VE.drain()
        VE.tensor_tensor(out=smc[:, 54 + p:55 + p], in0=smc[:, 34 + p:35 + p],
                         in1=smc[:, 52 + p:53 + p], op=ALU.mult)
        VE.drain()
        VE.tensor_tensor(out=smc[:, 56 + p:57 + p], in0=smc[:, 24 + p:25 + p],
                         in1=smc[:, 54 + p:55 + p], op=ALU.mult)
        VE.drain()
        VE.tensor_scalar(out=smc[:, 58 + p:59 + p], in0=smc[:, 56 + p:57 + p],
                         scalar1=-1.0, scalar2=RM, op0=ALU.mult, op1=ALU.add)
        VE.drain()
        VE.tensor_scalar(out=u16[:, :], in0=x_own[:, bx:bx + D],
                         scalar1=smc[:, 54 + p:55 + p],
                         scalar2=smc[:, 58 + p:59 + p],
                         op0=ALU.mult, op1=ALU.add)
        VE.drain()
        ve_xq = s_ve.inc(VE.tensor_single_scalar(out=xq_t[:, :], in_=u16[:, :],
                                                 scalar=-RM, op=ALU.add))
        xo_rd[tc] = ve_xq
        PE.wait_ge(s_ve.h, ve_xq)
        evd = {}
        for dk in range(DK):
            px = psX1 if dk % 2 == 0 else psX2
            if dk >= 2:
                PE.wait_ge(s_ve.h, evd[dk - 2])
            elif psxb_ve:
                PE.wait_ge(s_ve.h, psxb_ve)
            elif psx_ve:
                PE.wait_ge(s_ve.h, psx_ve)
            pet = s_pe.inc(PE.transpose(px[:, 0:128],
                                        xq_t[:, dk * 128:dk * 128 + 128],
                                        id_sb[:]))
            VE.wait_ge(s_pe.h, pet)
            evd[dk] = s_ve.inc(VE.tensor_copy(
                out=xq_fm[:, dk * TPC + tc * 128:dk * TPC + tc * 128 + 128],
                in_=px[:, 0:128]))
        psxb_ve = evd[DK - 1]
        xq_ready[tc] = evd[DK - 1]

    VE.tensor_single_scalar(out=sa1[:, 0:8], in_=fac_x[:, 0:8],
                            scalar=smc[:, 8:9], op=ALU.mult)
    VE.tensor_single_scalar(out=sb2[:, 0:8], in_=fac_x[:, 0:8],
                            scalar=smc[:, 9:10], op=ALU.mult)
    VE.drain()
    SC_READY = s_ve.inc(VE.engine_nop())
    B1_PE = s_pe.n

    if PHASE_LIMIT < 4:
        VE.wait_ge(s_ve.h, s_ve.n)
        return nc
    w12r = bigB[:, 0:2 * 2 * DK * WIN]
    wr_done = {}
    wr_used = {-2: 0, -1: 0}
    hb_guard = {0: None, 1: None}
    silu_g = {0: 0, 1: 0}
    hsp_par = {0: 0, 1: 0}
    for hs in range(NHS):
        wpar = hs % 2
        wb = wpar * (2 * DK * WIN)
        SY.wait_ge(s_pe.h, max(wr_used[hs - 2], B1_PE))
        if hs == 0:
            SY.wait_ge(s_ccw[0].h, 1)
            SY.wait_ge(s_ccw[1].h, 1)
        for dk in range(DK):
            s_wr[wpar].inc(SY.dma_start(
                out=w12r[:, wb + dk * WIN:wb + (dk + 1) * WIN],
                in_=w1tg.ap()[dk * 128:dk * 128 + 128, hs * WIN:(hs + 1) * WIN]), 16)
            s_wr[wpar].inc(SY.dma_start(
                out=w12r[:, wb + DK * WIN + dk * WIN:wb + DK * WIN + (dk + 1) * WIN],
                in_=w2tg.ap()[dk * 128:dk * 128 + 128, hs * WIN:(hs + 1) * WIN]), 16)
        wr_done[hs] = s_wr[wpar].n
        for tc in range(8):
            pp = tc % 2
            bankA = psA if pp == 0 else psB
            PE.wait_ge(s_wr[wpar].h, wr_done[hs])
            if hs == 0:
                PE.wait_ge(s_ve.h, xq_ready[tc])
                if tc < 2:
                    PE.wait_ge(s_ve.h, opbank_ve)
            g = hb_guard[pp]
            if g is not None:
                PE.wait_ge(s_ve.h, g[0])
                PE.wait_ge(s_ac.h, g[1])
            last = None
            for dk in range(DK):
                st, en = (dk == 0), (dk == DK - 1)
                PE.matmul(bankA[:, 0:WIN],
                          xq_fm[:, dk * TPC + tc * 128:dk * TPC + tc * 128 + 128],
                          w12r[:, wb + dk * WIN:wb + (dk + 1) * WIN],
                          start=st, stop=en)
                last = PE.matmul(bankA[:, WIN:2 * WIN],
                                 xq_fm[:, dk * TPC + tc * 128:dk * TPC + tc * 128 + 128],
                                 w12r[:, wb + DK * WIN + dk * WIN:
                                      wb + DK * WIN + (dk + 1) * WIN],
                                 start=st, stop=en)
            pe_mm = s_pe.inc(last)
            wr_used[hs] = pe_mm
            sst = pp * WIN
            SC.wait_ge(s_pe.h, pe_mm)
            if hs == 0 and tc == 0:
                SC.wait_ge(s_ve.h, SC_READY)
            if silu_g[pp]:
                SC.wait_ge(s_ve.h, silu_g[pp])
            VE.wait_ge(s_hsp[pp].h, hsp_par[pp])
            ac_si = s_ac.inc(SC.activation(out=silu_s[:, sst:sst + WIN],
                                           in_=bankA[:, 0:WIN], func=AF.Silu,
                                           scale=sa1[:, tc:tc + 1]))
            VE.wait_ge(s_ac.h, ac_si)
            ve_h = s_ve.inc(VE.scalar_tensor_tensor(
                out=h_t[:, sst:sst + WIN], in0=bankA[:, WIN:2 * WIN],
                scalar=sb2[:, tc:tc + 1], in1=silu_s[:, sst:sst + WIN],
                op0=ALU.mult, op1=ALU.mult))
            silu_g[pp] = ve_h
            VE.drain()
            ve_hm = s_ve.inc(VE.reduce_max(
                out=hm_c[:, tc * 12 + hs:tc * 12 + hs + 1],
                in_=h_t[:, sst:sst + WIN], axis=AX.X, apply_absolute_value=True))
            hb_guard[pp] = (ve_h, ac_si)
            GP.wait_ge(s_ve.h, ve_hm)
            hsp_par[pp] = s_hsp[pp].inc(GP.dma_start(
                out=h_d[tc * 128:tc * 128 + 128, hs * WIN:(hs + 1) * WIN],
                in_=h_t[:, sst:sst + WIN]), 16)
    B3_PE = s_pe.n
    VE.drain()
    for tc in range(8):
        p = tc % 2
        VE.reduce_max(out=smc[:, 20 + p:21 + p],
                      in_=hm_c[:, tc * 12:tc * 12 + 11], axis=AX.X)
        VE.drain()
        VE.tensor_scalar_max(out=smc[:, 22 + p:23 + p],
                             in0=smc[:, 20 + p:21 + p], scalar1=1e-5)
        VE.drain()
        VE.reciprocal(out=smc[:, 24 + p:25 + p], in_=smc[:, 22 + p:23 + p])
        VE.tensor_scalar_mul(out=smc[:, 26 + p:27 + p], in0=smc[:, 22 + p:23 + p],
                             scalar1=1.0 / 127.0)
        VE.drain()
        VE.tensor_scalar_mul(out=hmax_sc[:, tc:tc + 1], in0=smc[:, 24 + p:25 + p],
                             scalar1=127.0)
        VE.tensor_single_scalar(out=s3s[:, tc:tc + 1], in_=smc[:, 26 + p:27 + p],
                                scalar=smc[:, 10:11], op=ALU.mult)
        VE.drain()
    HS_READY = s_ve.inc(VE.engine_nop())
    B3_VE = s_ve.n

    if PHASE_LIMIT < 5:
        GP.wait_ge(s_hsp[0].h, s_hsp[0].n)
        GP.wait_ge(s_hsp[1].h, s_hsp[1].n)
        return nc
    ab.reset(0)
    hq_fm = ab.alloc(HKN * TPC)
    hq_t = ab.alloc(HID // 2)
    w3r = ab.alloc(HKN * WIN)
    af.reset(0)
    h_ld = af.alloc(HID)
    x2_ld = af.alloc(2 * WIN)
    out_st = af.alloc(2 * WIN)
    u16h = bigH[:, 0:HID]

    psxc_ve = psxb_ve
    SY.wait_ge(s_hsp[0].h, s_hsp[0].n)
    SY.wait_ge(s_hsp[1].h, s_hsp[1].n)
    SY.wait_ge(s_x2sp[0].h, s_x2sp[0].n)
    SY.wait_ge(s_x2sp[1].h, s_x2sp[1].n)
    HHALF = HID // 2
    QH = HHALF // 2
    NTQ = QH // 128
    hl_rd = {-2: 0, -1: 0}
    hl_ld = {}
    qt_pe = {-2: 0, -1: 0}
    pxg = {0: psxc_ve, 1: psxc_ve}

    def issue_hl(idx):
        if idx >= 16:
            return
        tc_, hf_ = idx // 2, idx % 2
        sl = idx % 2
        SY.wait_ge(s_ve.h, hl_rd[idx - 2])
        hl_ld[idx] = s_hl[sl].inc(SY.dma_start(
            out=h_ld[:, sl * HHALF:(sl + 1) * HHALF],
            in_=h_d[tc_ * 128:tc_ * 128 + 128,
                    hf_ * HHALF:(hf_ + 1) * HHALF]), 16)

    issue_hl(0)
    issue_hl(1)
    for idx in range(16):
        tc, hf = idx // 2, idx % 2
        sl = idx % 2
        VE.wait_ge(s_hl[sl].h, hl_ld[idx])
        if idx == 0:
            VE.wait_ge(s_ve.h, HS_READY)
        for q in range(2):
            gq = idx * 2 + q
            qsl = gq % 2
            VE.wait_ge(s_pe.h, qt_pe[gq - 2])
            VE.tensor_scalar(out=u16h[:, qsl * QH:(qsl + 1) * QH],
                             in0=h_ld[:, sl * HHALF + q * QH:
                                      sl * HHALF + (q + 1) * QH],
                             scalar1=hmax_sc[:, tc:tc + 1], scalar2=RM,
                             op0=ALU.mult, op1=ALU.add)
            VE.drain()
            ve_hq = s_ve.inc(VE.tensor_single_scalar(
                out=hq_t[:, qsl * QH:(qsl + 1) * QH],
                in_=u16h[:, qsl * QH:(qsl + 1) * QH],
                scalar=-RM, op=ALU.add))
            if q == 1:
                hl_rd[idx] = ve_hq
            PE.wait_ge(s_ve.h, ve_hq)
            if gq == 0:
                PE.wait_ge(s_pe.h, B3_PE)
            for hj in range(NTQ):
                hk = hf * (HKN // 2) + q * NTQ + hj
                b = hj % 2
                PE.wait_ge(s_ve.h, pxg[b])
                pet = s_pe.inc(PE.transpose(
                    (psX1 if b == 0 else psX2)[:, 0:128],
                    hq_t[:, qsl * QH + hj * 128:qsl * QH + hj * 128 + 128],
                    id_sb[:]))
                VE.wait_ge(s_pe.h, pet)
                pxg[b] = s_ve.inc(VE.tensor_copy(
                    out=hq_fm[:, hk * TPC + tc * 128:hk * TPC + tc * 128 + 128],
                    in_=(psX1 if b == 0 else psX2)[:, 0:128]))
            qt_pe[gq] = s_pe.n
        issue_hl(idx + 2)
    HQ_ALL = s_ve.n

    if PHASE_LIMIT < 6:
        VE.wait_ge(s_ve.h, s_ve.n)
        return nc
    w3_used = {-1: 0}
    b5_g = {0: 0, 1: 0}
    out_g = {0: 0, 1: 0}
    x2l = {-2: 0, -1: 0}
    it = 0
    for dq in range(4):
        SY.wait_ge(s_pe.h, max(w3_used[dq - 1], B3_PE))
        if dq == 0:
            SY.wait_ge(s_ve.h, B3_VE)
        if dq == 0:
            SY.wait_ge(s_ccw[2].h, 1)
        W3L = 0
        for hk in range(HKN):
            W3L = s_w3r.inc(SY.dma_start(
                out=w3r[:, hk * WIN:(hk + 1) * WIN],
                in_=w3tg.ap()[hk * 128:hk * 128 + 128, dq * WIN:(dq + 1) * WIN]),
                16)
        for tc in range(8):
            pp = tc % 2
            bank = psZ if pp == 0 else psS
            PE.wait_ge(s_w3r.h, W3L)
            if dq == 0 and tc == 0:
                PE.wait_ge(s_ve.h, HQ_ALL)
            if b5_g[pp]:
                PE.wait_ge(s_ve.h, b5_g[pp])
            last = None
            for hk in range(HKN):
                last = PE.matmul(bank[:],
                                 hq_fm[:, hk * TPC + tc * 128:hk * TPC + tc * 128 + 128],
                                 w3r[:, hk * WIN:(hk + 1) * WIN],
                                 start=(hk == 0), stop=(hk == HKN - 1))
            pe_m3 = s_pe.inc(last)
            w3_used[dq] = pe_m3
            xb = pp * WIN
            SY.wait_ge(s_ve.h, x2l[it - 2])
            if dq == 0 and tc == 0:
                SY.wait_ge(s_x2sp[0].h, s_x2sp[0].n)
                SY.wait_ge(s_x2sp[1].h, s_x2sp[1].n)
            X2L = s_x2l[pp].inc(SY.dma_start(
                out=x2_ld[:, xb:xb + WIN],
                in_=x2_d[tc * 128:tc * 128 + 128,
                         dq * WIN:(dq + 1) * WIN]), 16)
            VE.wait_ge(s_x2l[pp].h, X2L)
            VE.wait_ge(s_pe.h, pe_m3)
            if out_g[pp]:
                VE.wait_ge(s_ou[pp].h, out_g[pp])
            ve_o = s_ve.inc(VE.scalar_tensor_tensor(
                out=out_st[:, xb:xb + WIN], in0=bank[:], scalar=s3s[:, tc:tc + 1],
                in1=x2_ld[:, xb:xb + WIN], op0=ALU.mult, op1=ALU.add))
            b5_g[pp] = ve_o
            x2l[it] = ve_o
            SY.wait_ge(s_ve.h, ve_o)
            out_g[pp] = s_ou[pp].inc(SY.dma_start(
                out=out_d[tc * 128:tc * 128 + 128, dq * WIN:(dq + 1) * WIN],
                in_=out_st[:, xb:xb + WIN]), 16)
            it += 1
    for pp in (0, 1):
        SY.wait_ge(s_ou[pp].h, s_ou[pp].n)
        GP.wait_ge(s_ou[pp].h, s_ou[pp].n)
        VE.wait_ge(s_ou[pp].h, s_ou[pp].n)

    return nc


_NC_CACHE = {}


def kernel(L0_attn_x, L0_attn_W_Q, L0_attn_W_K, L0_attn_W_V, L0_attn_W_O,
           L0_attn_b_Q, L0_attn_b_K, L0_attn_b_V, L0_attn_b_O,
           L0_ffn_w1, L0_ffn_w2, L0_ffn_w3):
    x = np.ascontiguousarray(np.asarray(L0_attn_x, dtype=np.float32))
    WQ = np.asarray(L0_attn_W_Q, dtype=np.float32)
    WK = np.asarray(L0_attn_W_K, dtype=np.float32)
    WV = np.asarray(L0_attn_W_V, dtype=np.float32)
    WO = np.ascontiguousarray(np.asarray(L0_attn_W_O, dtype=np.float32))
    bQ = np.asarray(L0_attn_b_Q, dtype=np.float32)
    bK = np.asarray(L0_attn_b_K, dtype=np.float32)
    bV = np.asarray(L0_attn_b_V, dtype=np.float32)
    bO = np.asarray(L0_attn_b_O, dtype=np.float32).reshape(1, D)
    w1 = np.asarray(L0_ffn_w1, dtype=np.float32)
    w2 = np.asarray(L0_ffn_w2, dtype=np.float32)
    w3 = np.asarray(L0_ffn_w3, dtype=np.float32)

    import ml_dtypes
    ident = np.eye(128, dtype=ml_dtypes.bfloat16)
    onesb = np.ones((128, 2), dtype=ml_dtypes.bfloat16)
    onesf = np.ones((128, 128), dtype=np.float32)
    uu = np.arange(896)[None, :]
    ii = np.arange(128)[:, None]
    tril = (uu >= ii + 384).astype(ml_dtypes.bfloat16)

    bqT = np.ascontiguousarray(bQ.T)
    bkT = np.ascontiguousarray(bK.T)
    bvT = np.ascontiguousarray(bV.T)

    if "nc" not in _NC_CACHE:
        _NC_CACHE["nc"] = build_program()
    nc = _NC_CACHE["nc"]

    in_maps = []
    for c in range(8):
        s = c // 4
        g = c % 4
        hs = slice(4 * g, 4 * g + 4)
        in_maps.append({
            "xs": np.ascontiguousarray(x[s]),
            "wq": np.ascontiguousarray(WQ[hs]),
            "wk": np.ascontiguousarray(WK[hs]),
            "wv": np.ascontiguousarray(WV[hs]),
            "wo": WO,
            "bqT": np.ascontiguousarray(bqT[:, hs]),
            "bkT": np.ascontiguousarray(bkT[:, hs]),
            "bvTf": bvT,
            "bvfl": np.ascontiguousarray(bV[hs].reshape(1, EPC)),
            "bo": bO,
            "w1s": np.ascontiguousarray(w1[:, c * W12_SH:(c + 1) * W12_SH]),
            "w2s": np.ascontiguousarray(w2[:, c * W12_SH:(c + 1) * W12_SH]),
            "w3s": np.ascontiguousarray(w3[:, c * W3_SH:(c + 1) * W3_SH]),
            "ident": ident,
            "onesb": onesb,
            "onesf": onesf,
            "tril": tril,
        })
    res = run_bass_kernel_spmd(nc, in_maps, core_ids=list(range(8)))
    out = np.empty((2, S, D), dtype=np.float32)
    for c in range(8):
        s, g = c // 4, c % 4
        out[s, g * TPC:(g + 1) * TPC, :] = res.results[c]["out"]
    return out


# version 15 (final)
# speedup vs baseline: 1.0552x; 1.0188x over previous
import math

import numpy as np
import concourse.bass as bass
import concourse.mybir as mybir
from concourse.bass_utils import run_bass_kernel_spmd

F32 = mybir.dt.float32
BF16 = mybir.dt.bfloat16
FP16 = mybir.dt.float16
AF = mybir.ActivationFunctionType
ALU = mybir.AluOpType
AX = mybir.AxisListType

S, D, H, DH = 4096, 2048, 16, 128
HID = 5632
HPC, EPC, TPC, WIN = 4, 512, 1024, 512
NW = S // WIN
DK = D // 128
HKN = HID // 128
NHS = HID // WIN
W12_SH = D // 8
W3_SH = HID // 8
ISQ = 1.0 / math.sqrt(DH)
RM = 1536.0
NWELEM = float(HID * D)
INV_D = 1.0 / D


class Sem:
    def __init__(self, nc, name):
        self.h = nc.semaphore(name).__enter__()
        self.n = 0

    def inc(self, instr, by=1):
        instr.then_inc(self.h, by)
        self.n += by
        return self.n


class Arena:
    def __init__(self, t, cols):
        self.t, self.cols, self.cur = t, cols, 0

    def alloc(self, n):
        assert self.cur + n <= self.cols, ("arena", self.cur, n, self.cols)
        off = self.cur
        self.cur += n
        return self.t[:, off:off + n]

    def reset(self, m=0):
        self.cur = m

    def mark(self):
        return self.cur


DEBUG_INTERNALS = False
PHASE_LIMIT = 99
A1B_CHUNKS = None
A1B_ONLY_W3 = False


def build_program():
    nc = bass.Bass("TRN2", target_bir_lowering=False, debug=False)
    dbg = "ExternalOutput" if DEBUG_INTERNALS else "Internal"

    xs = nc.dram_tensor("xs", [S, D], F32, kind="ExternalInput").ap()
    wqd = nc.dram_tensor("wq", [HPC, D, DH], F32, kind="ExternalInput").ap()
    wkd = nc.dram_tensor("wk", [HPC, D, DH], F32, kind="ExternalInput").ap()
    wvd = nc.dram_tensor("wv", [HPC, D, DH], F32, kind="ExternalInput").ap()
    wod = nc.dram_tensor("wo", [H, DH, D], F32, kind="ExternalInput").ap()
    bqT = nc.dram_tensor("bqT", [DH, HPC], F32, kind="ExternalInput").ap()
    bkT = nc.dram_tensor("bkT", [DH, HPC], F32, kind="ExternalInput").ap()
    bvTf = nc.dram_tensor("bvTf", [DH, H], F32, kind="ExternalInput").ap()
    bvfl = nc.dram_tensor("bvfl", [1, EPC], F32, kind="ExternalInput").ap()
    bod = nc.dram_tensor("bo", [1, D], F32, kind="ExternalInput").ap()
    w1sd = nc.dram_tensor("w1s", [HID, W12_SH], F32, kind="ExternalInput").ap()
    w2sd = nc.dram_tensor("w2s", [HID, W12_SH], F32, kind="ExternalInput").ap()
    w3sd = nc.dram_tensor("w3s", [D, W3_SH], F32, kind="ExternalInput").ap()
    identd = nc.dram_tensor("ident", [128, 128], BF16, kind="ExternalInput").ap()
    onesbd = nc.dram_tensor("onesb", [128, 2], BF16, kind="ExternalInput").ap()
    onesfd = nc.dram_tensor("onesf", [128, 128], F32, kind="ExternalInput").ap()
    trild = nc.dram_tensor("tril", [128, 896], BF16, kind="ExternalInput").ap()
    out_d = nc.dram_tensor("out", [TPC, D], F32, kind="ExternalOutput").ap()

    k_d = nc.dram_tensor("k_d", [EPC, S], BF16, kind=dbg).ap()
    v_d = nc.dram_tensor("v_d", [S, EPC], BF16, kind=dbg).ap()
    zin_t = nc.dram_tensor("zin", [NW, EPC, WIN], BF16, kind=dbg)
    zg_t = nc.dram_tensor("zg", [NW, D, WIN], BF16)
    zin, zg = zin_t.ap(), zg_t.ap()
    mp_in = nc.dram_tensor("mp_in", [1, 8], F32)
    mp_g = nc.dram_tensor("mp_g", [8, 8], F32)
    w1ti = nc.dram_tensor("w1t_in", [W12_SH, HID], BF16, kind=dbg)
    w2ti = nc.dram_tensor("w2t_in", [W12_SH, HID], BF16)
    w3ti = nc.dram_tensor("w3t_in", [W3_SH, D], BF16)
    w1tg = nc.dram_tensor("w1t_g", [D, HID], BF16)
    w2tg = nc.dram_tensor("w2t_g", [D, HID], BF16)
    w3tg = nc.dram_tensor("w3t_g", [HID, D], BF16)
    x2_d = nc.dram_tensor("x2_d", [TPC, D], F32, kind=dbg).ap()
    h_d = nc.dram_tensor("h_d", [TPC, HID], F32, kind=dbg).ap()

    def sbt(name, shape, dt):
        return nc.sbuf_tensor(name, shape, dt).__enter__()

    def pst(name, shape, dt=F32):
        return nc.psum_tensor(name, shape, dt).__enter__()

    id_sb = sbt("id_sb", [128, 128], BF16)
    onesb = sbt("onesb_s", [128, 2], BF16)
    onesf = sbt("onesf_s", [128, 128], F32)
    tril = sbt("tril_s", [128, 896], BF16)
    bq_sb = sbt("bq_sb", [128, HPC], F32)
    bk_sb = sbt("bk_sb", [128, HPC], F32)
    bvT_sb = sbt("bvT_sb", [128, H], F32)
    bvTb = sbt("bvTb", [128, H], BF16)
    bo_row = sbt("bo_row", [1, WIN], F32)
    bvf_sb = sbt("bvf_sb", [1, EPC], F32)
    smr = sbt("smr", [1, 64], F32)
    smc = sbt("smc", [128, 64], F32)
    mp_sb = sbt("mp_sb", [8, 8], F32)
    kmax = sbt("kmax", [1, 2 * HPC], F32)
    rs_sb = sbt("rs_sb", [1, WIN], F32)
    fac_x = sbt("fac_x", [128, 8], F32)
    sa1 = sbt("sa1", [128, 8], F32)
    sb2 = sbt("sb2", [128, 8], F32)
    s3s = sbt("s3s", [128, 8], F32)
    hmax_sc = sbt("hmax_sc", [128, 8], F32)
    hm_c = sbt("hm_c", [128, 96], F32)
    wpart = sbt("wpart", [128, 128], F32)

    bigB = sbt("bigB", [128, 70400], BF16)
    bigF = sbt("bigF", [128, 10300], F32)
    bigH = sbt("bigH", [128, HID], FP16)
    ab, af = Arena(bigB, 70400), Arena(bigF, 10300)

    psA = pst("psA", [128, 1024])
    psB = pst("psB", [128, 1024])
    psZ = pst("psZ", [128, 512])
    psS = pst("psS", [128, 512])
    psW = psS
    psX1 = pst("psX1", [128, 1024], BF16)
    psX2 = pst("psX2", [128, 1024], BF16)

    SY, GP, VE, SC, PE = nc.sync, nc.gpsimd, nc.vector, nc.scalar, nc.tensor

    s_ld = Sem(nc, "s_ld")
    s_x = [Sem(nc, "s_x0"), Sem(nc, "s_x1")]
    s_w = Sem(nc, "s_w")
    s_ws = [Sem(nc, "s_ws0"), Sem(nc, "s_ws1")]
    s_ve = Sem(nc, "s_ve")
    s_ac = Sem(nc, "s_ac")
    s_pe = Sem(nc, "s_pe")
    s_wsp = Sem(nc, "s_wsp")
    s_wspy = Sem(nc, "s_wspy")
    s_kvsp = [Sem(nc, "s_kvsp0"), Sem(nc, "s_kvsp1")]
    s_zsp = [Sem(nc, "s_zsp0"), Sem(nc, "s_zsp1")]
    s_x2sp = [Sem(nc, "s_x2sp0"), Sem(nc, "s_x2sp1")]
    s_hsp = [Sem(nc, "s_hsp0"), Sem(nc, "s_hsp1")]
    s_ccm = Sem(nc, "s_ccm")
    s_ccw = [Sem(nc, "s_ccw1"), Sem(nc, "s_ccw2"), Sem(nc, "s_ccw3")]
    s_ccz = Sem(nc, "s_ccz")
    s_k = [Sem(nc, "s_k%d" % i) for i in range(4)]
    s_v = [Sem(nc, "s_v%d" % i) for i in range(4)]
    s_zl = [Sem(nc, "s_zl0"), Sem(nc, "s_zl1")]
    s_wr = [Sem(nc, "s_wr0"), Sem(nc, "s_wr1")]
    s_w3r = Sem(nc, "s_w3r")
    s_hl = [Sem(nc, "s_hl0"), Sem(nc, "s_hl1")]
    s_x2l = [Sem(nc, "s_x2l0"), Sem(nc, "s_x2l1")]
    s_ou = [Sem(nc, "s_ou0"), Sem(nc, "s_ou1")]

    for dst, src in ((id_sb, identd), (onesb, onesbd), (onesf, onesfd),
                     (tril, trild), (bq_sb, bqT), (bk_sb, bkT),
                     (bvT_sb, bvTf), (bvf_sb, bvfl)):
        s_ld.inc(SY.dma_start(out=dst[:], in_=src[:, :]), 16)
    LD0 = s_ld.n

    VE.wait_ge(s_ld.h, LD0)
    VE.memset(kmax[:, HPC:2 * HPC], 0.0)
    VE.memset(smc[:, 63:64], 1e-5)
    ve_bvtb = s_ve.inc(VE.tensor_copy(out=bvTb[:], in_=bvT_sb[:]))
    wqkv = ab.alloc(3 * HPC * DK * 128)
    OKO = HPC * DK * 128
    OVO = 2 * OKO
    for hh in range(HPC):
        for dk in range(DK):
            o = (hh * DK + dk) * 128
            s_w.inc(GP.dma_start(out=wqkv[:, o:o + 128],
                                 in_=wqd[hh, dk * 128:dk * 128 + 128, :]), 16)
            s_w.inc(GP.dma_start(out=wqkv[:, OKO + o:OKO + o + 128],
                                 in_=wkd[hh, dk * 128:dk * 128 + 128, :]), 16)
            s_w.inc(GP.dma_start(out=wqkv[:, OVO + dk * EPC + hh * DH:
                                          OVO + dk * EPC + hh * DH + DH],
                                 in_=wvd[hh, dk * 128:dk * 128 + 128, :]), 16)
    W_QKV = s_w.n
    if PHASE_LIMIT < 0.2:
        GP.wait_ge(s_w.h, W_QKV)
        return nc

    bvbc = af.alloc(EPC)
    wshf = af.alloc(2 * 1408)
    af_a2 = af.mark()

    PE.wait_ge(s_ld.h, LD0)
    pe_bv = s_pe.inc(PE.matmul(psS[:], onesf[0:1, :], bvf_sb[:, :],
                               start=True, stop=True))
    VE.wait_ge(s_pe.h, pe_bv)
    ve_bvbc = s_ve.inc(VE.tensor_copy(out=bvbc[:], in_=psS[:]))
    VE.drain()


    wsha = ab.alloc(1408)
    wshb = ab.alloc(1408)
    wshq = ab.alloc(1408)
    wsht = ab.alloc(1408)
    ab_a2 = ab.mark()

    ab.reset(ab_a2)
    af.reset(af_a2)
    x_t = af.alloc(2 * D)
    ksq = af.alloc(WIN)
    rb_f = af.alloc(WIN)
    xn_tm = ab.alloc(2 * D)
    xn_fm = ab.alloc(DK * WIN)
    q_sb = ab.alloc(HPC * WIN)
    kvst = ab.alloc(2 * WIN)
    ktile = ab.alloc(8 * 128)
    vtile = ab.alloc(8 * 128)
    pt_a = ab.alloc(1024)
    pt_b = ab.alloc(1024)
    pt_m = ab.alloc(1024)
    zst = ab.alloc(2 * WIN)

    x_rd = {}
    xn_ev = {}
    psx_ve = 0
    prod_guard = [None, None, None, None]
    kvst_sp = {0: 0, 1: 0}
    kv_spilled = {}
    exp_done = {}
    exp_bank = {0: 0, 1: 0}
    pt_cons = {0: 0, 1: 0}
    ptm_cons = 0
    kv_ld = {}
    kv_ring = {}
    zsp_par = {0: 0, 1: 0}
    norm_psS_free = ve_bvbc
    negc_psW_free = ve_bvbc
    negc_slot_rd = {}
    attn_psAB_free = None

    w1v = w1sd.rearrange("(c p) w -> p c w", p=128)
    w2v = w2sd.rearrange("(c p) w -> p c w", p=128)
    a1a_units = ([("s", w1v, scn, W12_SH, scn) for scn in range(11)]
                 + [("s", w2v, scn, W12_SH, 44 + scn) for scn in range(11)]
                 + [("c", w3sd, ch, W3_SH, 88 + ch) for ch in range(DK)])
    a1a_red = {}
    a1a_i = [0]
    a1v = {}
    a1b_units = ([("s", w1v, scn, W12_SH, 0, w1ti.ap(), scn == 10)
                  for scn in range(11)]
                 + [("s", w2v, scn, W12_SH, 1, w2ti.ap(), scn == 10)
                    for scn in range(11)]
                 + [("c", w3sd, ch, W3_SH, 2, w3ti.ap(), ch == DK - 1)
                    for ch in range(DK)])
    a1b_flat = a1b_units
    a1b_i = [0]
    a1b_st = {"ig": 0, "sp_done": {-1: 0}, "q_done": {}, "pet": 0}

    def emit_a1a_batch(n):
        i0 = a1a_i[0]
        for i in range(i0, min(i0 + n, len(a1a_units))):
            kind, dsrc, idx, wdt, pcol = a1a_units[i]
            par = i % 2
            buf = par * 1408
            ww = 4 * wdt if kind == "s" else wdt
            if i >= 2:
                GP.wait_ge(s_ve.h, a1a_red[i - 2])
            if kind == "s":
                s_ws[par].inc(GP.dma_start(
                    out=wshf[:, buf:buf + ww].rearrange("p (c w) -> p c w",
                                                        w=wdt),
                    in_=dsrc[:, 4 * idx:4 * idx + 4, :]), 16)
            else:
                s_ws[par].inc(GP.dma_start(
                    out=wshf[:, buf:buf + ww],
                    in_=dsrc[idx * 128:idx * 128 + 128, :]), 16)
            VE.wait_ge(s_ws[par].h, s_ws[par].n)
            a1a_red[i] = s_ve.inc(VE.reduce_sum(
                out=wpart[:, pcol:pcol + 1], in_=wshf[:, buf:buf + ww],
                axis=AX.X, apply_absolute_value=True))
        a1a_i[0] = min(i0 + n, len(a1a_units))

    def emit_a1a_tail_a():
        nonlocal norm_psS_free, negc_psW_free
        VE.drain()
        VE.reduce_sum(out=smc[:, 0:1], in_=wpart[:, 0:11], axis=AX.X)
        VE.reduce_sum(out=smc[:, 1:2], in_=wpart[:, 44:55], axis=AX.X)
        VE.reduce_sum(out=smc[:, 2:3], in_=wpart[:, 88:104], axis=AX.X)
        VE.drain()
        ve_wp = s_ve.inc(VE.engine_nop())
        PE.wait_ge(s_ve.h, max(ve_wp, ve_bvbc, norm_psS_free))
        PE.wait_ge(s_ac.h, s_ac.n)
        pe_mp = s_pe.inc(PE.matmul(psS[0:1, 0:3], onesf[:, 0:1], smc[:, 0:3],
                                   start=True, stop=True))
        VE.wait_ge(s_pe.h, pe_mp)
        VE.memset(smr[:, 0:16], 0.0)
        VE.drain()
        ve_mp = s_ve.inc(VE.tensor_copy(out=smr[:, 0:3], in_=psS[0:1, 0:3]))
        VE.drain()
        GP.wait_ge(s_ve.h, ve_mp)
        s_wsp.inc(GP.dma_start(out=mp_in.ap()[:, :], in_=smr[:, 0:8]), 16)
        GP.wait_ge(s_wsp.h, s_wsp.n)
        a1v["cc_mp"] = s_ccm.inc(GP.collective_compute(
            "AllGather", ALU.bypass, replica_groups=[list(range(8))],
            ins=[mp_in.ap().opt()], outs=[mp_g.ap().opt()]))
        a1v["ve_mp"] = ve_mp
        norm_psS_free = max(norm_psS_free, ve_mp)
        negc_psW_free = max(negc_psW_free, ve_mp)

    def emit_a1a_tail_b():
        nonlocal norm_psS_free, negc_psW_free
        GP.wait_ge(s_ccm.h, a1v["cc_mp"])
        sp_mpg = s_wsp.inc(GP.dma_start(out=mp_sb[:, :], in_=mp_g.ap()[:, :]),
                           16)
        PE.wait_ge(s_wsp.h, sp_mpg)
        PE.wait_ge(s_ve.h, max(a1v["ve_mp"], norm_psS_free))
        PE.wait_ge(s_ac.h, s_ac.n)
        pe_mp2 = s_pe.inc(PE.matmul(psS[0:1, 0:8], onesf[0:8, 0:1], mp_sb[:, :],
                                    start=True, stop=True))
        VE.wait_ge(s_pe.h, pe_mp2)
        VE.tensor_scalar(out=smr[:, 8:11], in0=psS[0:1, 0:3],
                         scalar1=1.0 / NWELEM, scalar2=1e-5,
                         op0=ALU.mult, op1=ALU.max)
        VE.drain()
        ve_mean = s_ve.inc(VE.tensor_scalar_mul(out=smr[:, 12:15],
                                                in0=smr[:, 8:11], scalar1=0.5))
        VE.drain()
        PE.wait_ge(s_ve.h, ve_mean)
        pe_bc = s_pe.inc(PE.matmul(psW[:, 0:8], onesf[0:1, :], smr[:, 8:16],
                                   start=True, stop=True))
        VE.wait_ge(s_pe.h, pe_bc)
        VE.tensor_copy(out=smc[:, 8:16], in_=psW[:, 0:8])
        VE.drain()
        VE.tensor_scalar_mul(out=smc[:, 16:19], in0=smc[:, 12:15], scalar1=-1.0)
        VE.drain()
        ve_thr = s_ve.inc(VE.engine_nop())
        a1v["ve_thr"] = ve_thr
        norm_psS_free = max(norm_psS_free, ve_thr)
        negc_psW_free = max(negc_psW_free, ve_thr)

    def emit_a1b_batch(n):
        nonlocal psx_ve
        i0 = a1b_i[0]
        q_done = a1b_st["q_done"]
        sp_done = a1b_st["sp_done"]
        for ix in range(i0, min(i0 + n, len(a1b_units))):
            kind, dsrc, idx, wdt, wi, dout, lastch = a1b_units[ix]
            ig = a1b_st["ig"]
            par = ig % 2
            buf = par * 1408
            ww = 4 * wdt if kind == "s" else wdt
            if ig >= 2:
                GP.wait_ge(s_ve.h, q_done[ig - 2])
            else:
                GP.wait_ge(s_ve.h, a1v["ve_thr"])
            if kind == "s":
                s_ws[par].inc(GP.dma_start(
                    out=wshf[:, buf:buf + ww].rearrange("p (c w) -> p c w",
                                                        w=wdt),
                    in_=dsrc[:, 4 * idx:4 * idx + 4, :]), 16)
            else:
                s_ws[par].inc(GP.dma_start(
                    out=wshf[:, buf:buf + ww],
                    in_=dsrc[idx * 128:idx * 128 + 128, :]), 16)
            VE.wait_ge(s_ws[par].h, s_ws[par].n)
            if ig == 0:
                VE.wait_ge(s_ve.h, a1v["ve_thr"])
            VE.wait_ge(s_pe.h, a1b_st["pet"])
            VE.tensor_single_scalar(out=wsha[:, 0:ww],
                                    in_=wshf[:, buf:buf + ww],
                                    scalar=smc[:, 12 + wi:13 + wi], op=ALU.is_ge)
            VE.tensor_single_scalar(out=wshb[:, 0:ww],
                                    in_=wshf[:, buf:buf + ww],
                                    scalar=smc[:, 16 + wi:17 + wi], op=ALU.is_le)
            VE.drain()
            q_done[ig] = s_ve.inc(VE.tensor_tensor(out=wshq[:, 0:ww],
                                                   in0=wsha[:, 0:ww],
                                                   in1=wshb[:, 0:ww],
                                                   op=ALU.subtract))
            PE.wait_ge(s_ve.h, q_done[ig])
            ev = {}
            ntl, tw = (8, 128) if kind == "s" else (11, 64)
            for j in range(ntl):
                px = psX1 if j % 2 == 0 else psX2
                if j >= 2:
                    PE.wait_ge(s_ve.h, ev[j - 2])
                pet = s_pe.inc(PE.transpose(px[0:tw, 0:128],
                                            wshq[0:128, j * tw:(j + 1) * tw],
                                            id_sb[:]))
                VE.wait_ge(s_pe.h, pet)
                if j == 0:
                    VE.wait_ge(s_wspy.h, sp_done[ig - 1])
                ev[j] = s_ve.inc(VE.tensor_copy(
                    out=wsht[0:tw, j * 128:(j + 1) * 128],
                    in_=px[0:tw, 0:128]))
            a1b_st["pet"] = s_pe.n
            psx_ve = max(psx_ve, ev[ntl - 1])
            GP.wait_ge(s_ve.h, ev[ntl - 1])
            if kind == "s":
                wst3 = wsht[0:128, 0:1024].rearrange("p (c j t) -> p c j t",
                                                     j=2, t=128)
                for j in range(2):
                    s_wspy.inc(GP.dma_start(
                        out=dout[j * 128:(j + 1) * 128,
                                 idx * 512:(idx + 1) * 512]
                        .rearrange("p (c t) -> p c t", t=128),
                        in_=wst3[:, :, j, :]), 16)
            else:
                for j in range(ntl):
                    s_wspy.inc(GP.dma_start(
                        out=dout[j * tw:(j + 1) * tw,
                                 idx * 128:idx * 128 + 128],
                        in_=wsht[0:tw, j * 128:(j + 1) * 128]), 16)
            sp_done[ig] = s_wspy.n
            a1b_st["ig"] = ig + 1
            if lastch:
                _wt = ((w1ti, w1tg), (w2ti, w2tg), (w3ti, w3tg))[wi]
                GP.wait_ge(s_wspy.h, s_wspy.n)
                s_ccw[wi].inc(GP.collective_compute(
                    "AllGather", ALU.bypass, replica_groups=[list(range(8))],
                    ins=[_wt[0].ap().opt()], outs=[_wt[1].ap().opt()]))
        a1b_i[0] = min(i0 + n, len(a1b_units))


    for w in range(NW):
        for ts_ in range(4):
            ti = 4 * w + ts_
            p = ti % 2
            bx = p * D
            if ti >= 2:
                SY.wait_ge(s_ve.h, x_rd[ti - 2][0])
                SY.wait_ge(s_ac.h, x_rd[ti - 2][1])
            s_x[p].inc(SY.dma_start(out=x_t[:, bx:bx + D],
                                    in_=xs[ti * 128:ti * 128 + 128, :]), 16)
            xl = s_x[p].n
            VE.wait_ge(s_x[p].h, xl)
            VE.reduce_sum(out=smc[:, 20 + p:21 + p], in_=x_t[:, bx:bx + D],
                          axis=AX.X)
            SC.wait_ge(s_x[p].h, xl)
            ac_sq = s_ac.inc(SC.activation(out=bigH[:, 0:D], in_=x_t[:, bx:bx + D],
                                           func=AF.Square,
                                           accum_out=smc[:, 22 + p:23 + p]))
            VE.drain()
            VE.wait_ge(s_ac.h, ac_sq)
            VE.tensor_scalar_mul(out=smc[:, 24 + p:25 + p],
                                 in0=smc[:, 20 + p:21 + p], scalar1=INV_D)
            VE.tensor_scalar_mul(out=smc[:, 26 + p:27 + p],
                                 in0=smc[:, 22 + p:23 + p], scalar1=INV_D)
            VE.drain()
            VE.tensor_tensor(out=smc[:, 28 + p:29 + p], in0=smc[:, 24 + p:25 + p],
                             in1=smc[:, 24 + p:25 + p], op=ALU.mult)
            VE.drain()
            ve_var = s_ve.inc(VE.tensor_tensor(out=smc[:, 30 + p:31 + p],
                                               in0=smc[:, 26 + p:27 + p],
                                               in1=smc[:, 28 + p:29 + p],
                                               op=ALU.subtract))
            SC.wait_ge(s_ve.h, ve_var)
            ac_st = s_ac.inc(SC.activation(out=smc[:, 32 + p:33 + p],
                                           in_=smc[:, 30 + p:31 + p],
                                           func=AF.Sqrt, bias=smc[:, 63:64]))
            VE.wait_ge(s_ac.h, ac_st)
            VE.reciprocal(out=smc[:, 34 + p:35 + p], in_=smc[:, 32 + p:33 + p])
            VE.drain()
            VE.tensor_tensor(out=smc[:, 36 + p:37 + p], in0=smc[:, 24 + p:25 + p],
                             in1=smc[:, 34 + p:35 + p], op=ALU.mult)
            VE.drain()
            if ti >= 2:
                VE.wait_ge(s_pe.h, xn_ev[ti - 2][1])
            ve_xn = s_ve.inc(VE.tensor_scalar(out=xn_tm[:, bx:bx + D],
                                              in0=x_t[:, bx:bx + D],
                                              scalar1=smc[:, 34 + p:35 + p],
                                              scalar2=smc[:, 36 + p:37 + p],
                                              op0=ALU.mult, op1=ALU.subtract))
            x_rd[ti] = (ve_xn, ac_sq)
            PE.wait_ge(s_ve.h, ve_xn)
            evd = {}
            for dk in range(DK):
                px = psX1 if dk % 2 == 0 else psX2
                if dk >= 2:
                    PE.wait_ge(s_ve.h, evd[dk - 2])
                elif psx_ve:
                    PE.wait_ge(s_ve.h, psx_ve)
                pet = s_pe.inc(PE.transpose(px[:, 0:128],
                                            xn_tm[:, bx + dk * 128:bx + dk * 128 + 128],
                                            id_sb[:]))
                VE.wait_ge(s_pe.h, pet)
                evd[dk] = s_ve.inc(VE.tensor_copy(
                    out=xn_fm[:, dk * WIN + ts_ * 128:dk * WIN + ts_ * 128 + 128],
                    in_=px[:, 0:128]))
            psx_ve = evd[DK - 1]
            xn_ev[ti] = (evd[DK - 1], s_pe.n)
        xn_ok = xn_ev[4 * w + 3][0]

        prod_banks = [(psA, 0), (psA, 512), (psB, 0), (psB, 512)]
        si = 0
        for kind in ("k", "q"):
            wb = OKO if kind == "k" else 0
            for hh in range(HPC):
                bank, boff = prod_banks[si % 4]
                g = prod_guard[si % 4]
                if g is not None:
                    PE.wait_ge((s_ac if g[0] == "ac" else s_ve).h, g[1])
                PE.wait_ge(s_ve.h, xn_ok)
                if w == 0 and si == 0:
                    PE.wait_ge(s_w.h, W_QKV)
                last = None
                for dk in range(DK):
                    last = PE.matmul(bank[:, boff:boff + WIN],
                                     wqkv[:, wb + (hh * DK + dk) * 128:
                                          wb + (hh * DK + dk) * 128 + 128],
                                     xn_fm[:, dk * WIN:(dk + 1) * WIN],
                                     start=(dk == 0), stop=(dk == DK - 1))
                pe_prod = s_pe.inc(last)
                bias = bk_sb[:, hh:hh + 1] if kind == "k" else bq_sb[:, hh:hh + 1]
                par = si % 2
                stg = par * WIN
                SC.wait_ge(s_pe.h, pe_prod)
                if kind == "k":
                    SC.wait_ge(s_kvsp[par].h, kvst_sp[par])
                if kind == "k":
                    dst = kvst[:, stg:stg + WIN]
                else:
                    dst = q_sb[:, hh * WIN:(hh + 1) * WIN]
                ac_e = s_ac.inc(SC.activation(out=dst, in_=bank[:, boff:boff + WIN],
                                              func=AF.Identity, bias=bias))
                ac_sqr = s_ac.inc(SC.activation(out=ksq[:, :],
                                                in_=bank[:, boff:boff + WIN],
                                                func=AF.Square, bias=bias))
                prod_guard[si % 4] = ("ac", ac_sqr)
                PE.wait_ge(s_ac.h, ac_sqr)
                PE.wait_ge(s_ve.h, max(norm_psS_free, negc_psW_free))
                pe_n = s_pe.inc(PE.matmul(psS[0:1, :], onesf[:, 0:1], ksq[:, :],
                                          start=True, stop=True))
                VE.wait_ge(s_pe.h, pe_n)
                VE.reduce_max(out=smr[:, 20:21], in_=psS[0:1, :], axis=AX.X)
                VE.drain()
                if kind == "k":
                    cur = (w % 2) * HPC + hh
                    prv = ((w + 1) % 2) * HPC + hh
                    VE.tensor_tensor(out=kmax[:, cur:cur + 1],
                                     in0=kmax[:, prv:prv + 1],
                                     in1=smr[:, 20:21], op=ALU.max)
                else:
                    VE.tensor_copy(out=smr[:, 24 + hh:25 + hh], in_=smr[:, 20:21])
                VE.drain()
                norm_psS_free = s_ve.inc(VE.engine_nop())
                if kind == "k":
                    GP.wait_ge(s_ac.h, ac_e)
                    kvst_sp[par] = s_kvsp[par].inc(GP.dma_start(
                        out=k_d[hh * DH:hh * DH + DH, w * WIN:(w + 1) * WIN],
                        in_=kvst[:, stg:stg + WIN]), 16)
                si += 1
        for ts_ in range(4):
            bank, boff = prod_banks[si % 4]
            g = prod_guard[si % 4]
            if g is not None:
                PE.wait_ge((s_ac if g[0] == "ac" else s_ve).h, g[1])
            last = None
            for dk in range(DK):
                last = PE.matmul(bank[:, boff:boff + WIN],
                                 xn_fm[:, dk * WIN + ts_ * 128:dk * WIN + ts_ * 128 + 128],
                                 wqkv[:, OVO + dk * EPC:OVO + (dk + 1) * EPC],
                                 start=(dk == 0), stop=(dk == DK - 1))
            pe_v = s_pe.inc(last)
            par = si % 2
            stg = par * WIN
            VE.wait_ge(s_pe.h, pe_v)
            VE.wait_ge(s_kvsp[par].h, kvst_sp[par])
            VE.wait_ge(s_ve.h, ve_bvbc)
            ve_v = s_ve.inc(VE.scalar_tensor_tensor(
                out=kvst[:, stg:stg + WIN], in0=bank[:, boff:boff + WIN],
                scalar=1.0, in1=bvbc[:, :], op0=ALU.mult, op1=ALU.add))
            prod_guard[si % 4] = ("ve", ve_v)
            GP.wait_ge(s_ve.h, ve_v)
            kvst_sp[par] = s_kvsp[par].inc(GP.dma_start(
                out=v_d[w * WIN + ts_ * 128:w * WIN + (ts_ + 1) * 128, :],
                in_=kvst[:, stg:stg + WIN]), 16)
            si += 1
        kv_spilled[w] = (s_kvsp[0].n, s_kvsp[1].n)

        cur0 = (w % 2) * HPC
        VE.tensor_tensor(out=smr[:, 44:48], in0=smr[:, 24:28],
                         in1=kmax[:, cur0:cur0 + 4], op=ALU.mult)
        VE.drain()
        ve_c2 = s_ve.inc(VE.engine_nop())
        SC.wait_ge(s_ve.h, ve_c2)
        ac_c = s_ac.inc(SC.activation(out=smr[:, 48:52], in_=smr[:, 44:48],
                                      func=AF.Sqrt, scale=1.0 / 128.0))
        VE.wait_ge(s_ac.h, ac_c)
        VE.tensor_scalar_mul(out=smr[:, 52:56], in0=smr[:, 48:52],
                             scalar1=-1.0)
        VE.drain()
        ve_nc = s_ve.inc(VE.engine_nop())
        PE.wait_ge(s_ve.h, max(ve_nc, negc_psW_free))
        PE.wait_ge(s_ac.h, max(negc_slot_rd.get(cur0 + h_, 0)
                               for h_ in range(HPC)))
        pe_nc = s_pe.inc(PE.matmul(psW[:, 0:4], onesf[0:1, :], smr[:, 52:56],
                                   start=True, stop=True))
        VE.wait_ge(s_pe.h, pe_nc)
        negc_psW_free = s_ve.inc(VE.tensor_copy(
            out=smc[:, 40 + cur0:44 + cur0], in_=psW[:, 0:4]))
        norm_psS_free = max(norm_psS_free, negc_psW_free)
        ve_negc = negc_psW_free

        for hh in range(HPC):
            cur = (w % 2) * HPC + hh
            ncol = 40 + cur

            n_kt = 4 * (w + 1)
            n_g = n_kt // 2

            def load_kv(kt):
                slot = kt % 4
                for which in (0, 1):
                    g = kv_ring.get((which, slot), 0)
                    if g:
                        SY.wait_ge(s_pe.h, g)
                    SY.wait_ge(s_kvsp[0].h, kv_spilled[w][0])
                    SY.wait_ge(s_kvsp[1].h, kv_spilled[w][1])
                    if which == 0:
                        ii = SY.dma_start(
                            out=ktile[:, slot * 128:slot * 128 + 128],
                            in_=k_d[hh * DH:hh * DH + DH,
                                    kt * 128:kt * 128 + 128])
                        kv_ld[(0, kt)] = s_k[slot].inc(ii, 16)
                    else:
                        ii = SY.dma_start(
                            out=vtile[:, slot * 128:slot * 128 + 128],
                            in_=v_d[kt * 128:kt * 128 + 128,
                                    hh * DH:hh * DH + DH])
                        kv_ld[(1, kt)] = s_v[slot].inc(ii, 16)

            for kt in range(min(4, n_kt)):
                load_kv(kt)
            for gi in range(n_g):
                pp = gi % 2
                bank = psA if pp == 0 else psB
                ptb = pt_a if pp == 0 else pt_b
                kt0, kt1 = 2 * gi, 2 * gi + 1
                PE.wait_ge(s_k[kt0 % 4].h, kv_ld[(0, kt0)])
                PE.wait_ge(s_k[kt1 % 4].h, kv_ld[(0, kt1)])
                if exp_bank[pp]:
                    PE.wait_ge(s_ac.h, exp_bank[pp])
                elif gi < 2:
                    g = prod_guard[2 * pp]
                    PE.wait_ge((s_ac if g[0] == "ac" else s_ve).h, g[1])
                    g = prod_guard[2 * pp + 1]
                    PE.wait_ge((s_ac if g[0] == "ac" else s_ve).h, g[1])
                PE.matmul(bank[:, 0:WIN],
                          ktile[:, (kt0 % 4) * 128:(kt0 % 4) * 128 + 128],
                          q_sb[:, hh * WIN:(hh + 1) * WIN], start=True, stop=True)
                pe_sc = s_pe.inc(PE.matmul(
                    bank[:, WIN:2 * WIN],
                    ktile[:, (kt1 % 4) * 128:(kt1 % 4) * 128 + 128],
                    q_sb[:, hh * WIN:(hh + 1) * WIN], start=True, stop=True))
                kv_ring[(0, kt0 % 4)] = pe_sc
                kv_ring[(0, kt1 % 4)] = pe_sc
                SC.wait_ge(s_pe.h, pe_sc)
                SC.wait_ge(s_ve.h, ve_negc)
                if pt_cons[pp]:
                    SC.wait_ge(s_pe.h, pt_cons[pp])
                ac_e = s_ac.inc(SC.activation(out=ptb[:, :], in_=bank[:, :],
                                              func=AF.Exp, scale=ISQ,
                                              bias=smc[:, ncol:ncol + 1]))
                exp_bank[pp] = ac_e
                exp_done[(w, hh)] = ac_e
                negc_slot_rd[cur] = ac_e
                use = ptb
                if kt1 >= 4 * w:
                    VE.wait_ge(s_ac.h, ac_e)
                    if ptm_cons:
                        VE.wait_ge(s_pe.h, ptm_cons)
                    for which, kt in ((0, kt0), (1, kt1)):
                        if kt >= 4 * w:
                            o = (kt - 4 * w) * 128
                            VE.tensor_tensor(
                                out=pt_m[:, which * WIN:(which + 1) * WIN],
                                in0=ptb[:, which * WIN:(which + 1) * WIN],
                                in1=tril[:, 384 - o:896 - o], op=ALU.mult)
                        else:
                            VE.tensor_copy(
                                out=pt_m[:, which * WIN:(which + 1) * WIN],
                                in_=ptb[:, which * WIN:(which + 1) * WIN])
                    ve_m = s_ve.inc(VE.engine_nop())
                    use = pt_m
                    PE.wait_ge(s_ve.h, ve_m)
                else:
                    PE.wait_ge(s_ac.h, ac_e)
                first = (gi == 0)
                last_g = (gi == n_g - 1)
                PE.wait_ge(s_v[kt0 % 4].h, kv_ld[(1, kt0)])
                PE.wait_ge(s_v[kt1 % 4].h, kv_ld[(1, kt1)])
                PE.matmul(psS[0:1, :], onesb[:, 0:1], use[:, 0:WIN],
                          start=first, stop=False)
                PE.matmul(psS[0:1, :], onesb[:, 0:1], use[:, WIN:2 * WIN],
                          start=False, stop=last_g)
                PE.matmul(psZ[:], vtile[:, (kt0 % 4) * 128:(kt0 % 4) * 128 + 128],
                          use[:, 0:WIN], start=first, stop=False)
                pe_z = s_pe.inc(PE.matmul(
                    psZ[:], vtile[:, (kt1 % 4) * 128:(kt1 % 4) * 128 + 128],
                    use[:, WIN:2 * WIN], start=False, stop=last_g))
                kv_ring[(1, kt0 % 4)] = pe_z
                kv_ring[(1, kt1 % 4)] = pe_z
                pt_cons[pp] = pe_z
                if use is pt_m:
                    ptm_cons = pe_z
                for kt in (2 * gi + 4, 2 * gi + 5):
                    if kt < n_kt:
                        load_kv(kt)
            VE.wait_ge(s_pe.h, s_pe.n)
            VE.reciprocal(out=rs_sb[:, :], in_=psS[0:1, :])
            VE.drain()
            ve_rs = s_ve.inc(VE.engine_nop())
            norm_psS_free = ve_rs
            PE.wait_ge(s_ve.h, max(ve_rs, negc_psW_free))
            pe_rb = s_pe.inc(PE.matmul(psW[:, :], onesf[0:1, :], rs_sb[:, :],
                                       start=True, stop=True))
            zpar = hh % 2
            zb = zpar * WIN
            VE.wait_ge(s_pe.h, pe_rb)
            VE.wait_ge(s_zsp[zpar].h, zsp_par[zpar])
            VE.tensor_copy(out=rb_f[:, :], in_=psW[:, 0:WIN])
            VE.drain()
            ve_z = s_ve.inc(VE.tensor_tensor(out=zst[:, zb:zb + WIN], in0=psZ[:],
                                             in1=rb_f[:, :], op=ALU.mult))
            negc_psW_free = ve_z
            GP.wait_ge(s_ve.h, ve_z)
            zsp_par[zpar] = s_zsp[zpar].inc(GP.dma_start(
                out=zin[w, hh * DH:hh * DH + DH, :],
                in_=zst[:, zb:zb + WIN]), 16)
            PE.wait_ge(s_ve.h, ve_z)
        GP.wait_ge(s_zsp[0].h, zsp_par[0])
        GP.wait_ge(s_zsp[1].h, zsp_par[1])
        s_ccz.inc(GP.collective_compute(
            "AllGather", ALU.bypass, replica_groups=[[0, 1, 2, 3], [4, 5, 6, 7]],
            ins=[zin[w, :, :].opt()], outs=[zg[w, :, :].opt()]))

        if w < 2:
            emit_a1a_batch(19)
        if w == 1:
            emit_a1a_tail_a()
        if w == 2:
            emit_a1a_tail_b()
        if w >= 2:
            emit_a1b_batch(7)

    A2_PE, A2_VE, A2_AC = s_pe.n, s_ve.n, s_ac.n
    emit_a1b_batch(len(a1b_units))
    assert a1b_i[0] == len(a1b_units) and a1a_i[0] == len(a1a_units)
    A1_PE, A1_VE, A1_AC = s_pe.n, s_ve.n, s_ac.n
    A1_SPILLS = s_wspy.n


    if PHASE_LIMIT < 3:
        GP.wait_ge(s_ccz.h, s_ccz.n)
        return nc
    ab.reset(0)
    wo_sb = ab.alloc(H * D)
    ztile = ab.alloc(2 * 2048)
    xq_t = ab.alloc(D)
    xq_fm = ab.alloc(DK * TPC)
    ab_b1 = ab.mark()
    af.reset(0)
    x_own = af.alloc(2 * D)
    x2_t = af.alloc(D)
    silu_s = af.alloc(2 * WIN)
    h_t = af.alloc(2 * WIN)
    r_bcf = af.alloc(D)
    af_b1 = af.mark()
    u16 = bigH[:, 0:D]

    GP.wait_ge(s_pe.h, A1_PE)
    GP.wait_ge(s_ve.h, A1_VE)
    GP.wait_ge(s_wspy.h, A1_SPILLS)
    SY.wait_ge(s_ve.h, A1_VE)
    wo_ld = {}
    for hx in range(H):
        wo_ld[hx] = s_w.inc(GP.dma_start(out=wo_sb[:, hx * D:(hx + 1) * D],
                                         in_=wod[hx, :, :]), 16)
    W_WO = s_w.n

    PE.wait_ge(s_ve.h, A1_VE)
    PE.wait_ge(s_ac.h, A1_AC)
    rb_pe = 0
    for dsd in range(4):
        if rb_pe:
            SY.wait_ge(s_pe.h, rb_pe)
        s_ld.inc(SY.dma_start(out=bo_row[:, :],
                              in_=bod[0:1, dsd * WIN:(dsd + 1) * WIN]), 16)
        BOL = s_ld.n
        PE.wait_ge(s_ld.h, BOL)
        if dsd > 0:
            PE.wait_ge(s_ve.h, rb_ve)
        rb_pe = s_pe.inc(PE.matmul(psW[:, :], onesf[0:1, :], bo_row[:, :],
                                   start=True, stop=True))
        VE.wait_ge(s_pe.h, rb_pe)
        rb_ve = s_ve.inc(VE.tensor_copy(out=r_bcf[:, dsd * WIN:(dsd + 1) * WIN],
                                        in_=psW[:, :]))
    R_DONE = s_ve.n

    SY.wait_ge(s_ccz.h, s_ccz.n)
    q2 = SY.compute_val(SY.partition_id() % 4 * 2)

    zgv = zg.rearrange("w (h e) (t c) -> w t e h c", h=H, c=128)
    xs3 = xs.rearrange("(u t c) d -> u t c d", t=4, c=128)

    zring = {-2: 0, -1: 0}
    xo_rd = {-2: 0, -1: 0}
    x2sp_par = {0: 0, 1: 0}
    opbank_ve = 0
    psxb_ve = 0
    xq_ready = {}
    for tc in range(8):
        zpar = tc % 2
        zb = zpar * 2048
        SY.wait_ge(s_pe.h, zring[tc - 2])
        s_zl[zpar].inc(SY.dma_start(
            out=ztile[:, zb:zb + 2048].rearrange("e (h c) -> e h c", h=H),
            in_=zgv[q2 + tc // 4, tc % 4, :, :, :]), 16)
        ZL = s_zl[zpar].n
        PE.wait_ge(s_zl[zpar].h, ZL)
        if tc == 0:
            PE.wait_ge(s_ve.h, R_DONE)
        if opbank_ve:
            PE.wait_ge(s_ve.h, opbank_ve)
        last = None
        for hx in range(H):
            if tc == 0:
                PE.wait_ge(s_w.h, wo_ld[hx])
            for dsd in range(4):
                bb = psA if dsd < 2 else psB
                bo_ = (dsd % 2) * 512
                last = PE.matmul(bb[:, bo_:bo_ + WIN],
                                 ztile[:, zb + hx * 128:zb + hx * 128 + 128],
                                 wo_sb[:, hx * D + dsd * WIN:hx * D + (dsd + 1) * WIN],
                                 start=(hx == 0), stop=(hx == H - 1))
        pe_op = s_pe.inc(last)
        zring[tc] = pe_op
        OP_PE = s_pe.n
        p = tc % 2
        bx = p * D
        SY.wait_ge(s_ve.h, xo_rd[tc - 2])
        SY.wait_ge(s_x2sp[p].h, x2sp_par[p])
        s_x[p].inc(SY.dma_start(out=x_own[:, bx:bx + D],
                                in_=xs3[q2 + tc // 4, tc % 4, :, :]), 16)
        XO = s_x[p].n
        VE.wait_ge(s_x[p].h, XO)
        VE.wait_ge(s_pe.h, OP_PE)
        for dsd in range(4):
            bb = psA if dsd < 2 else psB
            bo_ = (dsd % 2) * 512
            VE.scalar_tensor_tensor(
                out=x2_t[:, dsd * WIN:(dsd + 1) * WIN], in0=bb[:, bo_:bo_ + WIN],
                scalar=1.0, in1=x_own[:, bx + dsd * WIN:bx + (dsd + 1) * WIN],
                op0=ALU.mult, op1=ALU.add)
        VE.drain()
        opbank_ve = s_ve.inc(VE.engine_nop())
        VE.wait_ge(s_ve.h, max(opbank_ve, R_DONE))
        ve_x2 = s_ve.inc(VE.tensor_tensor(out=x_own[:, bx:bx + D], in0=x2_t[:, :],
                                          in1=r_bcf[:, :], op=ALU.add))
        VE.drain()
        GP.wait_ge(s_ve.h, ve_x2)
        x2sp_par[p] = s_x2sp[p].inc(GP.dma_start(
            out=x2_d[tc * 128:tc * 128 + 128, :], in_=x_own[:, bx:bx + D]), 16)
        VE.reduce_sum(out=smc[:, 20 + p:21 + p], in_=x_own[:, bx:bx + D], axis=AX.X)
        VE.reduce_max(out=smc[:, 48 + p:49 + p], in_=x_own[:, bx:bx + D], axis=AX.X)
        VE.tensor_reduce(out=smc[:, 50 + p:51 + p], in_=x_own[:, bx:bx + D],
                         axis=AX.X, op=ALU.min)
        SC.wait_ge(s_ve.h, ve_x2)
        ac_sq = s_ac.inc(SC.activation(out=bigH[:, D:2 * D], in_=x_own[:, bx:bx + D],
                                       func=AF.Square,
                                       accum_out=smc[:, 22 + p:23 + p]))
        VE.drain()
        VE.wait_ge(s_ac.h, ac_sq)
        VE.tensor_scalar_mul(out=smc[:, 24 + p:25 + p], in0=smc[:, 20 + p:21 + p],
                             scalar1=INV_D)
        VE.tensor_scalar_mul(out=smc[:, 26 + p:27 + p], in0=smc[:, 22 + p:23 + p],
                             scalar1=INV_D)
        VE.drain()
        VE.tensor_tensor(out=smc[:, 28 + p:29 + p], in0=smc[:, 24 + p:25 + p],
                         in1=smc[:, 24 + p:25 + p], op=ALU.mult)
        VE.drain()
        ve_var = s_ve.inc(VE.tensor_tensor(out=smc[:, 30 + p:31 + p],
                                           in0=smc[:, 26 + p:27 + p],
                                           in1=smc[:, 28 + p:29 + p],
                                           op=ALU.subtract))
        SC.wait_ge(s_ve.h, ve_var)
        ac_st = s_ac.inc(SC.activation(out=smc[:, 32 + p:33 + p],
                                       in_=smc[:, 30 + p:31 + p],
                                       func=AF.Sqrt, bias=smc[:, 63:64]))
        VE.wait_ge(s_ac.h, ac_st)
        VE.reciprocal(out=smc[:, 34 + p:35 + p], in_=smc[:, 32 + p:33 + p])
        VE.drain()
        VE.tensor_tensor(out=smc[:, 36 + p:37 + p], in0=smc[:, 48 + p:49 + p],
                         in1=smc[:, 24 + p:25 + p], op=ALU.subtract)
        VE.tensor_tensor(out=smc[:, 38 + p:39 + p], in0=smc[:, 24 + p:25 + p],
                         in1=smc[:, 50 + p:51 + p], op=ALU.subtract)
        VE.drain()
        VE.tensor_tensor(out=smc[:, 40 + p:41 + p], in0=smc[:, 36 + p:37 + p],
                         in1=smc[:, 38 + p:39 + p], op=ALU.max)
        VE.drain()
        VE.tensor_tensor(out=smc[:, 42 + p:43 + p], in0=smc[:, 40 + p:41 + p],
                         in1=smc[:, 34 + p:35 + p], op=ALU.mult)
        VE.drain()
        VE.tensor_scalar_max(out=smc[:, 44 + p:45 + p], in0=smc[:, 42 + p:43 + p],
                             scalar1=1e-5)
        VE.drain()
        VE.reciprocal(out=smc[:, 46 + p:47 + p], in_=smc[:, 44 + p:45 + p])
        VE.tensor_scalar_mul(out=fac_x[:, tc:tc + 1], in0=smc[:, 44 + p:45 + p],
                             scalar1=1.0 / 127.0)
        VE.drain()
        VE.tensor_scalar_mul(out=smc[:, 52 + p:53 + p], in0=smc[:, 46 + p:47 + p],
                             scalar1=127.0)
        VE.drain()
        VE.tensor_tensor(out=smc[:, 54 + p:55 + p], in0=smc[:, 34 + p:35 + p],
                         in1=smc[:, 52 + p:53 + p], op=ALU.mult)
        VE.drain()
        VE.tensor_tensor(out=smc[:, 56 + p:57 + p], in0=smc[:, 24 + p:25 + p],
                         in1=smc[:, 54 + p:55 + p], op=ALU.mult)
        VE.drain()
        VE.tensor_scalar(out=smc[:, 58 + p:59 + p], in0=smc[:, 56 + p:57 + p],
                         scalar1=-1.0, scalar2=RM, op0=ALU.mult, op1=ALU.add)
        VE.drain()
        VE.tensor_scalar(out=u16[:, :], in0=x_own[:, bx:bx + D],
                         scalar1=smc[:, 54 + p:55 + p],
                         scalar2=smc[:, 58 + p:59 + p],
                         op0=ALU.mult, op1=ALU.add)
        VE.drain()
        ve_xq = s_ve.inc(VE.tensor_single_scalar(out=xq_t[:, :], in_=u16[:, :],
                                                 scalar=-RM, op=ALU.add))
        xo_rd[tc] = ve_xq
        PE.wait_ge(s_ve.h, ve_xq)
        evd = {}
        for dk in range(DK):
            px = psX1 if dk % 2 == 0 else psX2
            if dk >= 2:
                PE.wait_ge(s_ve.h, evd[dk - 2])
            elif psxb_ve:
                PE.wait_ge(s_ve.h, psxb_ve)
            elif psx_ve:
                PE.wait_ge(s_ve.h, psx_ve)
            pet = s_pe.inc(PE.transpose(px[:, 0:128],
                                        xq_t[:, dk * 128:dk * 128 + 128],
                                        id_sb[:]))
            VE.wait_ge(s_pe.h, pet)
            evd[dk] = s_ve.inc(VE.tensor_copy(
                out=xq_fm[:, dk * TPC + tc * 128:dk * TPC + tc * 128 + 128],
                in_=px[:, 0:128]))
        psxb_ve = evd[DK - 1]
        xq_ready[tc] = evd[DK - 1]

    VE.tensor_single_scalar(out=sa1[:, 0:8], in_=fac_x[:, 0:8],
                            scalar=smc[:, 8:9], op=ALU.mult)
    VE.tensor_single_scalar(out=sb2[:, 0:8], in_=fac_x[:, 0:8],
                            scalar=smc[:, 9:10], op=ALU.mult)
    VE.drain()
    SC_READY = s_ve.inc(VE.engine_nop())
    B1_PE = s_pe.n

    if PHASE_LIMIT < 4:
        VE.wait_ge(s_ve.h, s_ve.n)
        return nc
    w12r = bigB[:, 0:2 * 2 * DK * WIN]
    wr_done = {}
    wr_used = {-2: 0, -1: 0}
    hb_guard = {0: None, 1: None}
    silu_g = {0: 0, 1: 0}
    hsp_par = {0: 0, 1: 0}
    for hs in range(NHS):
        wpar = hs % 2
        wb = wpar * (2 * DK * WIN)
        SY.wait_ge(s_pe.h, max(wr_used[hs - 2], B1_PE))
        if hs == 0:
            SY.wait_ge(s_ccw[0].h, 1)
            SY.wait_ge(s_ccw[1].h, 1)
        for dk in range(DK):
            s_wr[wpar].inc(SY.dma_start(
                out=w12r[:, wb + dk * WIN:wb + (dk + 1) * WIN],
                in_=w1tg.ap()[dk * 128:dk * 128 + 128, hs * WIN:(hs + 1) * WIN]), 16)
            s_wr[wpar].inc(SY.dma_start(
                out=w12r[:, wb + DK * WIN + dk * WIN:wb + DK * WIN + (dk + 1) * WIN],
                in_=w2tg.ap()[dk * 128:dk * 128 + 128, hs * WIN:(hs + 1) * WIN]), 16)
        wr_done[hs] = s_wr[wpar].n
        for tc in range(8):
            pp = tc % 2
            bankA = psA if pp == 0 else psB
            PE.wait_ge(s_wr[wpar].h, wr_done[hs])
            if hs == 0:
                PE.wait_ge(s_ve.h, xq_ready[tc])
                if tc < 2:
                    PE.wait_ge(s_ve.h, opbank_ve)
            g = hb_guard[pp]
            if g is not None:
                PE.wait_ge(s_ve.h, g[0])
                PE.wait_ge(s_ac.h, g[1])
            last = None
            for dk in range(DK):
                st, en = (dk == 0), (dk == DK - 1)
                PE.matmul(bankA[:, 0:WIN],
                          xq_fm[:, dk * TPC + tc * 128:dk * TPC + tc * 128 + 128],
                          w12r[:, wb + dk * WIN:wb + (dk + 1) * WIN],
                          start=st, stop=en)
                last = PE.matmul(bankA[:, WIN:2 * WIN],
                                 xq_fm[:, dk * TPC + tc * 128:dk * TPC + tc * 128 + 128],
                                 w12r[:, wb + DK * WIN + dk * WIN:
                                      wb + DK * WIN + (dk + 1) * WIN],
                                 start=st, stop=en)
            pe_mm = s_pe.inc(last)
            wr_used[hs] = pe_mm
            sst = pp * WIN
            SC.wait_ge(s_pe.h, pe_mm)
            if hs == 0 and tc == 0:
                SC.wait_ge(s_ve.h, SC_READY)
            if silu_g[pp]:
                SC.wait_ge(s_ve.h, silu_g[pp])
            VE.wait_ge(s_hsp[pp].h, hsp_par[pp])
            ac_si = s_ac.inc(SC.activation(out=silu_s[:, sst:sst + WIN],
                                           in_=bankA[:, 0:WIN], func=AF.Silu,
                                           scale=sa1[:, tc:tc + 1]))
            VE.wait_ge(s_ac.h, ac_si)
            ve_h = s_ve.inc(VE.scalar_tensor_tensor(
                out=h_t[:, sst:sst + WIN], in0=bankA[:, WIN:2 * WIN],
                scalar=sb2[:, tc:tc + 1], in1=silu_s[:, sst:sst + WIN],
                op0=ALU.mult, op1=ALU.mult))
            silu_g[pp] = ve_h
            VE.drain()
            ve_hm = s_ve.inc(VE.reduce_max(
                out=hm_c[:, tc * 12 + hs:tc * 12 + hs + 1],
                in_=h_t[:, sst:sst + WIN], axis=AX.X, apply_absolute_value=True))
            hb_guard[pp] = (ve_h, ac_si)
            GP.wait_ge(s_ve.h, ve_hm)
            hsp_par[pp] = s_hsp[pp].inc(GP.dma_start(
                out=h_d[tc * 128:tc * 128 + 128, hs * WIN:(hs + 1) * WIN],
                in_=h_t[:, sst:sst + WIN]), 16)
    B3_PE = s_pe.n
    VE.drain()
    for tc in range(8):
        p = tc % 2
        VE.reduce_max(out=smc[:, 20 + p:21 + p],
                      in_=hm_c[:, tc * 12:tc * 12 + 11], axis=AX.X)
        VE.drain()
        VE.tensor_scalar_max(out=smc[:, 22 + p:23 + p],
                             in0=smc[:, 20 + p:21 + p], scalar1=1e-5)
        VE.drain()
        VE.reciprocal(out=smc[:, 24 + p:25 + p], in_=smc[:, 22 + p:23 + p])
        VE.tensor_scalar_mul(out=smc[:, 26 + p:27 + p], in0=smc[:, 22 + p:23 + p],
                             scalar1=1.0 / 127.0)
        VE.drain()
        VE.tensor_scalar_mul(out=hmax_sc[:, tc:tc + 1], in0=smc[:, 24 + p:25 + p],
                             scalar1=127.0)
        VE.tensor_single_scalar(out=s3s[:, tc:tc + 1], in_=smc[:, 26 + p:27 + p],
                                scalar=smc[:, 10:11], op=ALU.mult)
        VE.drain()
    HS_READY = s_ve.inc(VE.engine_nop())
    B3_VE = s_ve.n

    if PHASE_LIMIT < 5:
        GP.wait_ge(s_hsp[0].h, s_hsp[0].n)
        GP.wait_ge(s_hsp[1].h, s_hsp[1].n)
        return nc
    ab.reset(0)
    hq_fm = ab.alloc(HKN * TPC)
    hq_t = ab.alloc(HID // 2)
    w3r = ab.alloc(HKN * WIN)
    af.reset(0)
    h_ld = af.alloc(HID)
    x2_ld = af.alloc(2 * WIN)
    out_st = af.alloc(2 * WIN)
    u16h = bigH[:, 0:HID]

    psxc_ve = psxb_ve
    SY.wait_ge(s_hsp[0].h, s_hsp[0].n)
    SY.wait_ge(s_hsp[1].h, s_hsp[1].n)
    SY.wait_ge(s_x2sp[0].h, s_x2sp[0].n)
    SY.wait_ge(s_x2sp[1].h, s_x2sp[1].n)
    HHALF = HID // 2
    QH = HHALF // 2
    NTQ = QH // 128
    hl_rd = {-2: 0, -1: 0}
    hl_ld = {}
    qt_pe = {-2: 0, -1: 0}
    pxg = {0: psxc_ve, 1: psxc_ve}

    def issue_hl(idx):
        if idx >= 16:
            return
        tc_, hf_ = idx // 2, idx % 2
        sl = idx % 2
        SY.wait_ge(s_ve.h, hl_rd[idx - 2])
        hl_ld[idx] = s_hl[sl].inc(SY.dma_start(
            out=h_ld[:, sl * HHALF:(sl + 1) * HHALF],
            in_=h_d[tc_ * 128:tc_ * 128 + 128,
                    hf_ * HHALF:(hf_ + 1) * HHALF]), 16)

    issue_hl(0)
    issue_hl(1)
    for idx in range(16):
        tc, hf = idx // 2, idx % 2
        sl = idx % 2
        VE.wait_ge(s_hl[sl].h, hl_ld[idx])
        if idx == 0:
            VE.wait_ge(s_ve.h, HS_READY)
        for q in range(2):
            gq = idx * 2 + q
            qsl = gq % 2
            VE.wait_ge(s_pe.h, qt_pe[gq - 2])
            VE.tensor_scalar(out=u16h[:, qsl * QH:(qsl + 1) * QH],
                             in0=h_ld[:, sl * HHALF + q * QH:
                                      sl * HHALF + (q + 1) * QH],
                             scalar1=hmax_sc[:, tc:tc + 1], scalar2=RM,
                             op0=ALU.mult, op1=ALU.add)
            VE.drain()
            ve_hq = s_ve.inc(VE.tensor_single_scalar(
                out=hq_t[:, qsl * QH:(qsl + 1) * QH],
                in_=u16h[:, qsl * QH:(qsl + 1) * QH],
                scalar=-RM, op=ALU.add))
            if q == 1:
                hl_rd[idx] = ve_hq
            PE.wait_ge(s_ve.h, ve_hq)
            if gq == 0:
                PE.wait_ge(s_pe.h, B3_PE)
            for hj in range(NTQ):
                hk = hf * (HKN // 2) + q * NTQ + hj
                b = hj % 2
                PE.wait_ge(s_ve.h, pxg[b])
                pet = s_pe.inc(PE.transpose(
                    (psX1 if b == 0 else psX2)[:, 0:128],
                    hq_t[:, qsl * QH + hj * 128:qsl * QH + hj * 128 + 128],
                    id_sb[:]))
                VE.wait_ge(s_pe.h, pet)
                pxg[b] = s_ve.inc(VE.tensor_copy(
                    out=hq_fm[:, hk * TPC + tc * 128:hk * TPC + tc * 128 + 128],
                    in_=(psX1 if b == 0 else psX2)[:, 0:128]))
            qt_pe[gq] = s_pe.n
        issue_hl(idx + 2)
    HQ_ALL = s_ve.n

    if PHASE_LIMIT < 6:
        VE.wait_ge(s_ve.h, s_ve.n)
        return nc
    w3_used = {-1: 0}
    b5_g = {0: 0, 1: 0}
    out_g = {0: 0, 1: 0}
    x2l = {-2: 0, -1: 0}
    it = 0
    for dq in range(4):
        SY.wait_ge(s_pe.h, max(w3_used[dq - 1], B3_PE))
        if dq == 0:
            SY.wait_ge(s_ve.h, B3_VE)
        if dq == 0:
            SY.wait_ge(s_ccw[2].h, 1)
        W3L = 0
        for hk in range(HKN):
            W3L = s_w3r.inc(SY.dma_start(
                out=w3r[:, hk * WIN:(hk + 1) * WIN],
                in_=w3tg.ap()[hk * 128:hk * 128 + 128, dq * WIN:(dq + 1) * WIN]),
                16)
        for tc in range(8):
            pp = tc % 2
            bank = psZ if pp == 0 else psS
            PE.wait_ge(s_w3r.h, W3L)
            if dq == 0 and tc == 0:
                PE.wait_ge(s_ve.h, HQ_ALL)
            if b5_g[pp]:
                PE.wait_ge(s_ve.h, b5_g[pp])
            last = None
            for hk in range(HKN):
                last = PE.matmul(bank[:],
                                 hq_fm[:, hk * TPC + tc * 128:hk * TPC + tc * 128 + 128],
                                 w3r[:, hk * WIN:(hk + 1) * WIN],
                                 start=(hk == 0), stop=(hk == HKN - 1))
            pe_m3 = s_pe.inc(last)
            w3_used[dq] = pe_m3
            xb = pp * WIN
            SY.wait_ge(s_ve.h, x2l[it - 2])
            if dq == 0 and tc == 0:
                SY.wait_ge(s_x2sp[0].h, s_x2sp[0].n)
                SY.wait_ge(s_x2sp[1].h, s_x2sp[1].n)
            X2L = s_x2l[pp].inc(SY.dma_start(
                out=x2_ld[:, xb:xb + WIN],
                in_=x2_d[tc * 128:tc * 128 + 128,
                         dq * WIN:(dq + 1) * WIN]), 16)
            VE.wait_ge(s_x2l[pp].h, X2L)
            VE.wait_ge(s_pe.h, pe_m3)
            if out_g[pp]:
                VE.wait_ge(s_ou[pp].h, out_g[pp])
            ve_o = s_ve.inc(VE.scalar_tensor_tensor(
                out=out_st[:, xb:xb + WIN], in0=bank[:], scalar=s3s[:, tc:tc + 1],
                in1=x2_ld[:, xb:xb + WIN], op0=ALU.mult, op1=ALU.add))
            b5_g[pp] = ve_o
            x2l[it] = ve_o
            SY.wait_ge(s_ve.h, ve_o)
            out_g[pp] = s_ou[pp].inc(SY.dma_start(
                out=out_d[tc * 128:tc * 128 + 128, dq * WIN:(dq + 1) * WIN],
                in_=out_st[:, xb:xb + WIN]), 16)
            it += 1
    for pp in (0, 1):
        SY.wait_ge(s_ou[pp].h, s_ou[pp].n)
        GP.wait_ge(s_ou[pp].h, s_ou[pp].n)
        VE.wait_ge(s_ou[pp].h, s_ou[pp].n)

    return nc


_NC_CACHE = {}


def kernel(L0_attn_x, L0_attn_W_Q, L0_attn_W_K, L0_attn_W_V, L0_attn_W_O,
           L0_attn_b_Q, L0_attn_b_K, L0_attn_b_V, L0_attn_b_O,
           L0_ffn_w1, L0_ffn_w2, L0_ffn_w3):
    x = np.ascontiguousarray(np.asarray(L0_attn_x, dtype=np.float32))
    WQ = np.asarray(L0_attn_W_Q, dtype=np.float32)
    WK = np.asarray(L0_attn_W_K, dtype=np.float32)
    WV = np.asarray(L0_attn_W_V, dtype=np.float32)
    WO = np.ascontiguousarray(np.asarray(L0_attn_W_O, dtype=np.float32))
    bQ = np.asarray(L0_attn_b_Q, dtype=np.float32)
    bK = np.asarray(L0_attn_b_K, dtype=np.float32)
    bV = np.asarray(L0_attn_b_V, dtype=np.float32)
    bO = np.asarray(L0_attn_b_O, dtype=np.float32).reshape(1, D)
    w1 = np.asarray(L0_ffn_w1, dtype=np.float32)
    w2 = np.asarray(L0_ffn_w2, dtype=np.float32)
    w3 = np.asarray(L0_ffn_w3, dtype=np.float32)

    import ml_dtypes
    ident = np.eye(128, dtype=ml_dtypes.bfloat16)
    onesb = np.ones((128, 2), dtype=ml_dtypes.bfloat16)
    onesf = np.ones((128, 128), dtype=np.float32)
    uu = np.arange(896)[None, :]
    ii = np.arange(128)[:, None]
    tril = (uu >= ii + 384).astype(ml_dtypes.bfloat16)

    bqT = np.ascontiguousarray(bQ.T)
    bkT = np.ascontiguousarray(bK.T)
    bvT = np.ascontiguousarray(bV.T)

    if "nc" not in _NC_CACHE:
        _NC_CACHE["nc"] = build_program()
    nc = _NC_CACHE["nc"]

    in_maps = []
    for c in range(8):
        s = c // 4
        g = c % 4
        hs = slice(4 * g, 4 * g + 4)
        in_maps.append({
            "xs": np.ascontiguousarray(x[s]),
            "wq": np.ascontiguousarray(WQ[hs]),
            "wk": np.ascontiguousarray(WK[hs]),
            "wv": np.ascontiguousarray(WV[hs]),
            "wo": WO,
            "bqT": np.ascontiguousarray(bqT[:, hs]),
            "bkT": np.ascontiguousarray(bkT[:, hs]),
            "bvTf": bvT,
            "bvfl": np.ascontiguousarray(bV[hs].reshape(1, EPC)),
            "bo": bO,
            "w1s": np.ascontiguousarray(w1[:, c * W12_SH:(c + 1) * W12_SH]),
            "w2s": np.ascontiguousarray(w2[:, c * W12_SH:(c + 1) * W12_SH]),
            "w3s": np.ascontiguousarray(w3[:, c * W3_SH:(c + 1) * W3_SH]),
            "ident": ident,
            "onesb": onesb,
            "onesf": onesf,
            "tril": tril,
        })
    res = run_bass_kernel_spmd(nc, in_maps, core_ids=list(range(8)))
    out = np.empty((2, S, D), dtype=np.float32)
    for c in range(8):
        s, g = c // 4, c % 4
        out[s, g * TPC:(g + 1) * TPC, :] = res.results[c]["out"]
    return out
